# version 01
# baseline (speedup 1.0000x reference)
import numpy as np
import ml_dtypes
import concourse.bass as bass
import concourse.mybir as mybir
from concourse.bass_utils import run_bass_kernel_spmd
from concourse import dve_ops
from concourse.dve_op import DveOp
from concourse.dve_spec import Spec, Src0, Src1, C2, lower
from concourse.dve_table_gen import dve_ver_for
import concourse.bass_isa as bass_isa
import bass_rust as _bass_rust
from concourse.library_config import all_libraries, standard as _lib_std
from contextlib import ExitStack

F32 = mybir.dt.float32
BF16 = mybir.dt.bfloat16
FP8 = mybir.dt.float8e4
A = mybir.AluOpType
ACT = mybir.ActivationFunctionType
AXX = mybir.AxisListType.X
RMax = bass_isa.ReduceOp.max

NCORES = 8
B, N, D = 2, 4096, 2048
H, DH = 8, 128
I = 5632
T = 1024
EXT = 1536
DC = D // 128
IC = I // 128
TC = T // 128
TCE = EXT // 128
VA = 132
LN_EPS = 1e-5
ENGINES = ("sync", "scalar", "vector", "tensor", "gpsimd")


def _qcast_ref(in0, in1, s0, s1, imm2):
    v = np.asarray(in0) * np.asarray(in1)
    q = v.astype(ml_dtypes.float8_e4m3fn).astype(v.dtype)
    return q * imm2


def _register_qcast():
    for op in dve_ops.OPS:
        if op.name == "QCAST":
            return op
    spec = Spec(body=(Src0 * Src1) * C2)
    ver = dve_ver_for("TRN2")
    sha = lower(spec, ver=ver, name="QCAST").sha(ver)
    op = DveOp("QCAST", spec, subdim=False, uops_sha={ver: sha}, reference=_qcast_ref)
    dve_ops.OPS.append(op)
    dve_ops.CUSTOM_DVE_SPECS["QCAST"] = spec
    dve_ops.CUSTOM_DVE_REFS["QCAST"] = _qcast_ref
    dve_ops._SUB_OPCODE_FOR_NAME["QCAST"] = dve_ops._CUSTOM_DVE_ROW_BASE + len(dve_ops.OPS) - 1
    return op


QCAST = _register_qcast()


class _Sem:
    __slots__ = ("h", "n")
    def __init__(self, h):
        self.h = h
        self.n = 0


SERIAL = True

class Ctx:
    def __init__(self, eng_name, nc, sems):
        self.eng = eng_name
        self.nc = nc
        self.sems = sems
        for s in sems.values():
            s.n = 0

    def _E(self):
        return getattr(self.nc, self.eng)

    def do(self, eng, fn, inc=None, dma=False):
        if self.eng == eng:
            E = getattr(self.nc, eng)
            if SERIAL:
                for s in self.sems.values():
                    if s.n > 0:
                        E.wait_ge(s.h, s.n)
            ins = fn(E)
            if inc is not None:
                ins.then_inc(inc.h, 16 if dma else 1)
        if inc is not None:
            inc.n += 16 if dma else 1

    def wait(self, eng, *sems):
        if self.eng == eng:
            for s in sems:
                getattr(self.nc, eng).wait_ge(s.h, s.n)


def build_nc():
    nc = bass.Bass("TRN2", target_bir_lowering=False)
    din = lambda name, shape, dt=F32: nc.dram_tensor(name, shape, dt, kind="ExternalInput").ap()
    xT = din("xT", [D, EXT])
    WqT = din("WqT", [D, H * DH])
    WkT = din("WkT", [D, H * DH])
    WvT = din("WvT", [D, H * DH])
    WgT = din("WgT", [D, H])
    WoT = din("WoT", [H * DH, D])
    wgP = din("wgP", [D, I])
    wuP = din("wuP", [D, I])
    wdT = din("wdT", [I, D])
    qs_i = din("qs", [128, 1])
    ks_i = din("ks", [128, 1])
    bgb_i = din("bgb", [128, H])
    mse_i = din("mse", [128, 32])
    id_i = din("ident", [128, 128])
    out_d = nc.dram_tensor("out", [D, T], F32, kind="ExternalOutput").ap()

    es = ExitStack()
    with es:
        SB = lambda st, name, shape, dt=F32: st.enter_context(nc.sbuf_tensor(name, shape, dt))
        SBR = lambda st, name, shape, dt=F32: st.enter_context(nc.sbuf_tensor(name, shape, dt, side="right"))
        PS = lambda st, name, shape, dt=F32: st.enter_context(nc.psum_tensor(name, shape, dt))
        qs = SB(es, "qs_t", [128, 1]); ks = SB(es, "ks_t", [128, 1])
        bgb = SB(es, "bgb_t", [128, H]); mse = SB(es, "mse_t", [128, 32])
        ident = SB(es, "ident_b", [128, 128], BF16)
        ones = SB(es, "ones_t", [128, 1])
        gates = SB(es, "gates_t", [128, TC * H])
        sxb = SB(es, "sxb", [128, T])
        oscl = SB(es, "oscl", [128, 2])
        wsg_pp = SB(es, "wsg_pp", [128, IC]); wsu_pp = SB(es, "wsu_pp", [128, IC])
        wsd_pp = SB(es, "wsd_pp", [128, DC])
        o_fm = SB(es, "o_fm", [128, H, T], BF16)
        vfm = SB(es, "vfm", [128, EXT], BF16)

        sems = {k: _Sem(es.enter_context(nc.semaphore("sem_" + k)))
                for k in ("x", "v", "s", "t", "g", "st", "ph")}

        es1 = ExitStack()
        mb = SB(es1, "mb", [128, EXT]); rb = SB(es1, "rb", [128, EXT])
        ps_a1 = [PS(es1, f"ps_a1{i}", [1, 512]) for i in range(3)]
        ps_b1 = [PS(es1, f"ps_b1{i}", [1, 512]) for i in range(3)]
        esX = ExitStack()
        rows = SBR(esX, "rows", [1, 4 * EXT])
        xr = [SBR(esX, f"xr{i}", [128, EXT]) for i in range(3)]
        es2 = ExitStack()
        xn = SBR(es2, "xn", [128, DC, EXT], BF16)
        rux448 = SBR(es2, "rux448", [128, T])
        rm_x = SBR(es2, "rm_x", [128, T])
        es3 = ExitStack()
        q_h = SBR(es3, "q_h", [128, T], BF16)
        k_h = SBR(es3, "k_h", [128, EXT], BF16)
        v_augh = SBR(es3, "v_augh", [128, TCE, VA], BF16)
        wf_st = [SBR(es3, f"wf_st{i}", [128, DC * 128]) for i in range(2)]
        wf_bf = [SBR(es3, f"wf_bf{i}", [128, DC * 128], BF16) for i in range(2)]
        rnb = SBR(es3, "rnb", [128, EXT])
        p_sb = SBR(es3, "p_sb", [128, 8, 512], BF16)
        sc_sb = SBR(es3, "sc_sb", [128, 512])
        otm_sb = SBR(es3, "otm", [128, 128], BF16)
        es1.close()
        ps_pr = [PS(es3, f"ps_pr{i}", [128, 512]) for i in range(4)]
        ps_tr = PS(es3, "ps_tr", [128, 128], BF16)
        ps_ss = [PS(es3, f"ps_ss{i}", [1, 512]) for i in range(2)]
        ps_pv = PS(es3, "ps_pv", [128, 129])
        es2b = ExitStack()
        qx = SB(es2b, "qx", [128, DC, T], FP8)
        es3.close()
        es2.close()
        esX.close()
        es4 = ExitStack()
        t_sb = SBR(es4, "t_sb", [128, IC, T], BF16)
        es4a = ExitStack()
        wg_st = SBR(es4a, "wg_st", [128, DC * 128])
        wu_st = SBR(es4a, "wu_st", [128, DC * 128])
        qwg = SBR(es4a, "qwg", [128, DC, 128], FP8)
        qwu = SBR(es4a, "qwu", [128, DC, 128], FP8)
        gu_sb = SBR(es4a, "gu_sb", [128, 1024])
        sil_sb = SBR(es4a, "sil_sb", [128, 512])
        rug = SBR(es4a, "rug", [128, 512])
        ps_gg = [PS(es4a, f"ps_gg{i}", [128, 512]) for i in range(2)]
        ps_uu = [PS(es4a, f"ps_uu{i}", [128, 512]) for i in range(2)]
        es4a_hold = es4a
        es5 = ExitStack()
        es2b.close()
        qt = SB(es5, "qt", [128, IC, T], FP8)
        rm_t = SB(es5, "rm_t", [128, T])
        rut448 = SB(es5, "rut448", [128, T])
        sxtb = SB(es5, "sxtb", [128, T])
        es4a.close()
        es4.close()
        es6 = ExitStack()
        qwd = SBR(es6, "qwd", [128, IC, 8 * 128], FP8)
        wd_st = [SBR(es6, f"wd_st{i}", [128, D]) for i in range(2)]
        rm_wd = SBR(es6, "rm_wd", [128, D])
        ruwd448 = SBR(es6, "ruwd448", [128, D])
        wo_st = SBR(es6, "wo_st", [128, H * 128])
        wo_bf = SBR(es6, "wo_bf", [128, H * 128], BF16)
        xo_sb = SBR(es6, "xo_sb", [128, T])
        out_sb = SBR(es6, "out_sb", [128, T])
        ps_d = [PS(es6, f"ps_d{i}", [128, 512]) for i in range(2)]
        ps_wo = [PS(es6, f"ps_wo{i}", [128, 512]) for i in range(2)]

        blk = es.enter_context(nc.Block())

        def walk(ctx: Ctx):
            sx, sv, ssc, st_, sg, sst, sph = (ctx.sems[k] for k in
                                              ("x", "v", "s", "t", "g", "st", "ph"))
            phase_n = [0]

            def barrier():
                phase_n[0] += 1
                tgt = 5 * phase_n[0]
                for e in ENGINES:
                    ctx.do(e, lambda E: E.nop(), inc=sph)
                if ctx.eng is not None:
                    getattr(ctx.nc, ctx.eng).wait_ge(sph.h, tgt)

            m_row = rows[:, 0:EXT]
            e2_row = rows[:, EXT:2 * EXT]
            w_row = rows[:, 2 * EXT:3 * EXT]
            r_row = rows[:, 3 * EXT:4 * EXT]

            for src, dst in ((qs_i, qs), (ks_i, ks), (bgb_i, bgb), (mse_i, mse)):
                ctx.do("sync", lambda E, s=src, d=dst: E.dma_start(out=d[:], in_=s),
                       inc=sx, dma=True)
            ctx.do("sync", lambda E: E.dma_start(out=xr[0][:, 0:128], in_=id_i),
                   inc=sx, dma=True)
            ctx.do("gpsimd", lambda E: E.memset(ones[:], 1.0), inc=sg)
            ctx.do("gpsimd", lambda E: E.memset(v_augh[:], 0.0), inc=sg)
            ctx.do("gpsimd", lambda E: E.memset(v_augh[:, :, 128:129], 1.0), inc=sg)
            ctx.wait("vector", sx)
            ctx.do("vector", lambda E: E.tensor_copy(ident[:], xr[0][:, 0:128]), inc=sv)
            barrier()

            for c in range(DC):
                b = c % 2
                ctx.wait("sync", st_)
                ctx.do("sync", lambda E, c=c, b=b: E.dma_start(
                    out=xr[b][:], in_=xT[c * 128:(c + 1) * 128, :]), inc=sx, dma=True)
                ctx.wait("scalar", sx, st_)
                ctx.do("scalar", lambda E, b=b: E.activation(
                    xr[2][:], xr[b][:], ACT.Square), inc=ssc)
                ctx.wait("tensor", sx, ssc)
                for h3 in range(3):
                    ctx.do("tensor", lambda E, c=c, b=b, h3=h3: E.matmul(
                        ps_a1[h3][:], ones[:], xr[b][:, h3 * 512:(h3 + 1) * 512],
                        start=(c == 0), stop=(c == DC - 1)), inc=st_)
                    ctx.do("tensor", lambda E, c=c, h3=h3: E.matmul(
                        ps_b1[h3][:], ones[:], xr[2][:, h3 * 512:(h3 + 1) * 512],
                        start=(c == 0), stop=(c == DC - 1)), inc=st_)
            ctx.wait("vector", st_)
            for h3 in range(3):
                ctx.do("vector", lambda E, h3=h3: E.tensor_scalar(
                    m_row[:, h3 * 512:(h3 + 1) * 512], ps_a1[h3][:], 1.0 / D, None,
                    A.mult), inc=sv)
                ctx.do("vector", lambda E, h3=h3: E.tensor_scalar(
                    e2_row[:, h3 * 512:(h3 + 1) * 512], ps_b1[h3][:], 1.0 / D, None,
                    A.mult), inc=sv)
            ctx.do("vector", lambda E: E.scalar_tensor_tensor(
                w_row, m_row, -1.0, m_row, A.mult, A.mult), inc=sv)
            ctx.do("vector", lambda E: E.tensor_tensor(w_row, e2_row, w_row, A.add), inc=sv)
            ctx.do("vector", lambda E: E.tensor_scalar(w_row, w_row, LN_EPS, None, A.add), inc=sv)
            ctx.wait("scalar", sv)
            ctx.do("scalar", lambda E: E.activation(r_row, w_row, ACT.Sqrt), inc=ssc)
            ctx.wait("vector", ssc)
            ctx.do("vector", lambda E: E.reciprocal(e2_row, r_row), inc=sv)
            ctx.wait("vector", sv)
            ctx.do("vector", lambda E: E.tensor_tensor(r_row, e2_row, e2_row, A.mult), inc=sv)
            ctx.do("vector", lambda E: E.tensor_tensor(r_row, w_row, r_row, A.mult), inc=sv)
            ctx.do("vector", lambda E: E.tensor_scalar(r_row, r_row, -0.5, 1.5, A.mult, A.add), inc=sv)
            ctx.do("vector", lambda E: E.tensor_tensor(r_row, e2_row, r_row, A.mult), inc=sv)
            ctx.wait("gpsimd", sv)
            ctx.do("gpsimd", lambda E: E.partition_broadcast(mb[:], m_row, channels=128), inc=sg)
            ctx.do("gpsimd", lambda E: E.partition_broadcast(rb[:], r_row, channels=128), inc=sg)
            barrier()

            for c in range(DC):
                b = c % 2
                ctx.wait("sync", sv)
                ctx.do("sync", lambda E, c=c, b=b: E.dma_start(
                    out=xr[b][:], in_=xT[c * 128:(c + 1) * 128, :]), inc=sx, dma=True)
                ctx.wait("vector", sx, sg)
                ctx.do("vector", lambda E, b=b: E.scalar_tensor_tensor(
                    xr[2][:], mb[:], -1.0, xr[b][:], A.mult, A.add), inc=sv)
                ctx.do("vector", lambda E, c=c: E.tensor_tensor(
                    xn[:, c, :], xr[2][:], rb[:], A.mult), inc=sv)
            barrier()

            ctx.do("sync", lambda E: E.dma_start(
                out=wf_st[0][:].rearrange("p (c w) -> p c w", c=DC)[:, :, 0:H],
                in_=WgT.rearrange("(c p) f -> p c f", p=128)), inc=sx, dma=True)
            ctx.wait("vector", sx)
            ctx.do("vector", lambda E: E.tensor_copy(
                wf_bf[0][:].rearrange("p (c w) -> p c w", c=DC)[:, :, 0:H],
                wf_st[0][:].rearrange("p (c w) -> p c w", c=DC)[:, :, 0:H]), inc=sv)
            for tc in range(TC):
                ctx.wait("tensor", sv)
                for c in range(DC):
                    ctx.do("tensor", lambda E, tc=tc, c=c: E.matmul(
                        ps_pr[3][:, 0:H],
                        xn[:, c, 512 + tc * 128: 512 + (tc + 1) * 128],
                        wf_bf[0][:].rearrange("p (c w) -> p c w", c=DC)[:, c, 0:H],
                        start=(c == 0), stop=(c == DC - 1)), inc=st_)
                ctx.wait("vector", st_, ssc)
                ctx.do("vector", lambda E: E.tensor_tensor(
                    xr[2][:, 0:H], ps_pr[3][:, 0:H], bgb[:], A.add), inc=sv)
                ctx.wait("scalar", sv)
                ctx.do("scalar", lambda E, tc=tc: E.activation(
                    gates[:, tc * H:(tc + 1) * H], xr[2][:, 0:H], ACT.Sigmoid), inc=ssc)
            ctx.do("vector", lambda E: E.tensor_reduce(
                rm_x[:], xn[:, :, 512:1536].rearrange("p c t -> p t c"), AXX, A.max,
                apply_absolute_value=True), inc=sv)
            ctx.wait("gpsimd", sv)
            ctx.do("gpsimd", lambda E: E.partition_all_reduce(rm_x[:], rm_x[:], 128, RMax), inc=sg)
            ctx.wait("vector", sg)
            ctx.do("vector", lambda E: E.reciprocal(rux448[:], rm_x[:]), inc=sv)
            ctx.wait("vector", sv)
            ctx.do("vector", lambda E: E.tensor_scalar(rux448[:], rux448[:], 448.0, None, A.mult), inc=sv)
            ctx.do("vector", lambda E: E.tensor_scalar(sxb[:], rm_x[:], 4.0 / 448.0, None, A.mult), inc=sv)
            for c in range(DC):
                ctx.do("vector", lambda E, c=c: E._custom_dve(
                    QCAST, out=qx[:, c, :], in0=xn[:, c, 512:1536], in1=rux448[:],
                    imm2=0.5), inc=sv)
            barrier()

            ssr = rows[:, 0:EXT]
            r0r = rows[:, EXT:2 * EXT]
            t1r = rows[:, 2 * EXT:3 * EXT]

            def l2norm_seq(psums, nhalf, scale8):
                ctx.wait("scalar", st_)
                for hf in range(nhalf):
                    ctx.do("scalar", lambda E, hf=hf: E.activation(
                        xr[2][:, hf * 512:(hf + 1) * 512], psums[hf][:], ACT.Square), inc=ssc)
                n = nhalf * 512
                for hf in range(nhalf):
                    ctx.wait("tensor", ssc, sv)
                    ctx.do("tensor", lambda E, hf=hf: E.matmul(
                        ps_ss[hf % 2][:], ones[:], xr[2][:, hf * 512:(hf + 1) * 512],
                        start=True, stop=True), inc=st_)
                    ctx.wait("vector", st_)
                    ctx.do("vector", lambda E, hf=hf: E.tensor_scalar(
                        ssr[:, hf * 512:(hf + 1) * 512], ps_ss[hf % 2][:], 1e-24, None,
                        A.max), inc=sv)
                ctx.wait("scalar", sv)
                ctx.do("scalar", lambda E, n=n: E.activation(
                    r0r[:, 0:n], ssr[:, 0:n], ACT.Sqrt), inc=ssc)
                ctx.wait("vector", ssc)
                ctx.do("vector", lambda E, n=n: E.reciprocal(t1r[:, 0:n], r0r[:, 0:n]), inc=sv)
                ctx.wait("vector", sv)
                ctx.do("vector", lambda E, n=n: E.tensor_tensor(
                    r0r[:, 0:n], t1r[:, 0:n], t1r[:, 0:n], A.mult), inc=sv)
                ctx.do("vector", lambda E, n=n: E.tensor_tensor(
                    r0r[:, 0:n], ssr[:, 0:n], r0r[:, 0:n], A.mult), inc=sv)
                ctx.do("vector", lambda E, n=n: E.tensor_scalar(
                    r0r[:, 0:n], r0r[:, 0:n], -0.5, 1.5, A.mult, A.add), inc=sv)
                ctx.do("vector", lambda E, n=n: E.tensor_tensor(
                    r0r[:, 0:n], t1r[:, 0:n], r0r[:, 0:n], A.mult), inc=sv)
                if scale8:
                    ctx.do("vector", lambda E, n=n: E.tensor_scalar(
                        r0r[:, 0:n], r0r[:, 0:n], 8.0, None, A.mult), inc=sv)
                ctx.wait("gpsimd", sv)
                ctx.do("gpsimd", lambda E, n=n: E.partition_broadcast(
                    rnb[:, 0:n], r0r[:, 0:n], channels=128), inc=sg)

            for h in range(H):
                b = h % 2
                ctx.wait("sync", sv)
                ctx.do("sync", lambda E, h=h, b=b: E.dma_start(
                    out=wf_st[b][:].rearrange("p (c w) -> p c w", c=DC),
                    in_=WqT.rearrange("(c p) f -> p c f", p=128)[:, :, h * 128:(h + 1) * 128]),
                    inc=sx, dma=True)
                ctx.wait("vector", sx)
                ctx.do("vector", lambda E, b=b: E.tensor_copy(wf_bf[b][:], wf_st[b][:]), inc=sv)
                ctx.wait("tensor", sv, ssc)
                for hf in range(2):
                    for c in range(DC):
                        ctx.do("tensor", lambda E, b=b, hf=hf, c=c: E.matmul(
                            ps_pr[2 + hf][:], wf_bf[b][:, c * 128:(c + 1) * 128],
                            xn[:, c, 512 + hf * 512: 512 + (hf + 1) * 512],
                            start=(c == 0), stop=(c == DC - 1)), inc=st_)
                ctx.wait("scalar", st_)
                l2norm_seq([ps_pr[2], ps_pr[3]], 2, scale8=True)
                ctx.wait("vector", sg)
                for hf in range(2):
                    ctx.do("vector", lambda E, hf=hf: E.scalar_tensor_tensor(
                        q_h[:, hf * 512:(hf + 1) * 512], ps_pr[2 + hf][:], qs[:],
                        rnb[:, hf * 512:(hf + 1) * 512], A.mult, A.mult), inc=sv)
                ctx.wait("sync", sv)
                ctx.do("sync", lambda E, h=h, b=b: E.dma_start(
                    out=wf_st[b][:].rearrange("p (c w) -> p c w", c=DC),
                    in_=WkT.rearrange("(c p) f -> p c f", p=128)[:, :, h * 128:(h + 1) * 128]),
                    inc=sx, dma=True)
                ctx.wait("vector", sx)
                ctx.do("vector", lambda E, b=b: E.tensor_copy(wf_bf[b][:], wf_st[b][:]), inc=sv)
                ctx.wait("tensor", sv, ssc)
                for hf in range(3):
                    for c in range(DC):
                        ctx.do("tensor", lambda E, b=b, hf=hf, c=c: E.matmul(
                            ps_pr[hf][:], wf_bf[b][:, c * 128:(c + 1) * 128],
                            xn[:, c, hf * 512:(hf + 1) * 512],
                            start=(c == 0), stop=(c == DC - 1)), inc=st_)
                ctx.wait("scalar", st_)
                l2norm_seq([ps_pr[0], ps_pr[1], ps_pr[2]], 3, scale8=False)
                ctx.wait("vector", sg)
                for hf in range(3):
                    ctx.do("vector", lambda E, hf=hf: E.scalar_tensor_tensor(
                        k_h[:, hf * 512:(hf + 1) * 512], ps_pr[hf][:], ks[:],
                        rnb[:, hf * 512:(hf + 1) * 512], A.mult, A.mult), inc=sv)
                ctx.wait("sync", sv)
                ctx.do("sync", lambda E, h=h, b=b: E.dma_start(
                    out=wf_st[b][:].rearrange("p (c w) -> p c w", c=DC),
                    in_=WvT.rearrange("(c p) f -> p c f", p=128)[:, :, h * 128:(h + 1) * 128]),
                    inc=sx, dma=True)
                ctx.wait("vector", sx)
                ctx.do("vector", lambda E, b=b: E.tensor_copy(wf_bf[b][:], wf_st[b][:]), inc=sv)
                ctx.wait("tensor", sv, ssc)
                for hf in range(3):
                    for c in range(DC):
                        ctx.do("tensor", lambda E, b=b, hf=hf, c=c: E.matmul(
                            ps_pr[hf][:], wf_bf[b][:, c * 128:(c + 1) * 128],
                            xn[:, c, hf * 512:(hf + 1) * 512],
                            start=(c == 0), stop=(c == DC - 1)), inc=st_)
                ctx.wait("vector", st_)
                for hf in range(3):
                    ctx.do("vector", lambda E, hf=hf: E.tensor_copy(
                        vfm[:, hf * 512:(hf + 1) * 512], ps_pr[hf][:]), inc=sv)
                for tc in range(TCE):
                    ctx.wait("tensor", sv)
                    ctx.do("tensor", lambda E, tc=tc: E.transpose(
                        ps_tr[:], vfm[:, tc * 128:(tc + 1) * 128], ident[:]), inc=st_)
                    ctx.wait("vector", st_)
                    ctx.do("vector", lambda E, tc=tc: E.tensor_copy(
                        v_augh[:, tc, 0:128], ps_tr[:]), inc=sv)
                for qb in range(2):
                    for kc in range(8):
                        ctx.wait("tensor", sv, ssc)
                        ctx.do("tensor", lambda E, h=h, qb=qb, kc=kc: E.matmul(
                            ps_pr[kc % 2][:],
                            k_h[:, qb * 512 + kc * 128: qb * 512 + (kc + 1) * 128],
                            q_h[:, qb * 512:(qb + 1) * 512],
                            start=True, stop=True), inc=st_)
                        ctx.wait("vector", st_)
                        col = (qb * 8 + kc) * 2
                        ctx.do("vector", lambda E, kc=kc, col=col: E.tensor_mask_reduce(
                            sc_sb[:], ps_pr[kc % 2][:],
                            mse[:, col:col + 1], mse[:, col + 1:col + 2],
                            1.0, 0.0, A.max), inc=sv)
                        ctx.wait("scalar", sv)
                        ctx.do("scalar", lambda E, kc=kc: E.activation(
                            p_sb[:, kc, :], sc_sb[:], ACT.Exp), inc=ssc)
                    for qc in range(4):
                        ctx.wait("tensor", ssc, sv)
                        for kc in range(8):
                            ctx.do("tensor", lambda E, qb=qb, qc=qc, kc=kc: E.matmul(
                                ps_pv[:],
                                p_sb[:, kc, qc * 128:(qc + 1) * 128],
                                v_augh[:, qb * 4 + kc, 0:129],
                                start=(kc == 0), stop=(kc == 7)), inc=st_)
                        ctx.wait("vector", st_)
                        ctx.do("vector", lambda E: E.reciprocal(
                            oscl[:, 0:1], ps_pv[:, 128:129]), inc=sv)
                        ctx.wait("vector", sv)
                        tc = qb * 4 + qc
                        ctx.do("vector", lambda E, tc=tc, h=h: E.tensor_tensor(
                            oscl[:, 0:1], oscl[:, 0:1],
                            gates[:, tc * H + h: tc * H + h + 1], A.mult), inc=sv)
                        ctx.wait("scalar", sv)
                        ctx.do("scalar", lambda E: E.activation(
                            otm_sb[:], ps_pv[:, 0:128], ACT.Copy,
                            scale=oscl[:, 0:1]), inc=ssc)
                        ctx.wait("tensor", ssc)
                        ctx.do("tensor", lambda E: E.transpose(
                            ps_tr[:], otm_sb[:], ident[:]), inc=st_)
                        ctx.wait("vector", st_)
                        ctx.do("vector", lambda E, h=h, qb=qb, qc=qc: E.tensor_copy(
                            o_fm[:, h, qb * 512 + qc * 128: qb * 512 + (qc + 1) * 128],
                            ps_tr[:]), inc=sv)
            barrier()

            for ic in range(IC):
                ctx.wait("sync", sv, st_)
                ctx.do("sync", lambda E, ic=ic: E.dma_start(
                    out=wg_st[:].rearrange("p (c w) -> p c w", c=DC),
                    in_=wgP.rearrange("(c p) f -> p c f", p=128)[:, :, ic * 128:(ic + 1) * 128]),
                    inc=sx, dma=True)
                ctx.do("sync", lambda E, ic=ic: E.dma_start(
                    out=wu_st[:].rearrange("p (c w) -> p c w", c=DC),
                    in_=wuP.rearrange("(c p) f -> p c f", p=128)[:, :, ic * 128:(ic + 1) * 128]),
                    inc=sx, dma=True)
                ctx.wait("vector", sx, sst)
                for wi, wst in ((0, wg_st), (1, wu_st)):
                    ctx.do("vector", lambda E, wi=wi, wst=wst: E.tensor_reduce(
                        rug[:, 256 + wi * 128:256 + (wi + 1) * 128],
                        wst[:].rearrange("p (c w) -> p w c", c=DC), AXX, A.max,
                        apply_absolute_value=True), inc=sv)
                ctx.wait("gpsimd", sv)
                for wi in range(2):
                    ctx.do("gpsimd", lambda E, wi=wi: E.partition_all_reduce(
                        rug[:, 256 + wi * 128:256 + (wi + 1) * 128],
                        rug[:, 256 + wi * 128:256 + (wi + 1) * 128], 128, RMax), inc=sg)
                ctx.wait("scalar", sg)
                ctx.do("scalar", lambda E, ic=ic: E.dma_start(
                    out=wsg_pp[:, ic:ic + 1], in_=rug[0:1, 256:384]), inc=sst, dma=True)
                ctx.do("scalar", lambda E, ic=ic: E.dma_start(
                    out=wsu_pp[:, ic:ic + 1], in_=rug[0:1, 384:512]), inc=sst, dma=True)
                ctx.wait("vector", sg, sst)
                ctx.do("vector", lambda E, ic=ic: E.tensor_scalar(
                    wsg_pp[:, ic:ic + 1], wsg_pp[:, ic:ic + 1], 1.0 / 448.0, None,
                    A.mult), inc=sv)
                ctx.do("vector", lambda E, ic=ic: E.tensor_scalar(
                    wsu_pp[:, ic:ic + 1], wsu_pp[:, ic:ic + 1], 1.0 / 448.0, None,
                    A.mult), inc=sv)
                for wi in range(2):
                    ctx.do("vector", lambda E, wi=wi: E.reciprocal(
                        rug[:, wi * 128:(wi + 1) * 128],
                        rug[:, 256 + wi * 128:256 + (wi + 1) * 128]), inc=sv)
                    ctx.wait("vector", sv)
                    ctx.do("vector", lambda E, wi=wi: E.tensor_scalar(
                        rug[:, wi * 128:(wi + 1) * 128], rug[:, wi * 128:(wi + 1) * 128],
                        448.0, None, A.mult), inc=sv)
                for c in range(DC):
                    ctx.do("vector", lambda E, c=c: E._custom_dve(
                        QCAST, out=qwg[:, c, :],
                        in0=wg_st[:].rearrange("p (c w) -> p c w", c=DC)[:, c, :],
                        in1=rug[:, 0:128], imm2=0.5), inc=sv)
                    ctx.do("vector", lambda E, c=c: E._custom_dve(
                        QCAST, out=qwu[:, c, :],
                        in0=wu_st[:].rearrange("p (c w) -> p c w", c=DC)[:, c, :],
                        in1=rug[:, 128:256], imm2=0.5), inc=sv)
                ctx.wait("tensor", sv)
                for hf in range(2):
                    for c in range(DC):
                        ctx.do("tensor", lambda E, hf=hf, c=c: E.matmul(
                            ps_gg[hf][:], qwg[:, c, :], qx[:, c, hf * 512:(hf + 1) * 512],
                            start=(c == 0), stop=(c == DC - 1)), inc=st_)
                    for c in range(DC):
                        ctx.do("tensor", lambda E, hf=hf, c=c: E.matmul(
                            ps_uu[hf][:], qwu[:, c, :], qx[:, c, hf * 512:(hf + 1) * 512],
                            start=(c == 0), stop=(c == DC - 1)), inc=st_)
                ctx.wait("vector", st_)
                for hf in range(2):
                    ctx.wait("vector", ssc)
                    ctx.do("vector", lambda E, ic=ic, hf=hf: E.scalar_tensor_tensor(
                        gu_sb[:, 0:512], ps_gg[hf][:], wsg_pp[:, ic:ic + 1],
                        sxb[:, hf * 512:(hf + 1) * 512], A.mult, A.mult), inc=sv)
                    ctx.wait("scalar", sv)
                    ctx.do("scalar", lambda E: E.activation(
                        sil_sb[:], gu_sb[:, 0:512], ACT.Silu), inc=ssc)
                    ctx.do("vector", lambda E, ic=ic, hf=hf: E.scalar_tensor_tensor(
                        gu_sb[:, 512:1024], ps_uu[hf][:], wsu_pp[:, ic:ic + 1],
                        sxb[:, hf * 512:(hf + 1) * 512], A.mult, A.mult), inc=sv)
                    ctx.wait("vector", ssc)
                    ctx.do("vector", lambda E, ic=ic, hf=hf: E.tensor_tensor(
                        t_sb[:, ic, hf * 512:(hf + 1) * 512], sil_sb[:],
                        gu_sb[:, 512:1024], A.mult), inc=sv)
            barrier()

            ctx.do("vector", lambda E: E.tensor_reduce(
                rm_t[:], t_sb[:].rearrange("p c t -> p t c"), AXX, A.max,
                apply_absolute_value=True), inc=sv)
            ctx.wait("gpsimd", sv)
            ctx.do("gpsimd", lambda E: E.partition_all_reduce(rm_t[:], rm_t[:], 128, RMax), inc=sg)
            ctx.wait("vector", sg)
            ctx.do("vector", lambda E: E.reciprocal(rut448[:], rm_t[:]), inc=sv)
            ctx.wait("vector", sv)
            ctx.do("vector", lambda E: E.tensor_scalar(rut448[:], rut448[:], 448.0, None, A.mult), inc=sv)
            ctx.do("vector", lambda E: E.tensor_scalar(sxtb[:], rm_t[:], 4.0 / 448.0, None, A.mult), inc=sv)
            for ic in range(IC):
                ctx.do("vector", lambda E, ic=ic: E._custom_dve(
                    QCAST, out=qt[:, ic, :], in0=t_sb[:, ic, :], in1=rut448[:],
                    imm2=0.5), inc=sv)
            barrier()

            for ic in range(IC):
                b = ic % 2
                ctx.wait("sync", sv)
                ctx.do("sync", lambda E, ic=ic, b=b: E.dma_start(
                    out=wd_st[b][:], in_=wdT[ic * 128:(ic + 1) * 128, :]), inc=sx, dma=True)
                ctx.wait("vector", sx)
                ctx.do("vector", lambda E, ic=ic, b=b: E.scalar_tensor_tensor(
                    rm_wd[:], wd_st[b][:], -1.0,
                    (wd_st[b][:] if ic == 0 else rm_wd[:]), A.mult, A.max), inc=sv)
                ctx.do("vector", lambda E, b=b: E.scalar_tensor_tensor(
                    rm_wd[:], wd_st[b][:], 1.0, rm_wd[:], A.mult, A.max), inc=sv)
            ctx.wait("gpsimd", sv)
            ctx.do("gpsimd", lambda E: E.partition_all_reduce(rm_wd[:], rm_wd[:], 128, RMax), inc=sg)
            ctx.wait("scalar", sg)
            for oc in range(DC):
                ctx.do("scalar", lambda E, oc=oc: E.dma_start(
                    out=wsd_pp[:, oc:oc + 1], in_=rm_wd[0:1, oc * 128:(oc + 1) * 128]),
                    inc=sst, dma=True)
            ctx.wait("vector", sst, sg)
            ctx.do("vector", lambda E: E.tensor_scalar(
                wsd_pp[:], wsd_pp[:], 1.0 / 448.0, None, A.mult), inc=sv)
            ctx.do("vector", lambda E: E.reciprocal(ruwd448[:], rm_wd[:]), inc=sv)
            ctx.wait("vector", sv)
            ctx.do("vector", lambda E: E.tensor_scalar(ruwd448[:], ruwd448[:], 448.0, None, A.mult), inc=sv)
            barrier()

            for grp in range(2):
                for ic in range(IC):
                    b = ic % 2
                    ctx.wait("sync", sv)
                    ctx.do("sync", lambda E, ic=ic, b=b: E.dma_start(
                        out=wd_st[b][:], in_=wdT[ic * 128:(ic + 1) * 128, :]),
                        inc=sx, dma=True)
                    ctx.wait("vector", sx)
                    ctx.do("vector", lambda E, grp=grp, ic=ic, b=b: E._custom_dve(
                        QCAST, out=qwd[:, ic, :],
                        in0=wd_st[b][:, grp * 1024:(grp + 1) * 1024],
                        in1=ruwd448[:, grp * 1024:(grp + 1) * 1024], imm2=0.5), inc=sv)
                for oci in range(8):
                    oc = grp * 8 + oci
                    ctx.wait("sync", sv, sst)
                    ctx.do("sync", lambda E, oc=oc: E.dma_start(
                        out=xo_sb[:], in_=xT[oc * 128:(oc + 1) * 128, 512:1536]),
                        inc=sx, dma=True)
                    ctx.do("sync", lambda E, oc=oc: E.dma_start(
                        out=wo_st[:].rearrange("p (c w) -> p c w", c=H),
                        in_=WoT.rearrange("(c p) f -> p c f", p=128)[:, :, oc * 128:(oc + 1) * 128]),
                        inc=sx, dma=True)
                    ctx.wait("vector", sx)
                    ctx.do("vector", lambda E: E.tensor_copy(wo_bf[:], wo_st[:]), inc=sv)
                    ctx.wait("tensor", sv)
                    for hf in range(2):
                        for ic in range(IC):
                            ctx.do("tensor", lambda E, oci=oci, hf=hf, ic=ic: E.matmul(
                                ps_d[hf][:], qwd[:, ic, oci * 128:(oci + 1) * 128],
                                qt[:, ic, hf * 512:(hf + 1) * 512],
                                start=(ic == 0), stop=(ic == IC - 1)), inc=st_)
                        for h in range(H):
                            ctx.do("tensor", lambda E, hf=hf, h=h: E.matmul(
                                ps_wo[hf][:], wo_bf[:, h * 128:(h + 1) * 128],
                                o_fm[:, h, hf * 512:(hf + 1) * 512],
                                start=(h == 0), stop=(h == H - 1)), inc=st_)
                    ctx.wait("vector", st_, sst)
                    for hf in range(2):
                        sl = slice(hf * 512, (hf + 1) * 512)
                        ctx.do("vector", lambda E, oc=oc, hf=hf, sl=sl: E.scalar_tensor_tensor(
                            out_sb[:, sl], ps_d[hf][:], wsd_pp[:, oc:oc + 1],
                            sxtb[:, sl], A.mult, A.mult), inc=sv)
                        ctx.do("vector", lambda E, hf=hf, sl=sl: E.scalar_tensor_tensor(
                            out_sb[:, sl], ps_wo[hf][:], 1.0, out_sb[:, sl],
                            A.mult, A.add), inc=sv)
                        ctx.do("vector", lambda E, hf=hf, sl=sl: E.tensor_tensor(
                            out_sb[:, sl], out_sb[:, sl], xo_sb[:, sl], A.add), inc=sv)
                    ctx.wait("scalar", sv)
                    ctx.do("scalar", lambda E, oc=oc: E.dma_start(
                        out=out_d[oc * 128:(oc + 1) * 128, :], in_=out_sb[:]),
                        inc=sst, dma=True)
            ctx.wait("scalar", sst)
            ctx.wait("sync", sst)

        @blk.sync
        def _(_e):
            walk(Ctx("sync", nc, sems))

        @blk.scalar
        def _(_e):
            walk(Ctx("scalar", nc, sems))

        @blk.vector
        def _(_e):
            walk(Ctx("vector", nc, sems))

        @blk.tensor
        def _(_e):
            walk(Ctx("tensor", nc, sems))

        @blk.gpsimd
        def _(_e):
            walk(Ctx("gpsimd", nc, sems))

        es5.close()
        es6.close()

    inst_map = {}
    for lib in all_libraries:
        for it in lib.instructions:
            inst_map[it] = inst_map.get(it, 0) | (1 << lib.index)
    _bass_rust.insert_library_loads(nc, inst_map, len(all_libraries), _lib_std.index)
    mybir.codegen_inst_isa_subclasses(nc)
    return nc


def _stage_inputs(inputs):
    f32 = np.float32
    x = np.ascontiguousarray(np.asarray(inputs["L0_attn_x"], dtype=f32)).reshape(B * N, D)
    Wq = np.asarray(inputs["L0_attn_Wq"], dtype=f32)
    Wkv = np.asarray(inputs["L0_attn_Wkv"], dtype=f32)
    Wg = np.asarray(inputs["L0_attn_Wg"], dtype=f32)
    bg = np.asarray(inputs["L0_attn_bg"], dtype=f32)
    Wo = np.asarray(inputs["L0_attn_Wo"], dtype=f32)
    w_gate = np.asarray(inputs["L0_ffn_w_gate"], dtype=f32)
    w_up = np.asarray(inputs["L0_ffn_w_up"], dtype=f32)
    w_down = np.asarray(inputs["L0_ffn_w_down"], dtype=f32)
    perm = np.asarray(inputs["L0_ffn_down_reorder_index"]).astype(np.int64)
    q_scale = np.asarray(inputs["L0_attn_q_scale"], dtype=f32)
    k_scale = np.asarray(inputs["L0_attn_k_scale"], dtype=f32)

    shared = {
        "WqT": np.ascontiguousarray(Wq.T),
        "WkT": np.ascontiguousarray(Wkv[:H * DH].T),
        "WvT": np.ascontiguousarray(Wkv[H * DH:].T),
        "WgT": np.ascontiguousarray(Wg.T),
        "WoT": np.ascontiguousarray(Wo.T),
        "wgP": np.ascontiguousarray(w_gate[perm].T),
        "wuP": np.ascontiguousarray(w_up[perm].T),
        "wdT": np.ascontiguousarray(w_down.T),
        "qs": np.ascontiguousarray(q_scale.reshape(128, 1)),
        "ks": np.ascontiguousarray(k_scale.reshape(128, 1)),
        "bgb": np.ascontiguousarray(np.tile(bg[None, :], (128, 1))),
        "ident": np.eye(128, dtype=f32),
    }
    in_maps = []
    for c in range(NCORES):
        r0 = c * T
        own = x[r0:r0 + T]
        halo = np.zeros((512, D), dtype=f32) if c % 4 == 0 else x[r0 - 512:r0]
        xTc = np.ascontiguousarray(np.concatenate([halo, own], axis=0).T)
        msk = np.zeros((128, 32), dtype=f32)
        p = np.arange(128, dtype=f32)
        for qb in range(2):
            for kc in range(8):
                j = kc * 128 + p
                start = j - 512.0
                end = j + 1.0
                if (c % 4 == 0) and qb == 0 and kc < 4:
                    start = np.full(128, 512.0, dtype=f32)
                    end = np.full(128, 513.0, dtype=f32)
                col = (qb * 8 + kc) * 2
                msk[:, col] = start
                msk[:, col + 1] = end
        m = dict(shared)
        m["xT"] = xTc
        m["mse"] = msk
        in_maps.append(m)
    return in_maps


def kernel(**inputs):
    in_maps = _stage_inputs(inputs)
    nc = build_nc()
    res = run_bass_kernel_spmd(nc, in_maps, core_ids=list(range(NCORES)))
    out = np.empty((B * N, D), dtype=np.float32)
    for c in range(NCORES):
        out[c * T:(c + 1) * T, :] = np.asarray(res.results[c]["out"]).T
    return out.reshape(B, N, D)


# version 02
# speedup vs baseline: 2.2710x; 2.2710x over previous
import numpy as np
import ml_dtypes
import concourse.bass as bass
import concourse.mybir as mybir
from concourse.bass_utils import run_bass_kernel_spmd
from concourse import dve_ops
from concourse.dve_op import DveOp
from concourse.dve_spec import Spec, Src0, Src1, C2, lower
from concourse.dve_table_gen import dve_ver_for
import concourse.bass_isa as bass_isa
import bass_rust as _bass_rust
from concourse.library_config import all_libraries, standard as _lib_std
from contextlib import ExitStack

F32 = mybir.dt.float32
BF16 = mybir.dt.bfloat16
FP8 = mybir.dt.float8e4
A = mybir.AluOpType
ACT = mybir.ActivationFunctionType
AXX = mybir.AxisListType.X
RMax = bass_isa.ReduceOp.max

NCORES = 8
B, N, D = 2, 4096, 2048
H, DH = 8, 128
I = 5632
T = 1024
EXT = 1536
DC = D // 128
IC = I // 128
TC = T // 128
TCE = EXT // 128
VA = 132
LN_EPS = 1e-5
ENGINES = ("sync", "scalar", "vector", "tensor", "gpsimd")


def _qcast_ref(in0, in1, s0, s1, imm2):
    v = np.asarray(in0) * np.asarray(in1)
    q = v.astype(ml_dtypes.float8_e4m3fn).astype(v.dtype)
    return q * imm2


def _register_qcast():
    for op in dve_ops.OPS:
        if op.name == "QCAST":
            return op
    spec = Spec(body=(Src0 * Src1) * C2)
    ver = dve_ver_for("TRN2")
    sha = lower(spec, ver=ver, name="QCAST").sha(ver)
    op = DveOp("QCAST", spec, subdim=False, uops_sha={ver: sha}, reference=_qcast_ref)
    dve_ops.OPS.append(op)
    dve_ops.CUSTOM_DVE_SPECS["QCAST"] = spec
    dve_ops.CUSTOM_DVE_REFS["QCAST"] = _qcast_ref
    dve_ops._SUB_OPCODE_FOR_NAME["QCAST"] = dve_ops._CUSTOM_DVE_ROW_BASE + len(dve_ops.OPS) - 1
    return op


QCAST = _register_qcast()


class _Sem:
    __slots__ = ("h", "n")
    def __init__(self, h):
        self.h = h
        self.n = 0


SERIAL = False

class Ctx:
    def __init__(self, eng_name, nc, sems):
        self.eng = eng_name
        self.nc = nc
        self.sems = sems
        for s in sems.values():
            s.n = 0

    def _E(self):
        return getattr(self.nc, self.eng)

    def do(self, eng, fn, inc=None, dma=False):
        if self.eng == eng:
            E = getattr(self.nc, eng)
            if SERIAL:
                for s in self.sems.values():
                    if s.n > 0:
                        E.wait_ge(s.h, s.n)
            ins = fn(E)
            if inc is not None:
                ins.then_inc(inc.h, 16 if dma else 1)
        if inc is not None:
            inc.n += 16 if dma else 1

    def wait(self, eng, *sems):
        if self.eng == eng:
            for s in sems:
                getattr(self.nc, eng).wait_ge(s.h, s.n)


def build_nc():
    nc = bass.Bass("TRN2", target_bir_lowering=False)
    din = lambda name, shape, dt=F32: nc.dram_tensor(name, shape, dt, kind="ExternalInput").ap()
    xT = din("xT", [D, EXT])
    WqT = din("WqT", [D, H * DH])
    WkT = din("WkT", [D, H * DH])
    WvT = din("WvT", [D, H * DH])
    WgT = din("WgT", [D, H])
    WoT = din("WoT", [H * DH, D])
    wgP = din("wgP", [D, I])
    wuP = din("wuP", [D, I])
    wdT = din("wdT", [I, D])
    qs_i = din("qs", [128, 1])
    ks_i = din("ks", [128, 1])
    bgb_i = din("bgb", [128, H])
    mse_i = din("mse", [128, 32])
    id_i = din("ident", [128, 128])
    out_d = nc.dram_tensor("out", [D, T], F32, kind="ExternalOutput").ap()

    es = ExitStack()
    with es:
        SB = lambda st, name, shape, dt=F32: st.enter_context(nc.sbuf_tensor(name, shape, dt))
        SBR = lambda st, name, shape, dt=F32: st.enter_context(nc.sbuf_tensor(name, shape, dt, side="right"))
        PS = lambda st, name, shape, dt=F32: st.enter_context(nc.psum_tensor(name, shape, dt))
        qs = SB(es, "qs_t", [128, 1]); ks = SB(es, "ks_t", [128, 1])
        bgb = SB(es, "bgb_t", [128, H]); mse = SB(es, "mse_t", [128, 32])
        ident = SB(es, "ident_b", [128, 128], BF16)
        ones = SB(es, "ones_t", [128, 1])
        gates = SB(es, "gates_t", [128, TC * H])
        sxb = SB(es, "sxb", [128, T])
        oscl = SB(es, "oscl", [128, 2])
        wsg_pp = SB(es, "wsg_pp", [128, IC]); wsu_pp = SB(es, "wsu_pp", [128, IC])
        wsd_pp = SB(es, "wsd_pp", [128, DC])
        o_fm = SB(es, "o_fm", [128, H, T], BF16)
        vfm = SB(es, "vfm", [128, EXT], BF16)

        sems = {k: _Sem(es.enter_context(nc.semaphore("sem_" + k)))
                for k in ("x", "v", "s", "t", "g", "st", "ph")}

        es1 = ExitStack()
        mb = SB(es1, "mb", [128, EXT]); rb = SB(es1, "rb", [128, EXT])
        ps_a1 = [PS(es1, f"ps_a1{i}", [1, 512]) for i in range(3)]
        ps_b1 = [PS(es1, f"ps_b1{i}", [1, 512]) for i in range(3)]
        esX = ExitStack()
        rows = SBR(esX, "rows", [1, 4 * EXT])
        xr = [SBR(esX, f"xr{i}", [128, EXT]) for i in range(3)]
        es2 = ExitStack()
        xn = SBR(es2, "xn", [128, DC, EXT], BF16)
        rux448 = SBR(es2, "rux448", [128, T])
        rm_x = SBR(es2, "rm_x", [128, T])
        es3 = ExitStack()
        q_h = SBR(es3, "q_h", [128, T], BF16)
        k_h = SBR(es3, "k_h", [128, EXT], BF16)
        v_augh = SBR(es3, "v_augh", [128, TCE, VA], BF16)
        wf_st = [SBR(es3, f"wf_st{i}", [128, DC * 128]) for i in range(2)]
        wf_bf = [SBR(es3, f"wf_bf{i}", [128, DC * 128], BF16) for i in range(2)]
        rnb = SBR(es3, "rnb", [128, EXT])
        p_sb = SBR(es3, "p_sb", [128, 8, 512], BF16)
        sc_sb = SBR(es3, "sc_sb", [128, 512])
        otm_sb = SBR(es3, "otm", [128, 128], BF16)
        es1.close()
        ps_pr = [PS(es3, f"ps_pr{i}", [128, 512]) for i in range(4)]
        ps_tr_full = PS(es3, "ps_tr", [128, 1024], BF16)
        ps_tr = ps_tr_full
        ps_ss = [PS(es3, f"ps_ss{i}", [1, 512]) for i in range(2)]
        ps_pv = PS(es3, "ps_pv", [128, 512])
        es2b = ExitStack()
        qx = SB(es2b, "qx", [128, DC, T], FP8)
        es3.close()
        es2.close()
        esX.close()
        es4 = ExitStack()
        t_sb = SBR(es4, "t_sb", [128, IC, T], BF16)
        es4a = ExitStack()
        wg_st = SBR(es4a, "wg_st", [128, DC * 128])
        wu_st = SBR(es4a, "wu_st", [128, DC * 128])
        qwg = SBR(es4a, "qwg", [128, DC, 128], FP8)
        qwu = SBR(es4a, "qwu", [128, DC, 128], FP8)
        gu_sb = SBR(es4a, "gu_sb", [128, 1024])
        sil_sb = SBR(es4a, "sil_sb", [128, 512])
        rug = SBR(es4a, "rug", [128, 512])
        ps_gg = [PS(es4a, f"ps_gg{i}", [128, 512]) for i in range(2)]
        ps_uu = [PS(es4a, f"ps_uu{i}", [128, 512]) for i in range(2)]
        es4a_hold = es4a
        es5 = ExitStack()
        es2b.close()
        qt = SB(es5, "qt", [128, IC, T], FP8)
        rm_t = SB(es5, "rm_t", [128, T])
        rut448 = SB(es5, "rut448", [128, T])
        sxtb = SB(es5, "sxtb", [128, T])
        es4a.close()
        es4.close()
        es6 = ExitStack()
        qwd = SBR(es6, "qwd", [128, IC, 8 * 128], FP8)
        wd_st = [SBR(es6, f"wd_st{i}", [128, D]) for i in range(2)]
        rm_wd = SBR(es6, "rm_wd", [128, D])
        ruwd448 = SBR(es6, "ruwd448", [128, D])
        wo_st = SBR(es6, "wo_st", [128, H * 128])
        wo_bf = SBR(es6, "wo_bf", [128, H * 128], BF16)
        xo_sb = SBR(es6, "xo_sb", [128, T])
        out_sb = SBR(es6, "out_sb", [128, T])
        ps_d = [PS(es6, f"ps_d{i}", [128, 512]) for i in range(2)]
        ps_wo = [PS(es6, f"ps_wo{i}", [128, 512]) for i in range(2)]

        blk = es.enter_context(nc.Block())

        def walk(ctx: Ctx):
            sx, sv, ssc, st_, sg, sst, sph = (ctx.sems[k] for k in
                                              ("x", "v", "s", "t", "g", "st", "ph"))
            phase_n = [0]

            def barrier():
                phase_n[0] += 1
                tgt = 5 * phase_n[0]
                for e in ENGINES:
                    ctx.do(e, lambda E: E.nop(), inc=sph)
                if ctx.eng is not None:
                    getattr(ctx.nc, ctx.eng).wait_ge(sph.h, tgt)

            m_row = rows[:, 0:EXT]
            e2_row = rows[:, EXT:2 * EXT]
            w_row = rows[:, 2 * EXT:3 * EXT]
            r_row = rows[:, 3 * EXT:4 * EXT]

            for src, dst in ((qs_i, qs), (ks_i, ks), (bgb_i, bgb), (mse_i, mse)):
                ctx.do("sync", lambda E, s=src, d=dst: E.dma_start(out=d[:], in_=s),
                       inc=sx, dma=True)
            ctx.do("sync", lambda E: E.dma_start(out=xr[0][:, 0:128], in_=id_i),
                   inc=sx, dma=True)
            ctx.do("gpsimd", lambda E: E.memset(ones[:], 1.0), inc=sg)
            ctx.do("gpsimd", lambda E: E.memset(v_augh[:], 0.0), inc=sg)
            ctx.do("gpsimd", lambda E: E.memset(v_augh[:, :, 128:129], 1.0), inc=sg)
            ctx.wait("vector", sx)
            ctx.do("vector", lambda E: E.tensor_copy(ident[:], xr[0][:, 0:128]), inc=sv)
            barrier()

            for c in range(DC):
                b = c % 2
                ctx.wait("sync", st_)
                ctx.do("sync", lambda E, c=c, b=b: E.dma_start(
                    out=xr[b][:], in_=xT[c * 128:(c + 1) * 128, :]), inc=sx, dma=True)
                ctx.wait("scalar", sx, st_)
                ctx.do("scalar", lambda E, b=b: E.activation(
                    xr[2][:], xr[b][:], ACT.Square), inc=ssc)
                ctx.wait("tensor", sx, ssc)
                for h3 in range(3):
                    ctx.do("tensor", lambda E, c=c, b=b, h3=h3: E.matmul(
                        ps_a1[h3][:], ones[:], xr[b][:, h3 * 512:(h3 + 1) * 512],
                        start=(c == 0), stop=(c == DC - 1)), inc=st_)
                    ctx.do("tensor", lambda E, c=c, h3=h3: E.matmul(
                        ps_b1[h3][:], ones[:], xr[2][:, h3 * 512:(h3 + 1) * 512],
                        start=(c == 0), stop=(c == DC - 1)), inc=st_)
            ctx.wait("vector", st_)
            for h3 in range(3):
                ctx.do("vector", lambda E, h3=h3: E.tensor_scalar(
                    m_row[:, h3 * 512:(h3 + 1) * 512], ps_a1[h3][:], 1.0 / D, None,
                    A.mult), inc=sv)
                ctx.do("vector", lambda E, h3=h3: E.tensor_scalar(
                    e2_row[:, h3 * 512:(h3 + 1) * 512], ps_b1[h3][:], 1.0 / D, None,
                    A.mult), inc=sv)
            ctx.do("vector", lambda E: E.scalar_tensor_tensor(
                w_row, m_row, -1.0, m_row, A.mult, A.mult), inc=sv)
            ctx.do("vector", lambda E: E.tensor_tensor(w_row, e2_row, w_row, A.add), inc=sv)
            ctx.do("vector", lambda E: E.tensor_scalar(w_row, w_row, LN_EPS, None, A.add), inc=sv)
            ctx.wait("scalar", sv)
            ctx.do("scalar", lambda E: E.activation(r_row, w_row, ACT.Sqrt), inc=ssc)
            ctx.wait("vector", ssc)
            ctx.do("vector", lambda E: E.reciprocal(e2_row, r_row), inc=sv)
            ctx.wait("vector", sv)
            ctx.do("vector", lambda E: E.tensor_tensor(r_row, e2_row, e2_row, A.mult), inc=sv)
            ctx.do("vector", lambda E: E.tensor_tensor(r_row, w_row, r_row, A.mult), inc=sv)
            ctx.do("vector", lambda E: E.tensor_scalar(r_row, r_row, -0.5, 1.5, A.mult, A.add), inc=sv)
            ctx.do("vector", lambda E: E.tensor_tensor(r_row, e2_row, r_row, A.mult), inc=sv)
            ctx.wait("gpsimd", sv)
            ctx.do("gpsimd", lambda E: E.partition_broadcast(mb[:], m_row, channels=128), inc=sg)
            ctx.do("gpsimd", lambda E: E.partition_broadcast(rb[:], r_row, channels=128), inc=sg)
            barrier()

            for c in range(DC):
                b = c % 2
                ctx.wait("sync", sv)
                ctx.do("sync", lambda E, c=c, b=b: E.dma_start(
                    out=xr[b][:], in_=xT[c * 128:(c + 1) * 128, :]), inc=sx, dma=True)
                ctx.wait("vector", sx, sg)
                ctx.do("vector", lambda E, b=b: E.scalar_tensor_tensor(
                    xr[2][:], mb[:], -1.0, xr[b][:], A.mult, A.add), inc=sv)
                ctx.do("vector", lambda E, c=c: E.tensor_tensor(
                    xn[:, c, :], xr[2][:], rb[:], A.mult), inc=sv)
            barrier()

            ctx.do("sync", lambda E: E.dma_start(
                out=wf_st[0][:].rearrange("p (c w) -> p c w", c=DC)[:, :, 0:H],
                in_=WgT.rearrange("(c p) f -> p c f", p=128)), inc=sx, dma=True)
            ctx.wait("vector", sx)
            ctx.do("vector", lambda E: E.tensor_copy(
                wf_bf[0][:].rearrange("p (c w) -> p c w", c=DC)[:, :, 0:H],
                wf_st[0][:].rearrange("p (c w) -> p c w", c=DC)[:, :, 0:H]), inc=sv)
            for tc in range(TC):
                ctx.wait("tensor", sv)
                for c in range(DC):
                    ctx.do("tensor", lambda E, tc=tc, c=c: E.matmul(
                        ps_pr[3][:, 0:H],
                        xn[:, c, 512 + tc * 128: 512 + (tc + 1) * 128],
                        wf_bf[0][:].rearrange("p (c w) -> p c w", c=DC)[:, c, 0:H],
                        start=(c == 0), stop=(c == DC - 1)), inc=st_)
                ctx.wait("vector", st_, ssc)
                ctx.do("vector", lambda E: E.tensor_tensor(
                    xr[2][:, 0:H], ps_pr[3][:, 0:H], bgb[:], A.add), inc=sv)
                ctx.wait("scalar", sv)
                ctx.do("scalar", lambda E, tc=tc: E.activation(
                    gates[:, tc * H:(tc + 1) * H], xr[2][:, 0:H], ACT.Sigmoid), inc=ssc)
            ctx.do("vector", lambda E: E.tensor_reduce(
                rm_x[:], xn[:, :, 512:1536].rearrange("p c t -> p t c"), AXX, A.max,
                apply_absolute_value=True), inc=sv)
            ctx.wait("gpsimd", sv)
            ctx.do("gpsimd", lambda E: E.partition_all_reduce(rm_x[:], rm_x[:], 128, RMax), inc=sg)
            ctx.wait("vector", sg)
            ctx.do("vector", lambda E: E.reciprocal(rux448[:], rm_x[:]), inc=sv)
            ctx.wait("vector", sv)
            ctx.do("vector", lambda E: E.tensor_scalar(rux448[:], rux448[:], 448.0, None, A.mult), inc=sv)
            ctx.do("vector", lambda E: E.tensor_scalar(sxb[:], rm_x[:], 4.0 / 448.0, None, A.mult), inc=sv)
            for c in range(DC):
                ctx.do("vector", lambda E, c=c: E._custom_dve(
                    QCAST, out=qx[:, c, :], in0=xn[:, c, 512:1536], in1=rux448[:],
                    imm2=0.5), inc=sv)
            barrier()

            ssr = rows[:, 0:EXT]
            r0r = rows[:, EXT:2 * EXT]
            t1r = rows[:, 2 * EXT:3 * EXT]

            def l2norm_seq(psums, nhalf, scale8):
                ctx.wait("scalar", st_)
                for hf in range(nhalf):
                    ctx.do("scalar", lambda E, hf=hf: E.activation(
                        xr[2][:, hf * 512:(hf + 1) * 512], psums[hf][:], ACT.Square), inc=ssc)
                n = nhalf * 512
                for hf in range(nhalf):
                    ctx.wait("tensor", ssc, sv)
                    ctx.do("tensor", lambda E, hf=hf: E.matmul(
                        ps_ss[hf % 2][:], ones[:], xr[2][:, hf * 512:(hf + 1) * 512],
                        start=True, stop=True), inc=st_)
                    ctx.wait("vector", st_)
                    ctx.do("vector", lambda E, hf=hf: E.tensor_scalar(
                        ssr[:, hf * 512:(hf + 1) * 512], ps_ss[hf % 2][:], 1e-24, None,
                        A.max), inc=sv)
                ctx.wait("scalar", sv)
                ctx.do("scalar", lambda E, n=n: E.activation(
                    r0r[:, 0:n], ssr[:, 0:n], ACT.Sqrt), inc=ssc)
                ctx.wait("vector", ssc)
                ctx.do("vector", lambda E, n=n: E.reciprocal(t1r[:, 0:n], r0r[:, 0:n]), inc=sv)
                ctx.wait("vector", sv)
                ctx.do("vector", lambda E, n=n: E.tensor_tensor(
                    r0r[:, 0:n], t1r[:, 0:n], t1r[:, 0:n], A.mult), inc=sv)
                ctx.do("vector", lambda E, n=n: E.tensor_tensor(
                    r0r[:, 0:n], ssr[:, 0:n], r0r[:, 0:n], A.mult), inc=sv)
                ctx.do("vector", lambda E, n=n: E.tensor_scalar(
                    r0r[:, 0:n], r0r[:, 0:n], -0.5, 1.5, A.mult, A.add), inc=sv)
                ctx.do("vector", lambda E, n=n: E.tensor_tensor(
                    r0r[:, 0:n], t1r[:, 0:n], r0r[:, 0:n], A.mult), inc=sv)
                if scale8:
                    ctx.do("vector", lambda E, n=n: E.tensor_scalar(
                        r0r[:, 0:n], r0r[:, 0:n], 8.0, None, A.mult), inc=sv)
                ctx.wait("gpsimd", sv)
                ctx.do("gpsimd", lambda E, n=n: E.partition_broadcast(
                    rnb[:, 0:n], r0r[:, 0:n], channels=128), inc=sg)

            for h in range(H):
                b = h % 2
                ctx.wait("sync", sv)
                ctx.do("sync", lambda E, h=h, b=b: E.dma_start(
                    out=wf_st[b][:].rearrange("p (c w) -> p c w", c=DC),
                    in_=WqT.rearrange("(c p) f -> p c f", p=128)[:, :, h * 128:(h + 1) * 128]),
                    inc=sx, dma=True)
                ctx.wait("vector", sx)
                ctx.do("vector", lambda E, b=b: E.tensor_copy(wf_bf[b][:], wf_st[b][:]), inc=sv)
                ctx.wait("tensor", sv, ssc)
                for hf in range(2):
                    for c in range(DC):
                        ctx.do("tensor", lambda E, b=b, hf=hf, c=c: E.matmul(
                            ps_pr[2 + hf][:], wf_bf[b][:, c * 128:(c + 1) * 128],
                            xn[:, c, 512 + hf * 512: 512 + (hf + 1) * 512],
                            start=(c == 0), stop=(c == DC - 1)), inc=st_)
                ctx.wait("scalar", st_)
                l2norm_seq([ps_pr[2], ps_pr[3]], 2, scale8=True)
                ctx.wait("vector", sg)
                for hf in range(2):
                    ctx.do("vector", lambda E, hf=hf: E.scalar_tensor_tensor(
                        q_h[:, hf * 512:(hf + 1) * 512], ps_pr[2 + hf][:], qs[:],
                        rnb[:, hf * 512:(hf + 1) * 512], A.mult, A.mult), inc=sv)
                ctx.wait("sync", sv)
                ctx.do("sync", lambda E, h=h, b=b: E.dma_start(
                    out=wf_st[b][:].rearrange("p (c w) -> p c w", c=DC),
                    in_=WkT.rearrange("(c p) f -> p c f", p=128)[:, :, h * 128:(h + 1) * 128]),
                    inc=sx, dma=True)
                ctx.wait("vector", sx)
                ctx.do("vector", lambda E, b=b: E.tensor_copy(wf_bf[b][:], wf_st[b][:]), inc=sv)
                ctx.wait("tensor", sv, ssc)
                for hf in range(3):
                    for c in range(DC):
                        ctx.do("tensor", lambda E, b=b, hf=hf, c=c: E.matmul(
                            ps_pr[hf][:], wf_bf[b][:, c * 128:(c + 1) * 128],
                            xn[:, c, hf * 512:(hf + 1) * 512],
                            start=(c == 0), stop=(c == DC - 1)), inc=st_)
                ctx.wait("scalar", st_)
                l2norm_seq([ps_pr[0], ps_pr[1], ps_pr[2]], 3, scale8=False)
                ctx.wait("vector", sg)
                for hf in range(3):
                    ctx.do("vector", lambda E, hf=hf: E.scalar_tensor_tensor(
                        k_h[:, hf * 512:(hf + 1) * 512], ps_pr[hf][:], ks[:],
                        rnb[:, hf * 512:(hf + 1) * 512], A.mult, A.mult), inc=sv)
                ctx.wait("sync", sv)
                ctx.do("sync", lambda E, h=h, b=b: E.dma_start(
                    out=wf_st[b][:].rearrange("p (c w) -> p c w", c=DC),
                    in_=WvT.rearrange("(c p) f -> p c f", p=128)[:, :, h * 128:(h + 1) * 128]),
                    inc=sx, dma=True)
                ctx.wait("vector", sx)
                ctx.do("vector", lambda E, b=b: E.tensor_copy(wf_bf[b][:], wf_st[b][:]), inc=sv)
                ctx.wait("tensor", sv, ssc)
                for hf in range(3):
                    for c in range(DC):
                        ctx.do("tensor", lambda E, b=b, hf=hf, c=c: E.matmul(
                            ps_pr[hf][:], wf_bf[b][:, c * 128:(c + 1) * 128],
                            xn[:, c, hf * 512:(hf + 1) * 512],
                            start=(c == 0), stop=(c == DC - 1)), inc=st_)
                ctx.wait("vector", st_)
                for hf in range(3):
                    ctx.do("vector", lambda E, hf=hf: E.tensor_copy(
                        vfm[:, hf * 512:(hf + 1) * 512], ps_pr[hf][:]), inc=sv)
                for tc in range(TCE):
                    ctx.wait("tensor", sv)
                    ctx.do("tensor", lambda E, tc=tc: E.transpose(
                        ps_tr[:, 0:128], vfm[:, tc * 128:(tc + 1) * 128], ident[:]), inc=st_)
                    ctx.wait("vector", st_)
                    ctx.do("vector", lambda E, tc=tc: E.tensor_copy(
                        v_augh[:, tc, 0:128], ps_tr[:, 0:128]), inc=sv)
                for qb in range(2):
                    for kc in range(8):
                        ctx.wait("tensor", sv, ssc)
                        ctx.do("tensor", lambda E, h=h, qb=qb, kc=kc: E.matmul(
                            ps_pr[kc % 2][:],
                            k_h[:, qb * 512 + kc * 128: qb * 512 + (kc + 1) * 128],
                            q_h[:, qb * 512:(qb + 1) * 512],
                            start=True, stop=True), inc=st_)
                        ctx.wait("vector", st_)
                        col = (qb * 8 + kc) * 2
                        ctx.do("vector", lambda E, kc=kc, col=col: E.tensor_mask_reduce(
                            sc_sb[:], ps_pr[kc % 2][:],
                            mse[:, col:col + 1], mse[:, col + 1:col + 2],
                            1.0, 0.0, A.max), inc=sv)
                        ctx.wait("scalar", sv)
                        ctx.do("scalar", lambda E, kc=kc: E.activation(
                            p_sb[:, kc, :], sc_sb[:], ACT.Exp), inc=ssc)
                    for qc in range(4):
                        ctx.wait("tensor", ssc, sv)
                        for kc in range(8):
                            ctx.do("tensor", lambda E, qb=qb, qc=qc, kc=kc: E.matmul(
                                ps_pv[:, 0:129],
                                p_sb[:, kc, qc * 128:(qc + 1) * 128],
                                v_augh[:, qb * 4 + kc, 0:129],
                                start=(kc == 0), stop=(kc == 7)), inc=st_)
                        ctx.wait("vector", st_)
                        ctx.do("vector", lambda E: E.reciprocal(
                            oscl[:, 0:1], ps_pv[:, 128:129]), inc=sv)
                        ctx.wait("vector", sv)
                        tc = qb * 4 + qc
                        ctx.do("vector", lambda E, tc=tc, h=h: E.tensor_tensor(
                            oscl[:, 0:1], oscl[:, 0:1],
                            gates[:, tc * H + h: tc * H + h + 1], A.mult), inc=sv)
                        ctx.wait("scalar", sv)
                        ctx.do("scalar", lambda E: E.activation(
                            otm_sb[:], ps_pv[:, 0:128], ACT.Copy,
                            scale=oscl[:, 0:1]), inc=ssc)
                        ctx.wait("tensor", ssc)
                        ctx.do("tensor", lambda E: E.transpose(
                            ps_tr[:, 0:128], otm_sb[:], ident[:]), inc=st_)
                        ctx.wait("vector", st_)
                        ctx.do("vector", lambda E, h=h, qb=qb, qc=qc: E.tensor_copy(
                            o_fm[:, h, qb * 512 + qc * 128: qb * 512 + (qc + 1) * 128],
                            ps_tr[:, 0:128]), inc=sv)
            barrier()

            for ic in range(IC):
                ctx.wait("sync", sv, st_)
                ctx.do("sync", lambda E, ic=ic: E.dma_start(
                    out=wg_st[:].rearrange("p (c w) -> p c w", c=DC),
                    in_=wgP.rearrange("(c p) f -> p c f", p=128)[:, :, ic * 128:(ic + 1) * 128]),
                    inc=sx, dma=True)
                ctx.do("sync", lambda E, ic=ic: E.dma_start(
                    out=wu_st[:].rearrange("p (c w) -> p c w", c=DC),
                    in_=wuP.rearrange("(c p) f -> p c f", p=128)[:, :, ic * 128:(ic + 1) * 128]),
                    inc=sx, dma=True)
                ctx.wait("vector", sx, sst)
                for wi, wst in ((0, wg_st), (1, wu_st)):
                    ctx.do("vector", lambda E, wi=wi, wst=wst: E.tensor_reduce(
                        rug[:, 256 + wi * 128:256 + (wi + 1) * 128],
                        wst[:].rearrange("p (c w) -> p w c", c=DC), AXX, A.max,
                        apply_absolute_value=True), inc=sv)
                ctx.wait("gpsimd", sv)
                for wi in range(2):
                    ctx.do("gpsimd", lambda E, wi=wi: E.partition_all_reduce(
                        rug[:, 256 + wi * 128:256 + (wi + 1) * 128],
                        rug[:, 256 + wi * 128:256 + (wi + 1) * 128], 128, RMax), inc=sg)
                ctx.wait("scalar", sg)
                ctx.do("scalar", lambda E, ic=ic: E.dma_start(
                    out=wsg_pp[:, ic:ic + 1], in_=rug[0:1, 256:384]), inc=sst, dma=True)
                ctx.do("scalar", lambda E, ic=ic: E.dma_start(
                    out=wsu_pp[:, ic:ic + 1], in_=rug[0:1, 384:512]), inc=sst, dma=True)
                ctx.wait("vector", sg, sst)
                ctx.do("vector", lambda E, ic=ic: E.tensor_scalar(
                    wsg_pp[:, ic:ic + 1], wsg_pp[:, ic:ic + 1], 1.0 / 448.0, None,
                    A.mult), inc=sv)
                ctx.do("vector", lambda E, ic=ic: E.tensor_scalar(
                    wsu_pp[:, ic:ic + 1], wsu_pp[:, ic:ic + 1], 1.0 / 448.0, None,
                    A.mult), inc=sv)
                for wi in range(2):
                    ctx.do("vector", lambda E, wi=wi: E.reciprocal(
                        rug[:, wi * 128:(wi + 1) * 128],
                        rug[:, 256 + wi * 128:256 + (wi + 1) * 128]), inc=sv)
                    ctx.wait("vector", sv)
                    ctx.do("vector", lambda E, wi=wi: E.tensor_scalar(
                        rug[:, wi * 128:(wi + 1) * 128], rug[:, wi * 128:(wi + 1) * 128],
                        448.0, None, A.mult), inc=sv)
                for c in range(DC):
                    ctx.do("vector", lambda E, c=c: E._custom_dve(
                        QCAST, out=qwg[:, c, :],
                        in0=wg_st[:].rearrange("p (c w) -> p c w", c=DC)[:, c, :],
                        in1=rug[:, 0:128], imm2=0.5), inc=sv)
                    ctx.do("vector", lambda E, c=c: E._custom_dve(
                        QCAST, out=qwu[:, c, :],
                        in0=wu_st[:].rearrange("p (c w) -> p c w", c=DC)[:, c, :],
                        in1=rug[:, 128:256], imm2=0.5), inc=sv)
                ctx.wait("tensor", sv)
                for hf in range(2):
                    for c in range(DC):
                        ctx.do("tensor", lambda E, hf=hf, c=c: E.matmul(
                            ps_gg[hf][:], qwg[:, c, :], qx[:, c, hf * 512:(hf + 1) * 512],
                            start=(c == 0), stop=(c == DC - 1)), inc=st_)
                    for c in range(DC):
                        ctx.do("tensor", lambda E, hf=hf, c=c: E.matmul(
                            ps_uu[hf][:], qwu[:, c, :], qx[:, c, hf * 512:(hf + 1) * 512],
                            start=(c == 0), stop=(c == DC - 1)), inc=st_)
                ctx.wait("vector", st_)
                for hf in range(2):
                    ctx.wait("vector", ssc)
                    ctx.do("vector", lambda E, ic=ic, hf=hf: E.scalar_tensor_tensor(
                        gu_sb[:, 0:512], ps_gg[hf][:], wsg_pp[:, ic:ic + 1],
                        sxb[:, hf * 512:(hf + 1) * 512], A.mult, A.mult), inc=sv)
                    ctx.wait("scalar", sv)
                    ctx.do("scalar", lambda E: E.activation(
                        sil_sb[:], gu_sb[:, 0:512], ACT.Silu), inc=ssc)
                    ctx.do("vector", lambda E, ic=ic, hf=hf: E.scalar_tensor_tensor(
                        gu_sb[:, 512:1024], ps_uu[hf][:], wsu_pp[:, ic:ic + 1],
                        sxb[:, hf * 512:(hf + 1) * 512], A.mult, A.mult), inc=sv)
                    ctx.wait("vector", ssc)
                    ctx.do("vector", lambda E, ic=ic, hf=hf: E.tensor_tensor(
                        t_sb[:, ic, hf * 512:(hf + 1) * 512], sil_sb[:],
                        gu_sb[:, 512:1024], A.mult), inc=sv)
            barrier()

            ctx.do("vector", lambda E: E.tensor_reduce(
                rm_t[:], t_sb[:].rearrange("p c t -> p t c"), AXX, A.max,
                apply_absolute_value=True), inc=sv)
            ctx.wait("gpsimd", sv)
            ctx.do("gpsimd", lambda E: E.partition_all_reduce(rm_t[:], rm_t[:], 128, RMax), inc=sg)
            ctx.wait("vector", sg)
            ctx.do("vector", lambda E: E.reciprocal(rut448[:], rm_t[:]), inc=sv)
            ctx.wait("vector", sv)
            ctx.do("vector", lambda E: E.tensor_scalar(rut448[:], rut448[:], 448.0, None, A.mult), inc=sv)
            ctx.do("vector", lambda E: E.tensor_scalar(sxtb[:], rm_t[:], 4.0 / 448.0, None, A.mult), inc=sv)
            for ic in range(IC):
                ctx.do("vector", lambda E, ic=ic: E._custom_dve(
                    QCAST, out=qt[:, ic, :], in0=t_sb[:, ic, :], in1=rut448[:],
                    imm2=0.5), inc=sv)
            barrier()

            for ic in range(IC):
                b = ic % 2
                ctx.wait("sync", sv)
                ctx.do("sync", lambda E, ic=ic, b=b: E.dma_start(
                    out=wd_st[b][:], in_=wdT[ic * 128:(ic + 1) * 128, :]), inc=sx, dma=True)
                ctx.wait("vector", sx)
                ctx.do("vector", lambda E, ic=ic, b=b: E.scalar_tensor_tensor(
                    rm_wd[:], wd_st[b][:], -1.0,
                    (wd_st[b][:] if ic == 0 else rm_wd[:]), A.mult, A.max), inc=sv)
                ctx.do("vector", lambda E, b=b: E.scalar_tensor_tensor(
                    rm_wd[:], wd_st[b][:], 1.0, rm_wd[:], A.mult, A.max), inc=sv)
            ctx.wait("gpsimd", sv)
            ctx.do("gpsimd", lambda E: E.partition_all_reduce(rm_wd[:], rm_wd[:], 128, RMax), inc=sg)
            ctx.wait("scalar", sg)
            for oc in range(DC):
                ctx.do("scalar", lambda E, oc=oc: E.dma_start(
                    out=wsd_pp[:, oc:oc + 1], in_=rm_wd[0:1, oc * 128:(oc + 1) * 128]),
                    inc=sst, dma=True)
            ctx.wait("vector", sst, sg)
            ctx.do("vector", lambda E: E.tensor_scalar(
                wsd_pp[:], wsd_pp[:], 1.0 / 448.0, None, A.mult), inc=sv)
            ctx.do("vector", lambda E: E.reciprocal(ruwd448[:], rm_wd[:]), inc=sv)
            ctx.wait("vector", sv)
            ctx.do("vector", lambda E: E.tensor_scalar(ruwd448[:], ruwd448[:], 448.0, None, A.mult), inc=sv)
            barrier()

            for grp in range(2):
                for ic in range(IC):
                    b = ic % 2
                    ctx.wait("sync", sv)
                    ctx.do("sync", lambda E, ic=ic, b=b: E.dma_start(
                        out=wd_st[b][:], in_=wdT[ic * 128:(ic + 1) * 128, :]),
                        inc=sx, dma=True)
                    ctx.wait("vector", sx)
                    ctx.do("vector", lambda E, grp=grp, ic=ic, b=b: E._custom_dve(
                        QCAST, out=qwd[:, ic, :],
                        in0=wd_st[b][:, grp * 1024:(grp + 1) * 1024],
                        in1=ruwd448[:, grp * 1024:(grp + 1) * 1024], imm2=0.5), inc=sv)
                for oci in range(8):
                    oc = grp * 8 + oci
                    ctx.wait("sync", sv, sst)
                    ctx.do("sync", lambda E, oc=oc: E.dma_start(
                        out=xo_sb[:], in_=xT[oc * 128:(oc + 1) * 128, 512:1536]),
                        inc=sx, dma=True)
                    ctx.do("sync", lambda E, oc=oc: E.dma_start(
                        out=wo_st[:].rearrange("p (c w) -> p c w", c=H),
                        in_=WoT.rearrange("(c p) f -> p c f", p=128)[:, :, oc * 128:(oc + 1) * 128]),
                        inc=sx, dma=True)
                    ctx.wait("vector", sx)
                    ctx.do("vector", lambda E: E.tensor_copy(wo_bf[:], wo_st[:]), inc=sv)
                    ctx.wait("tensor", sv)
                    for hf in range(2):
                        for ic in range(IC):
                            ctx.do("tensor", lambda E, oci=oci, hf=hf, ic=ic: E.matmul(
                                ps_d[hf][:], qwd[:, ic, oci * 128:(oci + 1) * 128],
                                qt[:, ic, hf * 512:(hf + 1) * 512],
                                start=(ic == 0), stop=(ic == IC - 1)), inc=st_)
                        for h in range(H):
                            ctx.do("tensor", lambda E, hf=hf, h=h: E.matmul(
                                ps_wo[hf][:], wo_bf[:, h * 128:(h + 1) * 128],
                                o_fm[:, h, hf * 512:(hf + 1) * 512],
                                start=(h == 0), stop=(h == H - 1)), inc=st_)
                    ctx.wait("vector", st_, sst)
                    for hf in range(2):
                        sl = slice(hf * 512, (hf + 1) * 512)
                        ctx.do("vector", lambda E, oc=oc, hf=hf, sl=sl: E.scalar_tensor_tensor(
                            out_sb[:, sl], ps_d[hf][:], wsd_pp[:, oc:oc + 1],
                            sxtb[:, sl], A.mult, A.mult), inc=sv)
                        ctx.do("vector", lambda E, hf=hf, sl=sl: E.scalar_tensor_tensor(
                            out_sb[:, sl], ps_wo[hf][:], 1.0, out_sb[:, sl],
                            A.mult, A.add), inc=sv)
                        ctx.do("vector", lambda E, hf=hf, sl=sl: E.tensor_tensor(
                            out_sb[:, sl], out_sb[:, sl], xo_sb[:, sl], A.add), inc=sv)
                    ctx.wait("scalar", sv)
                    ctx.do("scalar", lambda E, oc=oc: E.dma_start(
                        out=out_d[oc * 128:(oc + 1) * 128, :], in_=out_sb[:]),
                        inc=sst, dma=True)
            ctx.wait("scalar", sst)
            ctx.wait("sync", sst)

        @blk.sync
        def _(_e):
            walk(Ctx("sync", nc, sems))

        @blk.scalar
        def _(_e):
            walk(Ctx("scalar", nc, sems))

        @blk.vector
        def _(_e):
            walk(Ctx("vector", nc, sems))

        @blk.tensor
        def _(_e):
            walk(Ctx("tensor", nc, sems))

        @blk.gpsimd
        def _(_e):
            walk(Ctx("gpsimd", nc, sems))

        es5.close()
        es6.close()

    inst_map = {}
    for lib in all_libraries:
        for it in lib.instructions:
            inst_map[it] = inst_map.get(it, 0) | (1 << lib.index)
    _bass_rust.insert_library_loads(nc, inst_map, len(all_libraries), _lib_std.index)
    mybir.codegen_inst_isa_subclasses(nc)
    return nc


def _stage_inputs(inputs):
    f32 = np.float32
    x = np.ascontiguousarray(np.asarray(inputs["L0_attn_x"], dtype=f32)).reshape(B * N, D)
    Wq = np.asarray(inputs["L0_attn_Wq"], dtype=f32)
    Wkv = np.asarray(inputs["L0_attn_Wkv"], dtype=f32)
    Wg = np.asarray(inputs["L0_attn_Wg"], dtype=f32)
    bg = np.asarray(inputs["L0_attn_bg"], dtype=f32)
    Wo = np.asarray(inputs["L0_attn_Wo"], dtype=f32)
    w_gate = np.asarray(inputs["L0_ffn_w_gate"], dtype=f32)
    w_up = np.asarray(inputs["L0_ffn_w_up"], dtype=f32)
    w_down = np.asarray(inputs["L0_ffn_w_down"], dtype=f32)
    perm = np.asarray(inputs["L0_ffn_down_reorder_index"]).astype(np.int64)
    q_scale = np.asarray(inputs["L0_attn_q_scale"], dtype=f32)
    k_scale = np.asarray(inputs["L0_attn_k_scale"], dtype=f32)

    shared = {
        "WqT": np.ascontiguousarray(Wq.T),
        "WkT": np.ascontiguousarray(Wkv[:H * DH].T),
        "WvT": np.ascontiguousarray(Wkv[H * DH:].T),
        "WgT": np.ascontiguousarray(Wg.T),
        "WoT": np.ascontiguousarray(Wo.T),
        "wgP": np.ascontiguousarray(w_gate[perm].T),
        "wuP": np.ascontiguousarray(w_up[perm].T),
        "wdT": np.ascontiguousarray(w_down.T),
        "qs": np.ascontiguousarray(q_scale.reshape(128, 1)),
        "ks": np.ascontiguousarray(k_scale.reshape(128, 1)),
        "bgb": np.ascontiguousarray(np.tile(bg[None, :], (128, 1))),
        "ident": np.eye(128, dtype=f32),
    }
    in_maps = []
    for c in range(NCORES):
        r0 = c * T
        own = x[r0:r0 + T]
        halo = np.zeros((512, D), dtype=f32) if c % 4 == 0 else x[r0 - 512:r0]
        xTc = np.ascontiguousarray(np.concatenate([halo, own], axis=0).T)
        msk = np.zeros((128, 32), dtype=f32)
        p = np.arange(128, dtype=f32)
        for qb in range(2):
            for kc in range(8):
                j = kc * 128 + p
                start = j - 512.0
                end = j + 1.0
                if (c % 4 == 0) and qb == 0 and kc < 4:
                    start = np.full(128, 512.0, dtype=f32)
                    end = np.full(128, 513.0, dtype=f32)
                col = (qb * 8 + kc) * 2
                msk[:, col] = start
                msk[:, col + 1] = end
        m = dict(shared)
        m["xT"] = xTc
        m["mse"] = msk
        in_maps.append(m)
    return in_maps


def kernel(**inputs):
    in_maps = _stage_inputs(inputs)
    nc = build_nc()
    res = run_bass_kernel_spmd(nc, in_maps, core_ids=list(range(NCORES)))
    out = np.empty((B * N, D), dtype=np.float32)
    for c in range(NCORES):
        out[c * T:(c + 1) * T, :] = np.asarray(res.results[c]["out"]).T
    return out.reshape(B, N, D)


# version 03
# speedup vs baseline: 2.7866x; 1.2270x over previous
import numpy as np
import ml_dtypes
import concourse.bass as bass
import concourse.mybir as mybir
from concourse.bass_utils import run_bass_kernel_spmd
from concourse import dve_ops
from concourse.dve_op import DveOp
from concourse.dve_spec import Spec, Src0, Src1, C1, C2, lower
from concourse.dve_table_gen import dve_ver_for
import concourse.bass_isa as bass_isa
import bass_rust as _bass_rust
from concourse.library_config import all_libraries, standard as _lib_std
from contextlib import ExitStack
from concourse.bass import broadcast_tensor_aps

F32 = mybir.dt.float32
BF16 = mybir.dt.bfloat16
FP8 = mybir.dt.float8e4
A = mybir.AluOpType
ACT = mybir.ActivationFunctionType
AXX = mybir.AxisListType.X
RMax = bass_isa.ReduceOp.max

NCORES = 8
B, N, D = 2, 4096, 2048
H, DH = 8, 128
I = 5632
T = 1024
EXT = 1536
DC = D // 128
IC = I // 128
TC = T // 128
TCE = EXT // 128
VA = 132
LN_EPS = 1e-5
ENGINES = ("sync", "scalar", "vector", "tensor", "gpsimd")


def _qcast_ref(in0, in1, s0, s1, imm2):
    v = np.asarray(in0) * np.asarray(in1)
    q = v.astype(ml_dtypes.float8_e4m3fn).astype(v.dtype)
    return q * (np.asarray(s1) if s1 is not None else 1.0)


def _register_qcast():
    for op in dve_ops.OPS:
        if op.name == "QCAST":
            return op
    spec = Spec(body=(Src0 * Src1) * C1)
    ver = dve_ver_for("TRN2")
    sha = lower(spec, ver=ver, name="QCAST").sha(ver)
    op = DveOp("QCAST", spec, subdim=False, uops_sha={ver: sha}, reference=_qcast_ref)
    dve_ops.OPS.append(op)
    dve_ops.CUSTOM_DVE_SPECS["QCAST"] = spec
    dve_ops.CUSTOM_DVE_REFS["QCAST"] = _qcast_ref
    dve_ops._SUB_OPCODE_FOR_NAME["QCAST"] = dve_ops._CUSTOM_DVE_ROW_BASE + len(dve_ops.OPS) - 1
    return op


QCAST = _register_qcast()


class _Sem:
    __slots__ = ("h", "n")
    def __init__(self, h):
        self.h = h
        self.n = 0


SERIAL = False

class Ctx:
    def __init__(self, eng_name, nc, sems):
        self.eng = eng_name
        self.nc = nc
        self.sems = sems
        for s in sems.values():
            s.n = 0

    def _E(self):
        return getattr(self.nc, self.eng)

    def do(self, eng, fn, inc=None, dma=False):
        if self.eng == eng:
            E = getattr(self.nc, eng)
            if SERIAL:
                for s in self.sems.values():
                    if s.n > 0:
                        E.wait_ge(s.h, s.n)
            ins = fn(E)
            if inc is not None:
                ins.then_inc(inc.h, 16 if dma else 1)
        if inc is not None:
            inc.n += 16 if dma else 1

    def wait(self, eng, *sems):
        if self.eng == eng:
            for s in sems:
                getattr(self.nc, eng).wait_ge(s.h, s.n)

    def wait_val(self, eng, sem, val):
        if self.eng == eng and val > 0:
            getattr(self.nc, eng).wait_ge(sem.h, val)


def build_nc():
    nc = bass.Bass("TRN2", target_bir_lowering=False)
    din = lambda name, shape, dt=F32: nc.dram_tensor(name, shape, dt, kind="ExternalInput").ap()
    xT = din("xT", [D, EXT])
    WqT = din("WqT", [D, H * DH])
    WkT = din("WkT", [D, H * DH])
    WvT = din("WvT", [D, H * DH])
    WgT = din("WgT", [D, H])
    WoT = din("WoT", [H * DH, D])
    wgP = din("wgP", [D, I])
    wuP = din("wuP", [D, I])
    wdT = din("wdT", [I, D])
    qs_i = din("qs", [128, 1])
    ks_i = din("ks", [128, 1])
    bgb_i = din("bgb", [128, H])
    mse_i = din("mse", [128, 32])
    id_i = din("ident", [128, 128])
    out_d = nc.dram_tensor("out", [D, T], F32, kind="ExternalOutput").ap()

    es = ExitStack()
    with es:
        SB = lambda st, name, shape, dt=F32: st.enter_context(nc.sbuf_tensor(name, shape, dt))
        SBR = lambda st, name, shape, dt=F32: st.enter_context(nc.sbuf_tensor(name, shape, dt, side="right"))
        PS = lambda st, name, shape, dt=F32: st.enter_context(nc.psum_tensor(name, shape, dt))
        qs = SB(es, "qs_t", [128, 1]); ks = SB(es, "ks_t", [128, 1])
        bgb = SB(es, "bgb_t", [128, H]); mse = SB(es, "mse_t", [128, 32])
        ident = SB(es, "ident_b", [128, 128], BF16)
        ones = SB(es, "ones_t", [128, 1])
        gates = SB(es, "gates_t", [128, TC * H])
        sxb = SB(es, "sxb", [128, T])
        oscl = SB(es, "oscl", [128, 2])
        wsg_pp = SB(es, "wsg_pp", [128, IC]); wsu_pp = SB(es, "wsu_pp", [128, IC])
        wsd_pp = SB(es, "wsd_pp", [128, DC])
        o_fm = SB(es, "o_fm", [128, H, T], BF16)
        vfm = SB(es, "vfm", [128, EXT], BF16)

        sems = {k: _Sem(es.enter_context(nc.semaphore("sem_" + k)))
                for k in ("x", "v", "s", "t", "g", "st", "ph")}

        es1 = ExitStack()
        mb = SB(es1, "mb", [128, EXT]); rb = SB(es1, "rb", [128, EXT])
        ps_a1 = [PS(es1, f"ps_a1{i}", [1, 512]) for i in range(3)]
        ps_b1 = [PS(es1, f"ps_b1{i}", [1, 512]) for i in range(3)]
        esX = ExitStack()
        rows = SBR(esX, "rows", [1, 4 * EXT])
        xr = [SBR(esX, f"xr{i}", [128, EXT]) for i in range(3)]
        es2 = ExitStack()
        xn = SBR(es2, "xn", [128, DC, EXT], BF16)
        rux448 = SBR(es2, "rux448", [128, T])
        rm_x = SBR(es2, "rm_x", [128, T])
        es3 = ExitStack()
        q_h = SBR(es3, "q_h", [128, T], BF16)
        k_h = SBR(es3, "k_h", [128, EXT], BF16)
        v_augh = SBR(es3, "v_augh", [128, TCE, VA], BF16)
        wf_st = [SBR(es3, f"wf_st{i}", [128, DC * 128]) for i in range(2)]
        wf_bf = [SBR(es3, f"wf_bf{i}", [128, DC * 128], BF16) for i in range(2)]
        rnb = SBR(es3, "rnb", [128, EXT])
        p_sb = SBR(es3, "p_sb", [128, 8, 512], BF16)
        sc_sb = SBR(es3, "sc_sb", [128, 512])
        otm_sb = SBR(es3, "otm", [128, 128], BF16)
        es1.close()
        ps_pr = [PS(es3, f"ps_pr{i}", [128, 512]) for i in range(4)]
        ps_tr_full = PS(es3, "ps_tr", [128, 1024], BF16)
        ps_tr = ps_tr_full
        ps_ss = [PS(es3, f"ps_ss{i}", [1, 512]) for i in range(2)]
        ps_pv = PS(es3, "ps_pv", [128, 512])
        es2b = ExitStack()
        qx = SB(es2b, "qx", [128, DC, T], FP8)
        es3.close()
        es2.close()
        esX.close()
        es4 = ExitStack()
        t_sb = SBR(es4, "t_sb", [128, IC, T], BF16)
        es4a = ExitStack()
        wg_st = [SBR(es4a, f"wg_st{i}", [128, DC * 128]) for i in range(2)]
        wu_st = [SBR(es4a, f"wu_st{i}", [128, DC * 128]) for i in range(2)]
        qwg = [SBR(es4a, f"qwg{i}", [128, DC, 128], FP8) for i in range(2)]
        qwu = [SBR(es4a, f"qwu{i}", [128, DC, 128], FP8) for i in range(2)]
        gu_sb = SBR(es4a, "gu_sb", [128, 1024])
        sil_sb = [SBR(es4a, f"sil_sb{i}", [128, 512]) for i in range(2)]
        wab = SBR(es4a, "wab", [128, DC * 128])
        rug = [SBR(es4a, f"rug{i}", [128, 512]) for i in range(2)]
        ps_gu = [PS(es4a, f"ps_gu{i}", [128, 512]) for i in range(8)]
        es5 = ExitStack()
        es2b.close()
        es4a.close()
        qt = SB(es5, "qt", [128, IC, T], FP8)
        rm_t = SB(es5, "rm_t", [128, T])
        rut448 = SB(es5, "rut448", [128, T])
        sxtb = SB(es5, "sxtb", [128, T])
        es4.close()
        es6 = ExitStack()
        qwd = SBR(es6, "qwd", [128, IC, 8 * 128], FP8)
        wd_st = [SBR(es6, f"wd_st{i}", [128, D]) for i in range(2)]
        rm_wd = SBR(es6, "rm_wd", [128, D])
        ruwd448 = SBR(es6, "ruwd448", [128, D])
        wo_st = SBR(es6, "wo_st", [128, H * 128])
        wo_bf = SBR(es6, "wo_bf", [128, H * 128], BF16)
        xo_sb = SBR(es6, "xo_sb", [128, T])
        out_sb = SBR(es6, "out_sb", [128, T])
        ps_d = [PS(es6, f"ps_d{i}", [128, 512]) for i in range(2)]
        ps_wo = [PS(es6, f"ps_wo{i}", [128, 512]) for i in range(2)]

        blk = es.enter_context(nc.Block())

        def walk(ctx: Ctx):
            sx, sv, ssc, st_, sg, sst, sph = (ctx.sems[k] for k in
                                              ("x", "v", "s", "t", "g", "st", "ph"))
            phase_n = [0]

            def barrier():
                phase_n[0] += 1
                tgt = 5 * phase_n[0]
                for e in ENGINES:
                    ctx.do(e, lambda E: E.nop(), inc=sph)
                if ctx.eng is not None:
                    getattr(ctx.nc, ctx.eng).wait_ge(sph.h, tgt)

            m_row = rows[:, 0:EXT]
            e2_row = rows[:, EXT:2 * EXT]
            w_row = rows[:, 2 * EXT:3 * EXT]
            r_row = rows[:, 3 * EXT:4 * EXT]

            for src, dst in ((qs_i, qs), (ks_i, ks), (bgb_i, bgb), (mse_i, mse)):
                ctx.do("sync", lambda E, s=src, d=dst: E.dma_start(out=d[:], in_=s),
                       inc=sx, dma=True)
            ctx.do("sync", lambda E: E.dma_start(out=xr[0][:, 0:128], in_=id_i),
                   inc=sx, dma=True)
            ctx.do("gpsimd", lambda E: E.memset(ones[:], 1.0), inc=sg)
            ctx.do("gpsimd", lambda E: E.memset(v_augh[:], 0.0), inc=sg)
            ctx.do("gpsimd", lambda E: E.memset(v_augh[:, :, 128:129], 1.0), inc=sg)
            ctx.wait("vector", sx)
            ctx.do("vector", lambda E: E.tensor_copy(ident[:], xr[0][:, 0:128]), inc=sv)
            barrier()

            for c in range(DC):
                b = c % 2
                ctx.wait("sync", st_)
                ctx.do("sync", lambda E, c=c, b=b: E.dma_start(
                    out=xr[b][:], in_=xT[c * 128:(c + 1) * 128, :]), inc=sx, dma=True)
                ctx.wait("scalar", sx, st_)
                ctx.do("scalar", lambda E, b=b: E.activation(
                    xr[2][:], xr[b][:], ACT.Square), inc=ssc)
                ctx.wait("tensor", sx, ssc)
                for h3 in range(3):
                    ctx.do("tensor", lambda E, c=c, b=b, h3=h3: E.matmul(
                        ps_a1[h3][:], ones[:], xr[b][:, h3 * 512:(h3 + 1) * 512],
                        start=(c == 0), stop=(c == DC - 1)), inc=st_)
                    ctx.do("tensor", lambda E, c=c, h3=h3: E.matmul(
                        ps_b1[h3][:], ones[:], xr[2][:, h3 * 512:(h3 + 1) * 512],
                        start=(c == 0), stop=(c == DC - 1)), inc=st_)
            ctx.wait("vector", st_)
            for h3 in range(3):
                ctx.do("vector", lambda E, h3=h3: E.tensor_scalar(
                    m_row[:, h3 * 512:(h3 + 1) * 512], ps_a1[h3][:], 1.0 / D, None,
                    A.mult), inc=sv)
                ctx.do("vector", lambda E, h3=h3: E.tensor_scalar(
                    e2_row[:, h3 * 512:(h3 + 1) * 512], ps_b1[h3][:], 1.0 / D, None,
                    A.mult), inc=sv)
            ctx.do("vector", lambda E: E.scalar_tensor_tensor(
                w_row, m_row, -1.0, m_row, A.mult, A.mult), inc=sv)
            ctx.do("vector", lambda E: E.tensor_tensor(w_row, e2_row, w_row, A.add), inc=sv)
            ctx.do("vector", lambda E: E.tensor_scalar(w_row, w_row, LN_EPS, None, A.add), inc=sv)
            ctx.wait("scalar", sv)
            ctx.do("scalar", lambda E: E.activation(r_row, w_row, ACT.Sqrt), inc=ssc)
            ctx.wait("vector", ssc)
            ctx.do("vector", lambda E: E.reciprocal(e2_row, r_row), inc=sv)
            ctx.wait("vector", sv)
            ctx.do("vector", lambda E: E.tensor_tensor(r_row, e2_row, e2_row, A.mult), inc=sv)
            ctx.do("vector", lambda E: E.tensor_tensor(r_row, w_row, r_row, A.mult), inc=sv)
            ctx.do("vector", lambda E: E.tensor_scalar(r_row, r_row, -0.5, 1.5, A.mult, A.add), inc=sv)
            ctx.do("vector", lambda E: E.tensor_tensor(r_row, e2_row, r_row, A.mult), inc=sv)
            ctx.wait("gpsimd", sv)
            ctx.do("gpsimd", lambda E: E.partition_broadcast(mb[:], m_row, channels=128), inc=sg)
            ctx.do("gpsimd", lambda E: E.partition_broadcast(rb[:], r_row, channels=128), inc=sg)
            barrier()

            for c in range(DC):
                b = c % 2
                ctx.wait("sync", sv)
                ctx.do("sync", lambda E, c=c, b=b: E.dma_start(
                    out=xr[b][:], in_=xT[c * 128:(c + 1) * 128, :]), inc=sx, dma=True)
                ctx.wait("vector", sx, sg)
                ctx.do("vector", lambda E, b=b: E.scalar_tensor_tensor(
                    xr[2][:], mb[:], -1.0, xr[b][:], A.mult, A.add), inc=sv)
                ctx.do("vector", lambda E, c=c: E.tensor_tensor(
                    xn[:, c, :], xr[2][:], rb[:], A.mult), inc=sv)
            barrier()

            ctx.do("sync", lambda E: E.dma_start(
                out=wf_st[0][:].rearrange("p (c w) -> p c w", c=DC)[:, :, 0:H],
                in_=WgT.rearrange("(c p) f -> p c f", p=128)), inc=sx, dma=True)
            ctx.wait("vector", sx)
            ctx.do("vector", lambda E: E.tensor_copy(
                wf_bf[0][:].rearrange("p (c w) -> p c w", c=DC)[:, :, 0:H],
                wf_st[0][:].rearrange("p (c w) -> p c w", c=DC)[:, :, 0:H]), inc=sv)
            for tc in range(TC):
                ctx.wait("tensor", sv)
                for c in range(DC):
                    ctx.do("tensor", lambda E, tc=tc, c=c: E.matmul(
                        ps_pr[3][:, 0:H],
                        xn[:, c, 512 + tc * 128: 512 + (tc + 1) * 128],
                        wf_bf[0][:].rearrange("p (c w) -> p c w", c=DC)[:, c, 0:H],
                        start=(c == 0), stop=(c == DC - 1)), inc=st_)
                ctx.wait("vector", st_, ssc)
                ctx.do("vector", lambda E: E.tensor_tensor(
                    xr[2][:, 0:H], ps_pr[3][:, 0:H], bgb[:], A.add), inc=sv)
                ctx.wait("scalar", sv)
                ctx.do("scalar", lambda E, tc=tc: E.activation(
                    gates[:, tc * H:(tc + 1) * H], xr[2][:, 0:H], ACT.Sigmoid), inc=ssc)
            ctx.do("vector", lambda E: E.tensor_reduce(
                rm_x[:], xn[:, :, 512:1536].rearrange("p c t -> p t c"), AXX, A.max,
                apply_absolute_value=True), inc=sv)
            ctx.wait("gpsimd", sv)
            ctx.do("gpsimd", lambda E: E.partition_all_reduce(rm_x[:], rm_x[:], 128, RMax), inc=sg)
            ctx.wait("vector", sg)
            ctx.do("vector", lambda E: E.reciprocal(rux448[:], rm_x[:]), inc=sv)
            ctx.wait("vector", sv)
            ctx.do("vector", lambda E: E.tensor_scalar(rux448[:], rux448[:], 448.0, None, A.mult), inc=sv)
            ctx.do("vector", lambda E: E.tensor_scalar(sxb[:], rm_x[:], 4.0 / 448.0, None, A.mult), inc=sv)
            _qa, _qb = broadcast_tensor_aps(
                xn[:, :, 512:1536], rux448[:].rearrange("p t -> p () t"))
            ctx.do("vector", lambda E, a=_qa, b=_qb: E._custom_dve(
                QCAST, out=qx[:], in0=a, in1=b, s1=0.5), inc=sv)
            barrier()

            ssr = rows[:, 0:EXT]
            r0r = rows[:, EXT:2 * EXT]
            t1r = rows[:, 2 * EXT:3 * EXT]

            def l2norm_seq(psums, nhalf, scale8):
                ctx.wait("scalar", st_)
                for hf in range(nhalf):
                    ctx.do("scalar", lambda E, hf=hf: E.activation(
                        xr[2][:, hf * 512:(hf + 1) * 512], psums[hf][:], ACT.Square), inc=ssc)
                n = nhalf * 512
                for hf in range(nhalf):
                    ctx.wait("tensor", ssc, sv)
                    ctx.do("tensor", lambda E, hf=hf: E.matmul(
                        ps_ss[hf % 2][:], ones[:], xr[2][:, hf * 512:(hf + 1) * 512],
                        start=True, stop=True), inc=st_)
                    ctx.wait("vector", st_)
                    ctx.do("vector", lambda E, hf=hf: E.tensor_scalar(
                        ssr[:, hf * 512:(hf + 1) * 512], ps_ss[hf % 2][:], 1e-24, None,
                        A.max), inc=sv)
                ctx.wait("scalar", sv)
                ctx.do("scalar", lambda E, n=n: E.activation(
                    r0r[:, 0:n], ssr[:, 0:n], ACT.Sqrt), inc=ssc)
                ctx.wait("vector", ssc)
                ctx.do("vector", lambda E, n=n: E.reciprocal(t1r[:, 0:n], r0r[:, 0:n]), inc=sv)
                ctx.wait("vector", sv)
                ctx.do("vector", lambda E, n=n: E.tensor_tensor(
                    r0r[:, 0:n], t1r[:, 0:n], t1r[:, 0:n], A.mult), inc=sv)
                ctx.do("vector", lambda E, n=n: E.tensor_tensor(
                    r0r[:, 0:n], ssr[:, 0:n], r0r[:, 0:n], A.mult), inc=sv)
                ctx.do("vector", lambda E, n=n: E.tensor_scalar(
                    r0r[:, 0:n], r0r[:, 0:n], -0.5, 1.5, A.mult, A.add), inc=sv)
                ctx.do("vector", lambda E, n=n: E.tensor_tensor(
                    r0r[:, 0:n], t1r[:, 0:n], r0r[:, 0:n], A.mult), inc=sv)
                if scale8:
                    ctx.do("vector", lambda E, n=n: E.tensor_scalar(
                        r0r[:, 0:n], r0r[:, 0:n], 8.0, None, A.mult), inc=sv)
                ctx.wait("gpsimd", sv)
                ctx.do("gpsimd", lambda E, n=n: E.partition_broadcast(
                    rnb[:, 0:n], r0r[:, 0:n], channels=128), inc=sg)

            for h in range(H):
                b = h % 2
                ctx.wait("sync", sv)
                ctx.do("sync", lambda E, h=h, b=b: E.dma_start(
                    out=wf_st[b][:].rearrange("p (c w) -> p c w", c=DC),
                    in_=WqT.rearrange("(c p) f -> p c f", p=128)[:, :, h * 128:(h + 1) * 128]),
                    inc=sx, dma=True)
                ctx.wait("vector", sx)
                ctx.do("vector", lambda E, b=b: E.tensor_copy(wf_bf[b][:], wf_st[b][:]), inc=sv)
                ctx.wait("tensor", sv, ssc)
                for hf in range(2):
                    for c in range(DC):
                        ctx.do("tensor", lambda E, b=b, hf=hf, c=c: E.matmul(
                            ps_pr[2 + hf][:], wf_bf[b][:, c * 128:(c + 1) * 128],
                            xn[:, c, 512 + hf * 512: 512 + (hf + 1) * 512],
                            start=(c == 0), stop=(c == DC - 1)), inc=st_)
                ctx.wait("scalar", st_)
                l2norm_seq([ps_pr[2], ps_pr[3]], 2, scale8=True)
                ctx.wait("vector", sg)
                for hf in range(2):
                    ctx.do("vector", lambda E, hf=hf: E.scalar_tensor_tensor(
                        q_h[:, hf * 512:(hf + 1) * 512], ps_pr[2 + hf][:], qs[:],
                        rnb[:, hf * 512:(hf + 1) * 512], A.mult, A.mult), inc=sv)
                ctx.wait("sync", sv)
                ctx.do("sync", lambda E, h=h, b=b: E.dma_start(
                    out=wf_st[b][:].rearrange("p (c w) -> p c w", c=DC),
                    in_=WkT.rearrange("(c p) f -> p c f", p=128)[:, :, h * 128:(h + 1) * 128]),
                    inc=sx, dma=True)
                ctx.wait("vector", sx)
                ctx.do("vector", lambda E, b=b: E.tensor_copy(wf_bf[b][:], wf_st[b][:]), inc=sv)
                ctx.wait("tensor", sv, ssc)
                for hf in range(3):
                    for c in range(DC):
                        ctx.do("tensor", lambda E, b=b, hf=hf, c=c: E.matmul(
                            ps_pr[hf][:], wf_bf[b][:, c * 128:(c + 1) * 128],
                            xn[:, c, hf * 512:(hf + 1) * 512],
                            start=(c == 0), stop=(c == DC - 1)), inc=st_)
                ctx.wait("scalar", st_)
                l2norm_seq([ps_pr[0], ps_pr[1], ps_pr[2]], 3, scale8=False)
                ctx.wait("vector", sg)
                for hf in range(3):
                    ctx.do("vector", lambda E, hf=hf: E.scalar_tensor_tensor(
                        k_h[:, hf * 512:(hf + 1) * 512], ps_pr[hf][:], ks[:],
                        rnb[:, hf * 512:(hf + 1) * 512], A.mult, A.mult), inc=sv)
                ctx.wait("sync", sv)
                ctx.do("sync", lambda E, h=h, b=b: E.dma_start(
                    out=wf_st[b][:].rearrange("p (c w) -> p c w", c=DC),
                    in_=WvT.rearrange("(c p) f -> p c f", p=128)[:, :, h * 128:(h + 1) * 128]),
                    inc=sx, dma=True)
                ctx.wait("vector", sx)
                ctx.do("vector", lambda E, b=b: E.tensor_copy(wf_bf[b][:], wf_st[b][:]), inc=sv)
                ctx.wait("tensor", sv, ssc)
                for hf in range(3):
                    for c in range(DC):
                        ctx.do("tensor", lambda E, b=b, hf=hf, c=c: E.matmul(
                            ps_pr[hf][:], wf_bf[b][:, c * 128:(c + 1) * 128],
                            xn[:, c, hf * 512:(hf + 1) * 512],
                            start=(c == 0), stop=(c == DC - 1)), inc=st_)
                ctx.wait("vector", st_)
                for hf in range(3):
                    ctx.do("vector", lambda E, hf=hf: E.tensor_copy(
                        vfm[:, hf * 512:(hf + 1) * 512], ps_pr[hf][:]), inc=sv)
                for tc in range(TCE):
                    ctx.wait("tensor", sv)
                    ctx.do("tensor", lambda E, tc=tc: E.transpose(
                        ps_tr[:, 0:128], vfm[:, tc * 128:(tc + 1) * 128], ident[:]), inc=st_)
                    ctx.wait("vector", st_)
                    ctx.do("vector", lambda E, tc=tc: E.tensor_copy(
                        v_augh[:, tc, 0:128], ps_tr[:, 0:128]), inc=sv)
                for qb in range(2):
                    for kc in range(8):
                        ctx.wait("tensor", sv, ssc)
                        ctx.do("tensor", lambda E, h=h, qb=qb, kc=kc: E.matmul(
                            ps_pr[kc % 2][:],
                            k_h[:, qb * 512 + kc * 128: qb * 512 + (kc + 1) * 128],
                            q_h[:, qb * 512:(qb + 1) * 512],
                            start=True, stop=True), inc=st_)
                        ctx.wait("vector", st_)
                        col = (qb * 8 + kc) * 2
                        ctx.do("vector", lambda E, kc=kc, col=col: E.tensor_mask_reduce(
                            sc_sb[:], ps_pr[kc % 2][:],
                            mse[:, col:col + 1], mse[:, col + 1:col + 2],
                            1.0, 0.0, A.max), inc=sv)
                        ctx.wait("scalar", sv)
                        ctx.do("scalar", lambda E, kc=kc: E.activation(
                            p_sb[:, kc, :], sc_sb[:], ACT.Exp), inc=ssc)
                    for qc in range(4):
                        ctx.wait("tensor", ssc, sv)
                        for kc in range(8):
                            ctx.do("tensor", lambda E, qb=qb, qc=qc, kc=kc: E.matmul(
                                ps_pv[:, 0:129],
                                p_sb[:, kc, qc * 128:(qc + 1) * 128],
                                v_augh[:, qb * 4 + kc, 0:129],
                                start=(kc == 0), stop=(kc == 7)), inc=st_)
                        ctx.wait("vector", st_)
                        ctx.do("vector", lambda E: E.reciprocal(
                            oscl[:, 0:1], ps_pv[:, 128:129]), inc=sv)
                        ctx.wait("vector", sv)
                        tc = qb * 4 + qc
                        ctx.do("vector", lambda E, tc=tc, h=h: E.tensor_tensor(
                            oscl[:, 0:1], oscl[:, 0:1],
                            gates[:, tc * H + h: tc * H + h + 1], A.mult), inc=sv)
                        ctx.wait("scalar", sv)
                        ctx.do("scalar", lambda E: E.activation(
                            otm_sb[:], ps_pv[:, 0:128], ACT.Copy,
                            scale=oscl[:, 0:1]), inc=ssc)
                        ctx.wait("tensor", ssc)
                        ctx.do("tensor", lambda E: E.transpose(
                            ps_tr[:, 0:128], otm_sb[:], ident[:]), inc=st_)
                        ctx.wait("vector", st_)
                        ctx.do("vector", lambda E, h=h, qb=qb, qc=qc: E.tensor_copy(
                            o_fm[:, h, qb * 512 + qc * 128: qb * 512 + (qc + 1) * 128],
                            ps_tr[:, 0:128]), inc=sv)
            barrier()

            snap_qv = {}
            snap_qs = {}

            def ph5_dma(ic):
                b = ic % 2
                if ic - 2 in snap_qv:
                    ctx.wait_val("sync", sv, snap_qv[ic - 2])
                    ctx.wait_val("sync", ssc, snap_qs[ic - 2])
                ctx.do("sync", lambda E, ic=ic, b=b: E.dma_start(
                    out=wg_st[b][:].rearrange("p (c w) -> p c w", c=DC),
                    in_=wgP.rearrange("(c p) f -> p c f", p=128)[:, :, ic * 128:(ic + 1) * 128]),
                    inc=sx, dma=True)
                ctx.do("sync", lambda E, ic=ic, b=b: E.dma_start(
                    out=wu_st[b][:].rearrange("p (c w) -> p c w", c=DC),
                    in_=wuP.rearrange("(c p) f -> p c f", p=128)[:, :, ic * 128:(ic + 1) * 128]),
                    inc=sx, dma=True)

            def ph5_chain(ic):
                b = ic % 2
                for wi, wst, pp in ((0, wg_st, wsg_pp), (1, wu_st, wsu_pp)):
                    ctx.wait("scalar", sx, sv, sg)
                    ctx.do("scalar", lambda E, b=b, wst=wst: E.activation(
                        wab[:], wst[b][:], ACT.Abs), inc=ssc)
                    ctx.wait("vector", ssc)
                    w_ = DC * 128
                    for step in (w_ // 2, w_ // 4, w_ // 8, w_ // 16):
                        ctx.do("vector", lambda E, step=step: E.tensor_tensor(
                            wab[:, 0:step], wab[:, 0:step], wab[:, step:2 * step],
                            A.max), inc=sv)
                    ctx.wait("gpsimd", sv)
                    ctx.do("gpsimd", lambda E, b=b, wi=wi: E.partition_all_reduce(
                        rug[b][:, 256 + wi * 128:256 + (wi + 1) * 128], wab[:, 0:128],
                        128, RMax), inc=sg)
                snap_qs[ic] = ctx.sems["s"].n
                ctx.wait("scalar", sg)
                ctx.do("scalar", lambda E, ic=ic, b=b: E.dma_start(
                    out=wsg_pp[:, ic:ic + 1], in_=rug[b][0:1, 256:384]), inc=sst, dma=True)
                ctx.do("scalar", lambda E, ic=ic, b=b: E.dma_start(
                    out=wsu_pp[:, ic:ic + 1], in_=rug[b][0:1, 384:512]), inc=sst, dma=True)
                ctx.wait("vector", sg, sst)
                ctx.do("vector", lambda E, ic=ic: E.tensor_scalar(
                    wsg_pp[:, ic:ic + 1], wsg_pp[:, ic:ic + 1], 1.0 / 448.0, None,
                    A.mult), inc=sv)
                ctx.do("vector", lambda E, ic=ic: E.tensor_scalar(
                    wsu_pp[:, ic:ic + 1], wsu_pp[:, ic:ic + 1], 1.0 / 448.0, None,
                    A.mult), inc=sv)
                for wi in range(2):
                    ctx.do("vector", lambda E, b=b, wi=wi: E.reciprocal(
                        rug[b][:, wi * 128:(wi + 1) * 128],
                        rug[b][:, 256 + wi * 128:256 + (wi + 1) * 128]), inc=sv)
                    ctx.wait("vector", sv)
                    ctx.do("vector", lambda E, b=b, wi=wi: E.tensor_scalar(
                        rug[b][:, wi * 128:(wi + 1) * 128],
                        rug[b][:, wi * 128:(wi + 1) * 128], 448.0, None, A.mult), inc=sv)
                for wi, wst, qw in ((0, wg_st, qwg), (1, wu_st, qwu)):
                    _ia, _ib = broadcast_tensor_aps(
                        wst[b][:].rearrange("p (c w) -> p c w", c=DC),
                        rug[b][:, wi * 128:(wi + 1) * 128].rearrange("p c -> p () c"))
                    ctx.do("vector", lambda E, qw=qw, b=b, a=_ia, bb=_ib: E._custom_dve(
                        QCAST, out=qw[b][:], in0=a, in1=bb, s1=0.5), inc=sv)
                snap_qv[ic] = ctx.sems["v"].n

            ph5_dma(0)
            ph5_chain(0)
            for ic in range(IC):
                b = ic % 2
                if ic + 1 < IC:
                    ph5_dma(ic + 1)
                    ph5_chain(ic + 1)
                ctx.wait_val("tensor", sv, snap_qv[ic])
                for c in range(DC):
                    for wi, qw in ((0, qwg), (1, qwu)):
                        for hf in range(2):
                            ctx.do("tensor", lambda E, b=b, qw=qw, wi=wi, hf=hf, c=c: E.matmul(
                                ps_gu[b * 4 + wi * 2 + hf][:], qw[b][:, c, :],
                                qx[:, c, hf * 512:(hf + 1) * 512],
                                start=(c == 0), stop=(c == DC - 1)), inc=st_)
                ctx.wait("vector", st_)
                for hf in range(2):
                    ctx.wait("vector", ssc)
                    ctx.do("vector", lambda E, ic=ic, b=b, hf=hf: E.scalar_tensor_tensor(
                        gu_sb[:, 0:512], ps_gu[b * 4 + hf][:], wsg_pp[:, ic:ic + 1],
                        sxb[:, hf * 512:(hf + 1) * 512], A.mult, A.mult), inc=sv)
                    ctx.wait("scalar", sv)
                    ctx.do("scalar", lambda E, hf=hf: E.activation(
                        sil_sb[hf][:], gu_sb[:, 0:512], ACT.Silu), inc=ssc)
                    ctx.do("vector", lambda E, ic=ic, b=b, hf=hf: E.scalar_tensor_tensor(
                        gu_sb[:, 512:1024], ps_gu[b * 4 + 2 + hf][:], wsu_pp[:, ic:ic + 1],
                        sxb[:, hf * 512:(hf + 1) * 512], A.mult, A.mult), inc=sv)
                    ctx.wait("vector", ssc)
                    ctx.do("vector", lambda E, ic=ic, hf=hf: E.tensor_tensor(
                        t_sb[:, ic, hf * 512:(hf + 1) * 512], sil_sb[hf][:],
                        gu_sb[:, 512:1024], A.mult), inc=sv)
            barrier()

            ctx.do("vector", lambda E: E.tensor_reduce(
                rm_t[:], t_sb[:].rearrange("p c t -> p t c"), AXX, A.max,
                apply_absolute_value=True), inc=sv)
            ctx.wait("gpsimd", sv)
            ctx.do("gpsimd", lambda E: E.partition_all_reduce(rm_t[:], rm_t[:], 128, RMax), inc=sg)
            ctx.wait("vector", sg)
            ctx.do("vector", lambda E: E.reciprocal(rut448[:], rm_t[:]), inc=sv)
            ctx.wait("vector", sv)
            ctx.do("vector", lambda E: E.tensor_scalar(rut448[:], rut448[:], 448.0, None, A.mult), inc=sv)
            ctx.do("vector", lambda E: E.tensor_scalar(sxtb[:], rm_t[:], 4.0 / 448.0, None, A.mult), inc=sv)
            _ta, _tb = broadcast_tensor_aps(
                t_sb[:], rut448[:].rearrange("p t -> p () t"))
            ctx.do("vector", lambda E, a=_ta, b=_tb: E._custom_dve(
                QCAST, out=qt[:], in0=a, in1=b, s1=0.5), inc=sv)
            barrier()

            for ic in range(IC):
                b = ic % 2
                ctx.wait("sync", sv)
                ctx.do("sync", lambda E, ic=ic, b=b: E.dma_start(
                    out=wd_st[b][:], in_=wdT[ic * 128:(ic + 1) * 128, :]), inc=sx, dma=True)
                ctx.wait("vector", sx)
                ctx.do("vector", lambda E, ic=ic, b=b: E.scalar_tensor_tensor(
                    rm_wd[:], wd_st[b][:], -1.0,
                    (wd_st[b][:] if ic == 0 else rm_wd[:]), A.mult, A.max), inc=sv)
                ctx.do("vector", lambda E, b=b: E.scalar_tensor_tensor(
                    rm_wd[:], wd_st[b][:], 1.0, rm_wd[:], A.mult, A.max), inc=sv)
            ctx.wait("gpsimd", sv)
            ctx.do("gpsimd", lambda E: E.partition_all_reduce(rm_wd[:], rm_wd[:], 128, RMax), inc=sg)
            ctx.wait("scalar", sg)
            for oc in range(DC):
                ctx.do("scalar", lambda E, oc=oc: E.dma_start(
                    out=wsd_pp[:, oc:oc + 1], in_=rm_wd[0:1, oc * 128:(oc + 1) * 128]),
                    inc=sst, dma=True)
            ctx.wait("vector", sst, sg)
            ctx.do("vector", lambda E: E.tensor_scalar(
                wsd_pp[:], wsd_pp[:], 1.0 / 448.0, None, A.mult), inc=sv)
            ctx.do("vector", lambda E: E.reciprocal(ruwd448[:], rm_wd[:]), inc=sv)
            ctx.wait("vector", sv)
            ctx.do("vector", lambda E: E.tensor_scalar(ruwd448[:], ruwd448[:], 448.0, None, A.mult), inc=sv)
            barrier()

            for grp in range(2):
                for ic in range(IC):
                    b = ic % 2
                    ctx.wait("sync", sv)
                    ctx.do("sync", lambda E, ic=ic, b=b: E.dma_start(
                        out=wd_st[b][:], in_=wdT[ic * 128:(ic + 1) * 128, :]),
                        inc=sx, dma=True)
                    ctx.wait("vector", sx)
                    ctx.do("vector", lambda E, grp=grp, ic=ic, b=b: E._custom_dve(
                        QCAST, out=qwd[:, ic, :],
                        in0=wd_st[b][:, grp * 1024:(grp + 1) * 1024],
                        in1=ruwd448[:, grp * 1024:(grp + 1) * 1024], s1=0.5), inc=sv)
                for oci in range(8):
                    oc = grp * 8 + oci
                    ctx.wait("sync", sv, sst)
                    ctx.do("sync", lambda E, oc=oc: E.dma_start(
                        out=xo_sb[:], in_=xT[oc * 128:(oc + 1) * 128, 512:1536]),
                        inc=sx, dma=True)
                    ctx.do("sync", lambda E, oc=oc: E.dma_start(
                        out=wo_st[:].rearrange("p (c w) -> p c w", c=H),
                        in_=WoT.rearrange("(c p) f -> p c f", p=128)[:, :, oc * 128:(oc + 1) * 128]),
                        inc=sx, dma=True)
                    ctx.wait("vector", sx)
                    ctx.do("vector", lambda E: E.tensor_copy(wo_bf[:], wo_st[:]), inc=sv)
                    ctx.wait("tensor", sv)
                    for hf in range(2):
                        for ic in range(IC):
                            ctx.do("tensor", lambda E, oci=oci, hf=hf, ic=ic: E.matmul(
                                ps_d[hf][:], qwd[:, ic, oci * 128:(oci + 1) * 128],
                                qt[:, ic, hf * 512:(hf + 1) * 512],
                                start=(ic == 0), stop=(ic == IC - 1)), inc=st_)
                        for h in range(H):
                            ctx.do("tensor", lambda E, hf=hf, h=h: E.matmul(
                                ps_wo[hf][:], wo_bf[:, h * 128:(h + 1) * 128],
                                o_fm[:, h, hf * 512:(hf + 1) * 512],
                                start=(h == 0), stop=(h == H - 1)), inc=st_)
                    ctx.wait("vector", st_, sst)
                    for hf in range(2):
                        sl = slice(hf * 512, (hf + 1) * 512)
                        ctx.do("vector", lambda E, oc=oc, hf=hf, sl=sl: E.scalar_tensor_tensor(
                            out_sb[:, sl], ps_d[hf][:], wsd_pp[:, oc:oc + 1],
                            sxtb[:, sl], A.mult, A.mult), inc=sv)
                        ctx.do("vector", lambda E, hf=hf, sl=sl: E.scalar_tensor_tensor(
                            out_sb[:, sl], ps_wo[hf][:], 1.0, out_sb[:, sl],
                            A.mult, A.add), inc=sv)
                        ctx.do("vector", lambda E, hf=hf, sl=sl: E.tensor_tensor(
                            out_sb[:, sl], out_sb[:, sl], xo_sb[:, sl], A.add), inc=sv)
                    ctx.wait("scalar", sv)
                    ctx.do("scalar", lambda E, oc=oc: E.dma_start(
                        out=out_d[oc * 128:(oc + 1) * 128, :], in_=out_sb[:]),
                        inc=sst, dma=True)
            ctx.wait("scalar", sst)
            ctx.wait("sync", sst)

        @blk.sync
        def _(_e):
            walk(Ctx("sync", nc, sems))

        @blk.scalar
        def _(_e):
            walk(Ctx("scalar", nc, sems))

        @blk.vector
        def _(_e):
            walk(Ctx("vector", nc, sems))

        @blk.tensor
        def _(_e):
            walk(Ctx("tensor", nc, sems))

        @blk.gpsimd
        def _(_e):
            walk(Ctx("gpsimd", nc, sems))

        es5.close()
        es6.close()

    inst_map = {}
    for lib in all_libraries:
        for it in lib.instructions:
            inst_map[it] = inst_map.get(it, 0) | (1 << lib.index)
    _bass_rust.insert_library_loads(nc, inst_map, len(all_libraries), _lib_std.index)
    mybir.codegen_inst_isa_subclasses(nc)
    return nc


def _stage_inputs(inputs):
    f32 = np.float32
    x = np.ascontiguousarray(np.asarray(inputs["L0_attn_x"], dtype=f32)).reshape(B * N, D)
    Wq = np.asarray(inputs["L0_attn_Wq"], dtype=f32)
    Wkv = np.asarray(inputs["L0_attn_Wkv"], dtype=f32)
    Wg = np.asarray(inputs["L0_attn_Wg"], dtype=f32)
    bg = np.asarray(inputs["L0_attn_bg"], dtype=f32)
    Wo = np.asarray(inputs["L0_attn_Wo"], dtype=f32)
    w_gate = np.asarray(inputs["L0_ffn_w_gate"], dtype=f32)
    w_up = np.asarray(inputs["L0_ffn_w_up"], dtype=f32)
    w_down = np.asarray(inputs["L0_ffn_w_down"], dtype=f32)
    perm = np.asarray(inputs["L0_ffn_down_reorder_index"]).astype(np.int64)
    q_scale = np.asarray(inputs["L0_attn_q_scale"], dtype=f32)
    k_scale = np.asarray(inputs["L0_attn_k_scale"], dtype=f32)

    shared = {
        "WqT": np.ascontiguousarray(Wq.T),
        "WkT": np.ascontiguousarray(Wkv[:H * DH].T),
        "WvT": np.ascontiguousarray(Wkv[H * DH:].T),
        "WgT": np.ascontiguousarray(Wg.T),
        "WoT": np.ascontiguousarray(Wo.T),
        "wgP": np.ascontiguousarray(w_gate[perm].T),
        "wuP": np.ascontiguousarray(w_up[perm].T),
        "wdT": np.ascontiguousarray(w_down.T),
        "qs": np.ascontiguousarray(q_scale.reshape(128, 1)),
        "ks": np.ascontiguousarray(k_scale.reshape(128, 1)),
        "bgb": np.ascontiguousarray(np.tile(bg[None, :], (128, 1))),
        "ident": np.eye(128, dtype=f32),
    }
    in_maps = []
    for c in range(NCORES):
        r0 = c * T
        own = x[r0:r0 + T]
        halo = np.zeros((512, D), dtype=f32) if c % 4 == 0 else x[r0 - 512:r0]
        xTc = np.ascontiguousarray(np.concatenate([halo, own], axis=0).T)
        msk = np.zeros((128, 32), dtype=f32)
        p = np.arange(128, dtype=f32)
        for qb in range(2):
            for kc in range(8):
                j = kc * 128 + p
                start = j - 512.0
                end = j + 1.0
                if (c % 4 == 0) and qb == 0 and kc < 4:
                    start = np.full(128, 512.0, dtype=f32)
                    end = np.full(128, 513.0, dtype=f32)
                col = (qb * 8 + kc) * 2
                msk[:, col] = start
                msk[:, col + 1] = end
        m = dict(shared)
        m["xT"] = xTc
        m["mse"] = msk
        in_maps.append(m)
    return in_maps


def kernel(**inputs):
    in_maps = _stage_inputs(inputs)
    nc = build_nc()
    res = run_bass_kernel_spmd(nc, in_maps, core_ids=list(range(NCORES)))
    out = np.empty((B * N, D), dtype=np.float32)
    for c in range(NCORES):
        out[c * T:(c + 1) * T, :] = np.asarray(res.results[c]["out"]).T
    return out.reshape(B, N, D)


# version 04
# speedup vs baseline: 3.0989x; 1.1120x over previous
import numpy as np
import ml_dtypes
import concourse.bass as bass
import concourse.mybir as mybir
from concourse.bass_utils import run_bass_kernel_spmd
from concourse import dve_ops
from concourse.dve_op import DveOp
from concourse.dve_spec import Spec, Src0, Src1, C1, C2, lower
from concourse.dve_table_gen import dve_ver_for
import concourse.bass_isa as bass_isa
import bass_rust as _bass_rust
from concourse.library_config import all_libraries, standard as _lib_std
from contextlib import ExitStack
from concourse.bass import broadcast_tensor_aps

F32 = mybir.dt.float32
BF16 = mybir.dt.bfloat16
FP8 = mybir.dt.float8e4
A = mybir.AluOpType
ACT = mybir.ActivationFunctionType
AXX = mybir.AxisListType.X
RMax = bass_isa.ReduceOp.max

NCORES = 8
B, N, D = 2, 4096, 2048
H, DH = 8, 128
I = 5632
T = 1024
EXT = 1536
DC = D // 128
IC = I // 128
TC = T // 128
TCE = EXT // 128
VA = 132
LN_EPS = 1e-5
ENGINES = ("sync", "scalar", "vector", "tensor", "gpsimd")


def _qcast_ref(in0, in1, s0, s1, imm2):
    v = np.asarray(in0) * np.asarray(in1)
    q = v.astype(ml_dtypes.float8_e4m3fn).astype(v.dtype)
    return q * (np.asarray(s1) if s1 is not None else 1.0)


def _register_qcast():
    for op in dve_ops.OPS:
        if op.name == "QCAST":
            return op
    spec = Spec(body=(Src0 * Src1) * C1)
    ver = dve_ver_for("TRN2")
    sha = lower(spec, ver=ver, name="QCAST").sha(ver)
    op = DveOp("QCAST", spec, subdim=False, uops_sha={ver: sha}, reference=_qcast_ref)
    dve_ops.OPS.append(op)
    dve_ops.CUSTOM_DVE_SPECS["QCAST"] = spec
    dve_ops.CUSTOM_DVE_REFS["QCAST"] = _qcast_ref
    dve_ops._SUB_OPCODE_FOR_NAME["QCAST"] = dve_ops._CUSTOM_DVE_ROW_BASE + len(dve_ops.OPS) - 1
    return op


QCAST = _register_qcast()


class _Sem:
    __slots__ = ("h", "n")
    def __init__(self, h):
        self.h = h
        self.n = 0


SERIAL = False

class Ctx:
    def __init__(self, eng_name, nc, sems):
        self.eng = eng_name
        self.nc = nc
        self.sems = sems
        for s in sems.values():
            s.n = 0

    def _E(self):
        return getattr(self.nc, self.eng)

    def do(self, eng, fn, inc=None, dma=False):
        if self.eng == eng:
            E = getattr(self.nc, eng)
            if SERIAL:
                for s in self.sems.values():
                    if s.n > 0:
                        E.wait_ge(s.h, s.n)
            ins = fn(E)
            if inc is not None:
                ins.then_inc(inc.h, 16 if dma else 1)
        if inc is not None:
            inc.n += 16 if dma else 1

    def wait(self, eng, *sems):
        if self.eng == eng:
            for s in sems:
                getattr(self.nc, eng).wait_ge(s.h, s.n)

    def wait_val(self, eng, sem, val):
        if self.eng == eng and val > 0:
            getattr(self.nc, eng).wait_ge(sem.h, val)


def build_nc():
    nc = bass.Bass("TRN2", target_bir_lowering=False)
    din = lambda name, shape, dt=F32: nc.dram_tensor(name, shape, dt, kind="ExternalInput").ap()
    xT = din("xT", [D, EXT])
    WqT = din("WqT", [D, H * DH])
    WkT = din("WkT", [D, H * DH])
    WvT = din("WvT", [D, H * DH])
    WgT = din("WgT", [D, H])
    WoT = din("WoT", [H * DH, D])
    wgP = din("wgP", [D, I])
    wuP = din("wuP", [D, I])
    wdT = din("wdT", [I, D])
    qs_i = din("qs", [128, 1])
    ks_i = din("ks", [128, 1])
    bgb_i = din("bgb", [128, H])
    mse_i = din("mse", [128, 32])
    id_i = din("ident", [128, 128])
    out_d = nc.dram_tensor("out", [D, T], F32, kind="ExternalOutput").ap()

    es = ExitStack()
    with es:
        SB = lambda st, name, shape, dt=F32: st.enter_context(nc.sbuf_tensor(name, shape, dt))
        SBR = lambda st, name, shape, dt=F32: st.enter_context(nc.sbuf_tensor(name, shape, dt, side="right"))
        PS = lambda st, name, shape, dt=F32: st.enter_context(nc.psum_tensor(name, shape, dt))
        qs = SB(es, "qs_t", [128, 1]); ks = SB(es, "ks_t", [128, 1])
        qs8 = SB(es, "qs8_t", [128, 1])
        bgb = SB(es, "bgb_t", [128, H]); mse = SB(es, "mse_t", [128, 32])
        ident = SB(es, "ident_b", [128, 128], BF16)
        ones = SB(es, "ones_t", [128, 1])
        gates = SB(es, "gates_t", [128, TC * H])
        sxb = SB(es, "sxb", [128, T])
        oscl = SB(es, "oscl", [128, 2])
        wsg_pp = SB(es, "wsg_pp", [128, IC]); wsu_pp = SB(es, "wsu_pp", [128, IC])
        wsd_pp = SB(es, "wsd_pp", [128, DC])
        o_fm = SB(es, "o_fm", [128, H, T], BF16)
        vfm = SB(es, "vfm", [128, EXT], BF16)

        sems = {k: _Sem(es.enter_context(nc.semaphore("sem_" + k)))
                for k in ("x", "v", "s", "t", "g", "st", "ph")}

        es1 = ExitStack()
        mb = SB(es1, "mb", [128, EXT]); rb = SB(es1, "rb", [128, EXT])
        ps_a1 = [PS(es1, f"ps_a1{i}", [1, 512]) for i in range(3)]
        ps_b1 = [PS(es1, f"ps_b1{i}", [1, 512]) for i in range(3)]
        esX = ExitStack()
        rows = SBR(esX, "rows", [1, 4 * EXT])
        xr = [SBR(esX, f"xr{i}", [128, EXT]) for i in range(3)]
        es2 = ExitStack()
        xn = SBR(es2, "xn", [128, DC, EXT], BF16)
        rux448 = SBR(es2, "rux448", [128, T])
        rm_x = SBR(es2, "rm_x", [128, T])
        es3 = ExitStack()
        q_h = SBR(es3, "q_h", [128, T], BF16)
        k_h = SBR(es3, "k_h", [128, EXT], BF16)
        v_augh = SBR(es3, "v_augh", [128, TCE, VA], BF16)
        wf_st = [SBR(es3, f"wf_st{i}", [128, DC * 128]) for i in range(2)]
        wf_bf = [SBR(es3, f"wf_bf{i}", [128, DC * 128], BF16) for i in range(2)]
        rnb = SBR(es3, "rnb", [128, EXT])
        p_sb = SBR(es3, "p_sb", [128, 8, 512], BF16)
        sc_sb = SBR(es3, "sc_sb", [128, 512])
        otm_sb = [SBR(es3, f"otm{i}", [128, 128], BF16) for i in range(2)]
        es1.close()
        ps_pr = [PS(es3, f"ps_pr{i}", [128, 512]) for i in range(5)]
        ps_tr = PS(es3, "ps_tr", [128, 1024], BF16)
        ps_pv = [PS(es3, f"ps_pv{i}", [128, 512]) for i in range(2)]
        es2b = ExitStack()
        qx = SB(es2b, "qx", [128, DC, T], FP8)
        es3.close()
        es2.close()
        esX.close()
        es4 = ExitStack()
        t_sb = SBR(es4, "t_sb", [128, IC, T], BF16)
        es4a = ExitStack()
        wg_st = [SBR(es4a, f"wg_st{i}", [128, DC * 128]) for i in range(2)]
        wu_st = [SBR(es4a, f"wu_st{i}", [128, DC * 128]) for i in range(2)]
        qwg = [SBR(es4a, f"qwg{i}", [128, DC, 128], FP8) for i in range(2)]
        qwu = [SBR(es4a, f"qwu{i}", [128, DC, 128], FP8) for i in range(2)]
        gu_sb = SBR(es4a, "gu_sb", [128, 1024])
        sil_sb = [SBR(es4a, f"sil_sb{i}", [128, 512]) for i in range(2)]
        wab = SBR(es4a, "wab", [128, DC * 128])
        rug = [SBR(es4a, f"rug{i}", [128, 512]) for i in range(2)]
        ps_gu = [PS(es4a, f"ps_gu{i}", [128, 512]) for i in range(8)]
        es5 = ExitStack()
        es2b.close()
        es4a.close()
        qt = SB(es5, "qt", [128, IC, T], FP8)
        rm_t = SB(es5, "rm_t", [128, T])
        rut448 = SB(es5, "rut448", [128, T])
        sxtb = SB(es5, "sxtb", [128, T])
        es4.close()
        es6 = ExitStack()
        qwd = SBR(es6, "qwd", [128, IC, 8 * 128], FP8)
        wd_st = [SBR(es6, f"wd_st{i}", [128, D]) for i in range(2)]
        rm_wd = SBR(es6, "rm_wd", [128, D])
        ruwd448 = SBR(es6, "ruwd448", [128, D])
        wo_st = SBR(es6, "wo_st", [128, H * 128])
        wo_bf = SBR(es6, "wo_bf", [128, H * 128], BF16)
        xo_sb = SBR(es6, "xo_sb", [128, T])
        out_sb = SBR(es6, "out_sb", [128, T])
        ps_d = [PS(es6, f"ps_d{i}", [128, 512]) for i in range(2)]
        ps_wo = [PS(es6, f"ps_wo{i}", [128, 512]) for i in range(2)]

        blk = es.enter_context(nc.Block())

        def walk(ctx: Ctx):
            sx, sv, ssc, st_, sg, sst, sph = (ctx.sems[k] for k in
                                              ("x", "v", "s", "t", "g", "st", "ph"))
            phase_n = [0]

            def barrier():
                phase_n[0] += 1
                tgt = 5 * phase_n[0]
                for e in ENGINES:
                    ctx.do(e, lambda E: E.nop(), inc=sph)
                if ctx.eng is not None:
                    getattr(ctx.nc, ctx.eng).wait_ge(sph.h, tgt)

            m_row = rows[:, 0:EXT]
            e2_row = rows[:, EXT:2 * EXT]
            w_row = rows[:, 2 * EXT:3 * EXT]
            r_row = rows[:, 3 * EXT:4 * EXT]

            for src, dst in ((qs_i, qs), (ks_i, ks), (bgb_i, bgb), (mse_i, mse)):
                ctx.do("sync", lambda E, s=src, d=dst: E.dma_start(out=d[:], in_=s),
                       inc=sx, dma=True)
            ctx.do("sync", lambda E: E.dma_start(out=xr[0][:, 0:128], in_=id_i),
                   inc=sx, dma=True)
            ctx.do("gpsimd", lambda E: E.memset(ones[:], 1.0), inc=sg)
            ctx.do("gpsimd", lambda E: E.memset(v_augh[:], 0.0), inc=sg)
            ctx.do("gpsimd", lambda E: E.memset(v_augh[:, :, 128:129], 1.0), inc=sg)
            ctx.wait("vector", sx)
            ctx.do("vector", lambda E: E.tensor_copy(ident[:], xr[0][:, 0:128]), inc=sv)
            ctx.do("vector", lambda E: E.tensor_scalar(qs8[:], qs[:], 8.0, None, A.mult), inc=sv)
            barrier()

            for c in range(DC):
                b = c % 2
                ctx.wait("sync", st_)
                ctx.do("sync", lambda E, c=c, b=b: E.dma_start(
                    out=xr[b][:], in_=xT[c * 128:(c + 1) * 128, :]), inc=sx, dma=True)
                ctx.wait("scalar", sx, st_)
                ctx.do("scalar", lambda E, b=b: E.activation(
                    xr[2][:], xr[b][:], ACT.Square), inc=ssc)
                ctx.wait("tensor", sx, ssc)
                for h3 in range(3):
                    ctx.do("tensor", lambda E, c=c, b=b, h3=h3: E.matmul(
                        ps_a1[h3][:], ones[:], xr[b][:, h3 * 512:(h3 + 1) * 512],
                        start=(c == 0), stop=(c == DC - 1)), inc=st_)
                    ctx.do("tensor", lambda E, c=c, h3=h3: E.matmul(
                        ps_b1[h3][:], ones[:], xr[2][:, h3 * 512:(h3 + 1) * 512],
                        start=(c == 0), stop=(c == DC - 1)), inc=st_)
            ctx.wait("vector", st_)
            for h3 in range(3):
                ctx.do("vector", lambda E, h3=h3: E.tensor_scalar(
                    m_row[:, h3 * 512:(h3 + 1) * 512], ps_a1[h3][:], 1.0 / D, None,
                    A.mult), inc=sv)
                ctx.do("vector", lambda E, h3=h3: E.tensor_scalar(
                    e2_row[:, h3 * 512:(h3 + 1) * 512], ps_b1[h3][:], 1.0 / D, None,
                    A.mult), inc=sv)
            ctx.do("vector", lambda E: E.scalar_tensor_tensor(
                w_row, m_row, -1.0, m_row, A.mult, A.mult), inc=sv)
            ctx.do("vector", lambda E: E.tensor_tensor(w_row, e2_row, w_row, A.add), inc=sv)
            ctx.do("vector", lambda E: E.tensor_scalar(w_row, w_row, LN_EPS, None, A.add), inc=sv)
            ctx.wait("scalar", sv)
            ctx.do("scalar", lambda E: E.activation(r_row, w_row, ACT.Sqrt), inc=ssc)
            ctx.wait("vector", ssc)
            ctx.do("vector", lambda E: E.reciprocal(e2_row, r_row), inc=sv)
            ctx.wait("vector", sv)
            ctx.do("vector", lambda E: E.tensor_tensor(r_row, e2_row, e2_row, A.mult), inc=sv)
            ctx.do("vector", lambda E: E.tensor_tensor(r_row, w_row, r_row, A.mult), inc=sv)
            ctx.do("vector", lambda E: E.tensor_scalar(r_row, r_row, -0.5, 1.5, A.mult, A.add), inc=sv)
            ctx.do("vector", lambda E: E.tensor_tensor(r_row, e2_row, r_row, A.mult), inc=sv)
            ctx.wait("gpsimd", sv)
            ctx.do("gpsimd", lambda E: E.partition_broadcast(mb[:], m_row, channels=128), inc=sg)
            ctx.do("gpsimd", lambda E: E.partition_broadcast(rb[:], r_row, channels=128), inc=sg)
            barrier()

            for c in range(DC):
                b = c % 2
                ctx.wait("sync", sv)
                ctx.do("sync", lambda E, c=c, b=b: E.dma_start(
                    out=xr[b][:], in_=xT[c * 128:(c + 1) * 128, :]), inc=sx, dma=True)
                ctx.wait("vector", sx, sg)
                ctx.do("vector", lambda E, b=b: E.scalar_tensor_tensor(
                    xr[2][:], mb[:], -1.0, xr[b][:], A.mult, A.add), inc=sv)
                ctx.do("vector", lambda E, c=c: E.tensor_tensor(
                    xn[:, c, :], xr[2][:], rb[:], A.mult), inc=sv)
            barrier()

            ctx.do("sync", lambda E: E.dma_start(
                out=wf_st[0][:].rearrange("p (c w) -> p c w", c=DC)[:, :, 0:H],
                in_=WgT.rearrange("(c p) f -> p c f", p=128)), inc=sx, dma=True)
            ctx.wait("vector", sx)
            ctx.do("vector", lambda E: E.tensor_copy(
                wf_bf[0][:].rearrange("p (c w) -> p c w", c=DC)[:, :, 0:H],
                wf_st[0][:].rearrange("p (c w) -> p c w", c=DC)[:, :, 0:H]), inc=sv)
            for tc in range(TC):
                ctx.wait("tensor", sv)
                for c in range(DC):
                    ctx.do("tensor", lambda E, tc=tc, c=c: E.matmul(
                        ps_pr[3][:, 0:H],
                        xn[:, c, 512 + tc * 128: 512 + (tc + 1) * 128],
                        wf_bf[0][:].rearrange("p (c w) -> p c w", c=DC)[:, c, 0:H],
                        start=(c == 0), stop=(c == DC - 1)), inc=st_)
                ctx.wait("vector", st_, ssc)
                ctx.do("vector", lambda E: E.tensor_tensor(
                    xr[2][:, 0:H], ps_pr[3][:, 0:H], bgb[:], A.add), inc=sv)
                ctx.wait("scalar", sv)
                ctx.do("scalar", lambda E, tc=tc: E.activation(
                    gates[:, tc * H:(tc + 1) * H], xr[2][:, 0:H], ACT.Sigmoid), inc=ssc)
            ctx.do("vector", lambda E: E.tensor_reduce(
                rm_x[:], xn[:, :, 512:1536].rearrange("p c t -> p t c"), AXX, A.max,
                apply_absolute_value=True), inc=sv)
            ctx.wait("gpsimd", sv)
            ctx.do("gpsimd", lambda E: E.partition_all_reduce(rm_x[:], rm_x[:], 128, RMax), inc=sg)
            ctx.wait("vector", sg)
            ctx.do("vector", lambda E: E.reciprocal(rux448[:], rm_x[:]), inc=sv)
            ctx.wait("vector", sv)
            ctx.do("vector", lambda E: E.tensor_scalar(rux448[:], rux448[:], 448.0, None, A.mult), inc=sv)
            ctx.do("vector", lambda E: E.tensor_scalar(sxb[:], rm_x[:], 4.0 / 448.0, None, A.mult), inc=sv)
            _qa, _qb = broadcast_tensor_aps(
                xn[:, :, 512:1536], rux448[:].rearrange("p t -> p () t"))
            ctx.do("vector", lambda E, a=_qa, b=_qb: E._custom_dve(
                QCAST, out=qx[:], in0=a, in1=b, s1=0.5), inc=sv)
            barrier()

            RAdd = bass_isa.ReduceOp.add
            snap = {}
            mmsnap = {}

            def wdma(src_ap, li, guard=None):
                b = li % 2
                if guard is not None:
                    ctx.wait_val("sync", sv, guard)
                ctx.do("sync", lambda E, src=src_ap, li=li, b=b: E.dma_start(
                    out=wf_st[b][:].rearrange("p (c w) -> p c w", c=DC),
                    in_=src.rearrange("(c p) f -> p c f", p=128)[:, :, (li // 3) * 128:(li // 3 + 1) * 128]),
                    inc=sx, dma=True)

            def wcast(li, mmguard=None):
                b = li % 2
                ctx.wait("vector", sx)
                if mmguard is not None:
                    ctx.wait_val("vector", st_, mmguard)
                ctx.do("vector", lambda E, b=b: E.tensor_copy(wf_bf[b][:], wf_st[b][:]), inc=sv)
                snap[("cast", li)] = ctx.sems["v"].n

            def l2chain(psums, n, mm_snap, xr2_guard):
                ctx.wait_val("scalar", st_, mm_snap)
                if xr2_guard:
                    ctx.wait_val("scalar", sv, xr2_guard)
                ctx.wait("scalar", sg)
                for hf in range(len(psums)):
                    ctx.do("scalar", lambda E, hf=hf, ps=psums[hf]: E.activation(
                        xr[2][:, hf * 512:(hf + 1) * 512], ps[:], ACT.Square), inc=ssc)
                sq_snap = ctx.sems["s"].n
                ctx.wait("gpsimd", ssc, sv)
                ctx.do("gpsimd", lambda E, n=n: E.partition_all_reduce(
                    rnb[:, 0:n], xr[2][:, 0:n], 128, RAdd), inc=sg)
                ctx.wait("vector", sg)
                ctx.do("vector", lambda E, n=n: E.tensor_scalar(
                    rnb[:, 0:n], rnb[:, 0:n], 1e-24, None, A.max), inc=sv)
                ctx.wait_val("scalar", sv, ctx.sems["v"].n)
                ctx.do("scalar", lambda E, n=n: E.activation(
                    xr[1][:, 0:n], rnb[:, 0:n], ACT.Sqrt), inc=ssc)
                ctx.wait("vector", ssc)
                ctx.do("vector", lambda E, n=n: E.reciprocal(xr[2][:, 0:n], xr[1][:, 0:n]), inc=sv)
                ctx.wait("vector", sv)
                ctx.do("vector", lambda E, n=n: E.tensor_tensor(
                    xr[1][:, 0:n], xr[2][:, 0:n], xr[2][:, 0:n], A.mult), inc=sv)
                ctx.do("vector", lambda E, n=n: E.tensor_tensor(
                    xr[1][:, 0:n], rnb[:, 0:n], xr[1][:, 0:n], A.mult), inc=sv)
                ctx.do("vector", lambda E, n=n: E.tensor_scalar(
                    xr[1][:, 0:n], xr[1][:, 0:n], -0.5, 1.5, A.mult, A.add), inc=sv)
                ctx.do("vector", lambda E, n=n: E.tensor_tensor(
                    rnb[:, 0:n], xr[2][:, 0:n], xr[1][:, 0:n], A.mult), inc=sv)
                return sq_snap

            wdma(WqT, 0)
            wcast(0)
            wdma(WkT, 1)
            wcast(1)
            for h in range(H):
                if h > 0:
                    ctx.wait_val("tensor", sv, snap[("attnv", h - 1)])
                else:
                    ctx.wait_val("tensor", sv, snap[("cast", 0)])
                for c in range(DC):
                    for hf in range(2):
                        ctx.do("tensor", lambda E, h=h, hf=hf, c=c: E.matmul(
                            ps_pr[3 + hf][:], wf_bf[(3 * h) % 2][:, c * 128:(c + 1) * 128],
                            xn[:, c, 512 + hf * 512: 512 + (hf + 1) * 512],
                            start=(c == 0), stop=(c == DC - 1)), inc=st_)
                mmsnap[3 * h] = ctx.sems["t"].n
                wdma(WvT, 3 * h + 2, guard=snap[("cast", 3 * h)])
                ctx.wait_val("tensor", sv,
                             max(snap[("cast", 3 * h + 1)], snap.get(("kn", h - 1), 0)))
                if ("ksq", h - 1) in snap:
                    ctx.wait_val("tensor", ssc, snap[("ksq", h - 1)])
                for c in range(DC):
                    for hf in range(3):
                        ctx.do("tensor", lambda E, h=h, hf=hf, c=c: E.matmul(
                            ps_pr[hf][:], wf_bf[(3 * h + 1) % 2][:, c * 128:(c + 1) * 128],
                            xn[:, c, hf * 512:(hf + 1) * 512],
                            start=(c == 0), stop=(c == DC - 1)), inc=st_)
                mmsnap[3 * h + 1] = ctx.sems["t"].n
                l2chain([ps_pr[3], ps_pr[4]], T, mmsnap[3 * h], None)
                for hf in range(2):
                    ctx.do("vector", lambda E, hf=hf: E.scalar_tensor_tensor(
                        q_h[:, hf * 512:(hf + 1) * 512], ps_pr[3 + hf][:], qs8[:],
                        rnb[:, hf * 512:(hf + 1) * 512], A.mult, A.mult), inc=sv)
                snap[("qn", h)] = ctx.sems["v"].n
                wcast(3 * h + 2, mmguard=mmsnap[3 * h])
                vbanks = [ps_pr[3], ps_pr[4], ps_pv[0]]
                ctx.wait_val("tensor", sv, snap[("cast", 3 * h + 2)])
                for c in range(DC):
                    for hf in range(3):
                        ctx.do("tensor", lambda E, h=h, hf=hf, c=c, ps=vbanks[hf]: E.matmul(
                            ps[:], wf_bf[(3 * h + 2) % 2][:, c * 128:(c + 1) * 128],
                            xn[:, c, hf * 512:(hf + 1) * 512],
                            start=(c == 0), stop=(c == DC - 1)), inc=st_)
                mmsnap[3 * h + 2] = ctx.sems["t"].n
                if h + 1 < H:
                    wdma(WqT, 3 * h + 3, guard=snap[("cast", 3 * h + 1)])
                snap[("ksq", h)] = l2chain(
                    [ps_pr[0], ps_pr[1], ps_pr[2]], EXT, mmsnap[3 * h + 1], snap[("qn", h)])
                for hf in range(3):
                    ctx.do("vector", lambda E, hf=hf: E.scalar_tensor_tensor(
                        k_h[:, hf * 512:(hf + 1) * 512], ps_pr[hf][:], ks[:],
                        rnb[:, hf * 512:(hf + 1) * 512], A.mult, A.mult), inc=sv)
                snap[("kn", h)] = ctx.sems["v"].n
                if h + 1 < H:
                    wcast(3 * h + 3, mmguard=mmsnap[3 * h + 1])
                ctx.wait_val("vector", st_, mmsnap[3 * h + 2])
                for hf in range(3):
                    ctx.do("vector", lambda E, hf=hf, ps=vbanks[hf]: E.tensor_copy(
                        vfm[:, hf * 512:(hf + 1) * 512], ps[:]), inc=sv)
                snap[("vevac", h)] = ctx.sems["v"].n
                if h + 1 < H:
                    wdma(WkT, 3 * h + 4, guard=snap[("cast", 3 * h + 2)])
                prev = snap[("vevac", h)]
                for tc in range(TCE):
                    ctx.wait_val("tensor", sv, prev)
                    ctx.do("tensor", lambda E, tc=tc: E.transpose(
                        ps_tr[:, 0:128], vfm[:, tc * 128:(tc + 1) * 128], ident[:]), inc=st_)
                    ctx.wait("vector", st_)
                    ctx.do("vector", lambda E, tc=tc: E.tensor_copy(
                        v_augh[:, tc, 0:128], ps_tr[:, 0:128]), inc=sv)
                    prev = ctx.sems["v"].n
                if h + 1 < H:
                    wcast(3 * h + 4, mmguard=mmsnap[3 * h + 2])
                for qb in range(2):
                    mask_snap = {}
                    for kc in range(8):
                        if kc >= 2:
                            ctx.wait_val("tensor", sv, mask_snap[kc - 2])
                        ctx.do("tensor", lambda E, h=h, qb=qb, kc=kc: E.matmul(
                            ps_pr[3 + kc % 2][:],
                            k_h[:, qb * 512 + kc * 128: qb * 512 + (kc + 1) * 128],
                            q_h[:, qb * 512:(qb + 1) * 512],
                            start=True, stop=True), inc=st_)
                        ctx.wait("vector", st_, ssc)
                        col = (qb * 8 + kc) * 2
                        ctx.do("vector", lambda E, kc=kc, col=col: E.tensor_mask_reduce(
                            sc_sb[:], ps_pr[3 + kc % 2][:],
                            mse[:, col:col + 1], mse[:, col + 1:col + 2],
                            1.0, 0.0, A.max), inc=sv)
                        mask_snap[kc] = ctx.sems["v"].n
                        ctx.wait_val("scalar", sv, mask_snap[kc])
                        ctx.wait("scalar", st_)
                        ctx.do("scalar", lambda E, kc=kc: E.activation(
                            p_sb[:, kc, :], sc_sb[:], ACT.Exp), inc=ssc)
                    for qc in range(4):
                        ctx.wait("tensor", ssc)
                        for kc in range(8):
                            ctx.do("tensor", lambda E, qb=qb, qc=qc, kc=kc: E.matmul(
                                ps_pv[qc % 2][:, 0:129],
                                p_sb[:, kc, qc * 128:(qc + 1) * 128],
                                v_augh[:, qb * 4 + kc, 0:129],
                                start=(kc == 0), stop=(kc == 7)), inc=st_)
                        ctx.wait("vector", st_)
                        ctx.do("vector", lambda E, qc=qc: E.reciprocal(
                            oscl[:, qc % 2:qc % 2 + 1], ps_pv[qc % 2][:, 128:129]), inc=sv)
                        ctx.wait("vector", sv)
                        tc = qb * 4 + qc
                        ctx.do("vector", lambda E, qc=qc, tc=tc, h=h: E.tensor_tensor(
                            oscl[:, qc % 2:qc % 2 + 1], oscl[:, qc % 2:qc % 2 + 1],
                            gates[:, tc * H + h: tc * H + h + 1], A.mult), inc=sv)
                        ctx.wait("scalar", sv)
                        ctx.do("scalar", lambda E, qc=qc: E.activation(
                            otm_sb[qc % 2][:], ps_pv[qc % 2][:, 0:128], ACT.Copy,
                            scale=oscl[:, qc % 2:qc % 2 + 1]), inc=ssc)
                        ctx.wait("tensor", ssc, sv)
                        ctx.do("tensor", lambda E, qc=qc: E.transpose(
                            ps_tr[:, 128:256], otm_sb[qc % 2][:], ident[:]), inc=st_)
                        ctx.wait("vector", st_)
                        ctx.do("vector", lambda E, h=h, qb=qb, qc=qc: E.tensor_copy(
                            o_fm[:, h, qb * 512 + qc * 128: qb * 512 + (qc + 1) * 128],
                            ps_tr[:, 128:256]), inc=sv)
                snap[("attnv", h)] = ctx.sems["v"].n
            barrier()

            snap_qv = {}
            snap_qs = {}

            def ph5_dma(ic):
                b = ic % 2
                if ic - 2 in snap_qv:
                    ctx.wait_val("sync", sv, snap_qv[ic - 2])
                    ctx.wait_val("sync", ssc, snap_qs[ic - 2])
                ctx.do("sync", lambda E, ic=ic, b=b: E.dma_start(
                    out=wg_st[b][:].rearrange("p (c w) -> p c w", c=DC),
                    in_=wgP.rearrange("(c p) f -> p c f", p=128)[:, :, ic * 128:(ic + 1) * 128]),
                    inc=sx, dma=True)
                ctx.do("sync", lambda E, ic=ic, b=b: E.dma_start(
                    out=wu_st[b][:].rearrange("p (c w) -> p c w", c=DC),
                    in_=wuP.rearrange("(c p) f -> p c f", p=128)[:, :, ic * 128:(ic + 1) * 128]),
                    inc=sx, dma=True)

            def ph5_chain(ic):
                b = ic % 2
                for wi, wst, pp in ((0, wg_st, wsg_pp), (1, wu_st, wsu_pp)):
                    ctx.wait("scalar", sx, sv, sg)
                    ctx.do("scalar", lambda E, b=b, wst=wst: E.activation(
                        wab[:], wst[b][:], ACT.Abs), inc=ssc)
                    ctx.wait("vector", ssc)
                    w_ = DC * 128
                    for step in (w_ // 2, w_ // 4, w_ // 8, w_ // 16):
                        ctx.do("vector", lambda E, step=step: E.tensor_tensor(
                            wab[:, 0:step], wab[:, 0:step], wab[:, step:2 * step],
                            A.max), inc=sv)
                    ctx.wait("gpsimd", sv)
                    ctx.do("gpsimd", lambda E, b=b, wi=wi: E.partition_all_reduce(
                        rug[b][:, 256 + wi * 128:256 + (wi + 1) * 128], wab[:, 0:128],
                        128, RMax), inc=sg)
                snap_qs[ic] = ctx.sems["s"].n
                ctx.wait("scalar", sg)
                ctx.do("scalar", lambda E, ic=ic, b=b: E.dma_start(
                    out=wsg_pp[:, ic:ic + 1], in_=rug[b][0:1, 256:384]), inc=sst, dma=True)
                ctx.do("scalar", lambda E, ic=ic, b=b: E.dma_start(
                    out=wsu_pp[:, ic:ic + 1], in_=rug[b][0:1, 384:512]), inc=sst, dma=True)
                ctx.wait("vector", sg, sst)
                ctx.do("vector", lambda E, ic=ic: E.tensor_scalar(
                    wsg_pp[:, ic:ic + 1], wsg_pp[:, ic:ic + 1], 1.0 / 448.0, None,
                    A.mult), inc=sv)
                ctx.do("vector", lambda E, ic=ic: E.tensor_scalar(
                    wsu_pp[:, ic:ic + 1], wsu_pp[:, ic:ic + 1], 1.0 / 448.0, None,
                    A.mult), inc=sv)
                for wi in range(2):
                    ctx.do("vector", lambda E, b=b, wi=wi: E.reciprocal(
                        rug[b][:, wi * 128:(wi + 1) * 128],
                        rug[b][:, 256 + wi * 128:256 + (wi + 1) * 128]), inc=sv)
                    ctx.wait("vector", sv)
                    ctx.do("vector", lambda E, b=b, wi=wi: E.tensor_scalar(
                        rug[b][:, wi * 128:(wi + 1) * 128],
                        rug[b][:, wi * 128:(wi + 1) * 128], 448.0, None, A.mult), inc=sv)
                for wi, wst, qw in ((0, wg_st, qwg), (1, wu_st, qwu)):
                    _ia, _ib = broadcast_tensor_aps(
                        wst[b][:].rearrange("p (c w) -> p c w", c=DC),
                        rug[b][:, wi * 128:(wi + 1) * 128].rearrange("p c -> p () c"))
                    ctx.do("vector", lambda E, qw=qw, b=b, a=_ia, bb=_ib: E._custom_dve(
                        QCAST, out=qw[b][:], in0=a, in1=bb, s1=0.5), inc=sv)
                snap_qv[ic] = ctx.sems["v"].n

            ph5_dma(0)
            ph5_chain(0)
            for ic in range(IC):
                b = ic % 2
                if ic + 1 < IC:
                    ph5_dma(ic + 1)
                    ph5_chain(ic + 1)
                ctx.wait_val("tensor", sv, snap_qv[ic])
                for c in range(DC):
                    for wi, qw in ((0, qwg), (1, qwu)):
                        for hf in range(2):
                            ctx.do("tensor", lambda E, b=b, qw=qw, wi=wi, hf=hf, c=c: E.matmul(
                                ps_gu[b * 4 + wi * 2 + hf][:], qw[b][:, c, :],
                                qx[:, c, hf * 512:(hf + 1) * 512],
                                start=(c == 0), stop=(c == DC - 1)), inc=st_)
                ctx.wait("vector", st_)
                for hf in range(2):
                    ctx.wait("vector", ssc)
                    ctx.do("vector", lambda E, ic=ic, b=b, hf=hf: E.scalar_tensor_tensor(
                        gu_sb[:, 0:512], ps_gu[b * 4 + hf][:], wsg_pp[:, ic:ic + 1],
                        sxb[:, hf * 512:(hf + 1) * 512], A.mult, A.mult), inc=sv)
                    ctx.wait("scalar", sv)
                    ctx.do("scalar", lambda E, hf=hf: E.activation(
                        sil_sb[hf][:], gu_sb[:, 0:512], ACT.Silu), inc=ssc)
                    ctx.do("vector", lambda E, ic=ic, b=b, hf=hf: E.scalar_tensor_tensor(
                        gu_sb[:, 512:1024], ps_gu[b * 4 + 2 + hf][:], wsu_pp[:, ic:ic + 1],
                        sxb[:, hf * 512:(hf + 1) * 512], A.mult, A.mult), inc=sv)
                    ctx.wait("vector", ssc)
                    ctx.do("vector", lambda E, ic=ic, hf=hf: E.tensor_tensor(
                        t_sb[:, ic, hf * 512:(hf + 1) * 512], sil_sb[hf][:],
                        gu_sb[:, 512:1024], A.mult), inc=sv)
            barrier()

            ctx.do("vector", lambda E: E.tensor_reduce(
                rm_t[:], t_sb[:].rearrange("p c t -> p t c"), AXX, A.max,
                apply_absolute_value=True), inc=sv)
            ctx.wait("gpsimd", sv)
            ctx.do("gpsimd", lambda E: E.partition_all_reduce(rm_t[:], rm_t[:], 128, RMax), inc=sg)
            ctx.wait("vector", sg)
            ctx.do("vector", lambda E: E.reciprocal(rut448[:], rm_t[:]), inc=sv)
            ctx.wait("vector", sv)
            ctx.do("vector", lambda E: E.tensor_scalar(rut448[:], rut448[:], 448.0, None, A.mult), inc=sv)
            ctx.do("vector", lambda E: E.tensor_scalar(sxtb[:], rm_t[:], 4.0 / 448.0, None, A.mult), inc=sv)
            _ta, _tb = broadcast_tensor_aps(
                t_sb[:], rut448[:].rearrange("p t -> p () t"))
            ctx.do("vector", lambda E, a=_ta, b=_tb: E._custom_dve(
                QCAST, out=qt[:], in0=a, in1=b, s1=0.5), inc=sv)
            barrier()

            for ic in range(IC):
                b = ic % 2
                ctx.wait("sync", sv)
                ctx.do("sync", lambda E, ic=ic, b=b: E.dma_start(
                    out=wd_st[b][:], in_=wdT[ic * 128:(ic + 1) * 128, :]), inc=sx, dma=True)
                ctx.wait("vector", sx)
                ctx.do("vector", lambda E, ic=ic, b=b: E.scalar_tensor_tensor(
                    rm_wd[:], wd_st[b][:], -1.0,
                    (wd_st[b][:] if ic == 0 else rm_wd[:]), A.mult, A.max), inc=sv)
                ctx.do("vector", lambda E, b=b: E.scalar_tensor_tensor(
                    rm_wd[:], wd_st[b][:], 1.0, rm_wd[:], A.mult, A.max), inc=sv)
            ctx.wait("gpsimd", sv)
            ctx.do("gpsimd", lambda E: E.partition_all_reduce(rm_wd[:], rm_wd[:], 128, RMax), inc=sg)
            ctx.wait("scalar", sg)
            for oc in range(DC):
                ctx.do("scalar", lambda E, oc=oc: E.dma_start(
                    out=wsd_pp[:, oc:oc + 1], in_=rm_wd[0:1, oc * 128:(oc + 1) * 128]),
                    inc=sst, dma=True)
            ctx.wait("vector", sst, sg)
            ctx.do("vector", lambda E: E.tensor_scalar(
                wsd_pp[:], wsd_pp[:], 1.0 / 448.0, None, A.mult), inc=sv)
            ctx.do("vector", lambda E: E.reciprocal(ruwd448[:], rm_wd[:]), inc=sv)
            ctx.wait("vector", sv)
            ctx.do("vector", lambda E: E.tensor_scalar(ruwd448[:], ruwd448[:], 448.0, None, A.mult), inc=sv)
            barrier()

            for grp in range(2):
                for ic in range(IC):
                    b = ic % 2
                    ctx.wait("sync", sv)
                    ctx.do("sync", lambda E, ic=ic, b=b: E.dma_start(
                        out=wd_st[b][:], in_=wdT[ic * 128:(ic + 1) * 128, :]),
                        inc=sx, dma=True)
                    ctx.wait("vector", sx)
                    ctx.do("vector", lambda E, grp=grp, ic=ic, b=b: E._custom_dve(
                        QCAST, out=qwd[:, ic, :],
                        in0=wd_st[b][:, grp * 1024:(grp + 1) * 1024],
                        in1=ruwd448[:, grp * 1024:(grp + 1) * 1024], s1=0.5), inc=sv)
                for oci in range(8):
                    oc = grp * 8 + oci
                    ctx.wait("sync", sv, sst)
                    ctx.do("sync", lambda E, oc=oc: E.dma_start(
                        out=xo_sb[:], in_=xT[oc * 128:(oc + 1) * 128, 512:1536]),
                        inc=sx, dma=True)
                    ctx.do("sync", lambda E, oc=oc: E.dma_start(
                        out=wo_st[:].rearrange("p (c w) -> p c w", c=H),
                        in_=WoT.rearrange("(c p) f -> p c f", p=128)[:, :, oc * 128:(oc + 1) * 128]),
                        inc=sx, dma=True)
                    ctx.wait("vector", sx)
                    ctx.do("vector", lambda E: E.tensor_copy(wo_bf[:], wo_st[:]), inc=sv)
                    ctx.wait("tensor", sv)
                    for hf in range(2):
                        for ic in range(IC):
                            ctx.do("tensor", lambda E, oci=oci, hf=hf, ic=ic: E.matmul(
                                ps_d[hf][:], qwd[:, ic, oci * 128:(oci + 1) * 128],
                                qt[:, ic, hf * 512:(hf + 1) * 512],
                                start=(ic == 0), stop=(ic == IC - 1)), inc=st_)
                        for h in range(H):
                            ctx.do("tensor", lambda E, hf=hf, h=h: E.matmul(
                                ps_wo[hf][:], wo_bf[:, h * 128:(h + 1) * 128],
                                o_fm[:, h, hf * 512:(hf + 1) * 512],
                                start=(h == 0), stop=(h == H - 1)), inc=st_)
                    ctx.wait("vector", st_, sst)
                    for hf in range(2):
                        sl = slice(hf * 512, (hf + 1) * 512)
                        ctx.do("vector", lambda E, oc=oc, hf=hf, sl=sl: E.scalar_tensor_tensor(
                            out_sb[:, sl], ps_d[hf][:], wsd_pp[:, oc:oc + 1],
                            sxtb[:, sl], A.mult, A.mult), inc=sv)
                        ctx.do("vector", lambda E, hf=hf, sl=sl: E.scalar_tensor_tensor(
                            out_sb[:, sl], ps_wo[hf][:], 1.0, out_sb[:, sl],
                            A.mult, A.add), inc=sv)
                        ctx.do("vector", lambda E, hf=hf, sl=sl: E.tensor_tensor(
                            out_sb[:, sl], out_sb[:, sl], xo_sb[:, sl], A.add), inc=sv)
                    ctx.wait("scalar", sv)
                    ctx.do("scalar", lambda E, oc=oc: E.dma_start(
                        out=out_d[oc * 128:(oc + 1) * 128, :], in_=out_sb[:]),
                        inc=sst, dma=True)
            ctx.wait("scalar", sst)
            ctx.wait("sync", sst)

        @blk.sync
        def _(_e):
            walk(Ctx("sync", nc, sems))

        @blk.scalar
        def _(_e):
            walk(Ctx("scalar", nc, sems))

        @blk.vector
        def _(_e):
            walk(Ctx("vector", nc, sems))

        @blk.tensor
        def _(_e):
            walk(Ctx("tensor", nc, sems))

        @blk.gpsimd
        def _(_e):
            walk(Ctx("gpsimd", nc, sems))

        es5.close()
        es6.close()

    inst_map = {}
    for lib in all_libraries:
        for it in lib.instructions:
            inst_map[it] = inst_map.get(it, 0) | (1 << lib.index)
    _bass_rust.insert_library_loads(nc, inst_map, len(all_libraries), _lib_std.index)
    mybir.codegen_inst_isa_subclasses(nc)
    return nc


def _stage_inputs(inputs):
    f32 = np.float32
    x = np.ascontiguousarray(np.asarray(inputs["L0_attn_x"], dtype=f32)).reshape(B * N, D)
    Wq = np.asarray(inputs["L0_attn_Wq"], dtype=f32)
    Wkv = np.asarray(inputs["L0_attn_Wkv"], dtype=f32)
    Wg = np.asarray(inputs["L0_attn_Wg"], dtype=f32)
    bg = np.asarray(inputs["L0_attn_bg"], dtype=f32)
    Wo = np.asarray(inputs["L0_attn_Wo"], dtype=f32)
    w_gate = np.asarray(inputs["L0_ffn_w_gate"], dtype=f32)
    w_up = np.asarray(inputs["L0_ffn_w_up"], dtype=f32)
    w_down = np.asarray(inputs["L0_ffn_w_down"], dtype=f32)
    perm = np.asarray(inputs["L0_ffn_down_reorder_index"]).astype(np.int64)
    q_scale = np.asarray(inputs["L0_attn_q_scale"], dtype=f32)
    k_scale = np.asarray(inputs["L0_attn_k_scale"], dtype=f32)

    shared = {
        "WqT": np.ascontiguousarray(Wq.T),
        "WkT": np.ascontiguousarray(Wkv[:H * DH].T),
        "WvT": np.ascontiguousarray(Wkv[H * DH:].T),
        "WgT": np.ascontiguousarray(Wg.T),
        "WoT": np.ascontiguousarray(Wo.T),
        "wgP": np.ascontiguousarray(w_gate[perm].T),
        "wuP": np.ascontiguousarray(w_up[perm].T),
        "wdT": np.ascontiguousarray(w_down.T),
        "qs": np.ascontiguousarray(q_scale.reshape(128, 1)),
        "ks": np.ascontiguousarray(k_scale.reshape(128, 1)),
        "bgb": np.ascontiguousarray(np.tile(bg[None, :], (128, 1))),
        "ident": np.eye(128, dtype=f32),
    }
    in_maps = []
    for c in range(NCORES):
        r0 = c * T
        own = x[r0:r0 + T]
        halo = np.zeros((512, D), dtype=f32) if c % 4 == 0 else x[r0 - 512:r0]
        xTc = np.ascontiguousarray(np.concatenate([halo, own], axis=0).T)
        msk = np.zeros((128, 32), dtype=f32)
        p = np.arange(128, dtype=f32)
        for qb in range(2):
            for kc in range(8):
                j = kc * 128 + p
                start = j - 512.0
                end = j + 1.0
                if (c % 4 == 0) and qb == 0 and kc < 4:
                    start = np.full(128, 512.0, dtype=f32)
                    end = np.full(128, 513.0, dtype=f32)
                col = (qb * 8 + kc) * 2
                msk[:, col] = start
                msk[:, col + 1] = end
        m = dict(shared)
        m["xT"] = xTc
        m["mse"] = msk
        in_maps.append(m)
    return in_maps


def kernel(**inputs):
    in_maps = _stage_inputs(inputs)
    nc = build_nc()
    res = run_bass_kernel_spmd(nc, in_maps, core_ids=list(range(NCORES)))
    out = np.empty((B * N, D), dtype=np.float32)
    for c in range(NCORES):
        out[c * T:(c + 1) * T, :] = np.asarray(res.results[c]["out"]).T
    return out.reshape(B, N, D)


# version 06
# speedup vs baseline: 3.2242x; 1.0404x over previous
import numpy as np
import ml_dtypes
import concourse.bass as bass
import concourse.mybir as mybir
from concourse.bass_utils import run_bass_kernel_spmd
from concourse import dve_ops
from concourse.dve_op import DveOp
from concourse.dve_spec import Spec, Src0, Src1, C1, C2, lower
from concourse.dve_table_gen import dve_ver_for
import concourse.bass_isa as bass_isa
import bass_rust as _bass_rust
from concourse.library_config import all_libraries, standard as _lib_std
from contextlib import ExitStack
from concourse.bass import broadcast_tensor_aps

F32 = mybir.dt.float32
BF16 = mybir.dt.bfloat16
FP8 = mybir.dt.float8e4
A = mybir.AluOpType
ACT = mybir.ActivationFunctionType
AXX = mybir.AxisListType.X
RMax = bass_isa.ReduceOp.max

NCORES = 8
B, N, D = 2, 4096, 2048
H, DH = 8, 128
I = 5632
T = 1024
EXT = 1536
DC = D // 128
IC = I // 128
TC = T // 128
TCE = EXT // 128
VA = 132
LN_EPS = 1e-5
ENGINES = ("sync", "scalar", "vector", "tensor", "gpsimd")


def _qcast_ref(in0, in1, s0, s1, imm2):
    v = np.asarray(in0) * np.asarray(in1)
    q = v.astype(ml_dtypes.float8_e4m3fn).astype(v.dtype)
    return q * (np.asarray(s1) if s1 is not None else 1.0)


def _register_qcast():
    for op in dve_ops.OPS:
        if op.name == "QCAST":
            return op
    spec = Spec(body=(Src0 * Src1) * C1)
    ver = dve_ver_for("TRN2")
    sha = lower(spec, ver=ver, name="QCAST").sha(ver)
    op = DveOp("QCAST", spec, subdim=False, uops_sha={ver: sha}, reference=_qcast_ref)
    dve_ops.OPS.append(op)
    dve_ops.CUSTOM_DVE_SPECS["QCAST"] = spec
    dve_ops.CUSTOM_DVE_REFS["QCAST"] = _qcast_ref
    dve_ops._SUB_OPCODE_FOR_NAME["QCAST"] = dve_ops._CUSTOM_DVE_ROW_BASE + len(dve_ops.OPS) - 1
    return op


QCAST = _register_qcast()


class _Sem:
    __slots__ = ("h", "n")
    def __init__(self, h):
        self.h = h
        self.n = 0


SERIAL = False

class Ctx:
    def __init__(self, eng_name, nc, sems):
        self.eng = eng_name
        self.nc = nc
        self.sems = sems
        for s in sems.values():
            s.n = 0

    def _E(self):
        return getattr(self.nc, self.eng)

    def do(self, eng, fn, inc=None, dma=False):
        if self.eng == eng:
            E = getattr(self.nc, eng)
            if SERIAL:
                for s in self.sems.values():
                    if s.n > 0:
                        E.wait_ge(s.h, s.n)
            ins = fn(E)
            if inc is not None:
                ins.then_inc(inc.h, 16 if dma else 1)
        if inc is not None:
            inc.n += 16 if dma else 1

    def wait(self, eng, *sems):
        if self.eng == eng:
            for s in sems:
                getattr(self.nc, eng).wait_ge(s.h, s.n)

    def wait_val(self, eng, sem, val):
        if self.eng == eng and val > 0:
            getattr(self.nc, eng).wait_ge(sem.h, val)


def build_nc():
    nc = bass.Bass("TRN2", target_bir_lowering=False)
    din = lambda name, shape, dt=F32: nc.dram_tensor(name, shape, dt, kind="ExternalInput").ap()
    xT = din("xT", [D, EXT])
    WqT = din("WqT", [D, H * DH])
    WkT = din("WkT", [D, H * DH])
    WvT = din("WvT", [D, H * DH])
    WgT = din("WgT", [D, H])
    WoT = din("WoT", [H * DH, D])
    wgP = din("wgP", [D, I])
    wuP = din("wuP", [D, I])
    wdT = din("wdT", [I, D])
    qs_i = din("qs", [128, 1])
    ks_i = din("ks", [128, 1])
    bgb_i = din("bgb", [128, H])
    mse_i = din("mse", [128, 32])
    id_i = din("ident", [128, 128])
    out_d = nc.dram_tensor("out", [D, T], F32, kind="ExternalOutput").ap()

    es = ExitStack()
    with es:
        SB = lambda st, name, shape, dt=F32: st.enter_context(nc.sbuf_tensor(name, shape, dt))
        SBR = lambda st, name, shape, dt=F32: st.enter_context(nc.sbuf_tensor(name, shape, dt, side="right"))
        PS = lambda st, name, shape, dt=F32: st.enter_context(nc.psum_tensor(name, shape, dt))
        qs = SB(es, "qs_t", [128, 1]); ks = SB(es, "ks_t", [128, 1])
        qs8 = SB(es, "qs8_t", [128, 1])
        bgb = SB(es, "bgb_t", [128, H]); mse = SB(es, "mse_t", [128, 32])
        ident = SB(es, "ident_b", [128, 128], BF16)
        ones = SB(es, "ones_t", [128, 1])
        gates = SB(es, "gates_t", [128, TC * H])
        sxb = SB(es, "sxb", [128, T])
        oscl = SB(es, "oscl", [128, 2])
        wsg_pp = SB(es, "wsg_pp", [128, IC]); wsu_pp = SB(es, "wsu_pp", [128, IC])
        wsd_pp = SB(es, "wsd_pp", [128, DC])
        o_fm = SB(es, "o_fm", [128, H, T], BF16)
        vfm = SB(es, "vfm", [128, EXT], BF16)

        sems = {k: _Sem(es.enter_context(nc.semaphore("sem_" + k)))
                for k in ("x", "v", "s", "t", "g", "st", "ph")}

        es1 = ExitStack()
        mb = SB(es1, "mb", [128, EXT]); rb = SB(es1, "rb", [128, EXT])
        ps_a1 = [PS(es1, f"ps_a1{i}", [1, 512]) for i in range(3)]
        ps_b1 = [PS(es1, f"ps_b1{i}", [1, 512]) for i in range(3)]
        esX = ExitStack()
        rows = SBR(esX, "rows", [1, 4 * EXT])
        xr = [SBR(esX, f"xr{i}", [128, EXT]) for i in range(3)]
        es2 = ExitStack()
        xn = SBR(es2, "xn", [128, DC, EXT], BF16)
        rux448 = SBR(es2, "rux448", [128, T])
        rm_x = SBR(es2, "rm_x", [128, T])
        es3 = ExitStack()
        q_h = SBR(es3, "q_h", [128, T], BF16)
        k_h = SBR(es3, "k_h", [128, EXT], BF16)
        v_augh = SBR(es3, "v_augh", [128, TCE, VA], BF16)
        wf_st = [SBR(es3, f"wf_st{i}", [128, DC * 128]) for i in range(2)]
        wf_bf = [SBR(es3, f"wf_bf{i}", [128, DC * 128], BF16) for i in range(2)]
        rnb = SBR(es3, "rnb", [128, EXT])
        p_sb = SBR(es3, "p_sb", [128, 8, 512], BF16)
        sc_sb = SBR(es3, "sc_sb", [128, 512])
        otm_sb = [SBR(es3, f"otm{i}", [128, 128], BF16) for i in range(2)]
        es1.close()
        ps_pr = [PS(es3, f"ps_pr{i}", [128, 512]) for i in range(5)]
        ps_tr = PS(es3, "ps_tr", [128, 1024], BF16)
        ps_pv = [PS(es3, f"ps_pv{i}", [128, 512]) for i in range(2)]
        es2b = ExitStack()
        qx = SB(es2b, "qx", [128, DC, T], FP8)
        es3.close()
        es2.close()
        esX.close()
        es4 = ExitStack()
        t_sb = SBR(es4, "t_sb", [128, IC, T], BF16)
        es4a = ExitStack()
        wg_st = [SBR(es4a, f"wg_st{i}", [128, DC * 128]) for i in range(2)]
        wu_st = [SBR(es4a, f"wu_st{i}", [128, DC * 128]) for i in range(2)]
        qwg = [SBR(es4a, f"qwg{i}", [128, DC, 128], FP8) for i in range(2)]
        qwu = [SBR(es4a, f"qwu{i}", [128, DC, 128], FP8) for i in range(2)]
        gu_sb = SBR(es4a, "gu_sb", [128, 1024])
        sil_sb = [SBR(es4a, f"sil_sb{i}", [128, 512]) for i in range(2)]
        wab = SBR(es4a, "wab", [128, DC * 128])
        rug = [SBR(es4a, f"rug{i}", [128, 512]) for i in range(2)]
        ps_gu = [PS(es4a, f"ps_gu{i}", [128, 512]) for i in range(8)]
        es5 = ExitStack()
        es2b.close()
        es4a.close()
        qt = SB(es5, "qt", [128, IC, T], FP8)
        rm_t = SB(es5, "rm_t", [128, T])
        rut448 = SB(es5, "rut448", [128, T])
        sxtb = SB(es5, "sxtb", [128, T])
        es4.close()
        es6 = ExitStack()
        qwd = SBR(es6, "qwd", [128, IC, 8 * 128], FP8)
        wd_st = [SBR(es6, f"wd_st{i}", [128, D]) for i in range(2)]
        rm_wd = SBR(es6, "rm_wd", [128, D])
        ruwd448 = SBR(es6, "ruwd448", [128, D])
        wo_st = SBR(es6, "wo_st", [128, H * 128])
        wo_bf = SBR(es6, "wo_bf", [128, H * 128], BF16)
        xo_sb = SBR(es6, "xo_sb", [128, T])
        out_sb = SBR(es6, "out_sb", [128, T])
        ps_d = [PS(es6, f"ps_d{i}", [128, 512]) for i in range(2)]
        ps_wo = [PS(es6, f"ps_wo{i}", [128, 512]) for i in range(2)]

        blk = es.enter_context(nc.Block())

        def walk(ctx: Ctx):
            sx, sv, ssc, st_, sg, sst, sph = (ctx.sems[k] for k in
                                              ("x", "v", "s", "t", "g", "st", "ph"))
            phase_n = [0]

            def barrier():
                phase_n[0] += 1
                tgt = 5 * phase_n[0]
                for e in ENGINES:
                    ctx.do(e, lambda E: E.nop(), inc=sph)
                if ctx.eng is not None:
                    getattr(ctx.nc, ctx.eng).wait_ge(sph.h, tgt)

            m_row = rows[:, 0:EXT]
            e2_row = rows[:, EXT:2 * EXT]
            w_row = rows[:, 2 * EXT:3 * EXT]
            r_row = rows[:, 3 * EXT:4 * EXT]

            for src, dst in ((qs_i, qs), (ks_i, ks), (bgb_i, bgb), (mse_i, mse)):
                ctx.do("sync", lambda E, s=src, d=dst: E.dma_start(out=d[:], in_=s),
                       inc=sx, dma=True)
            ctx.do("sync", lambda E: E.dma_start(out=xr[0][:, 0:128], in_=id_i),
                   inc=sx, dma=True)
            ctx.do("gpsimd", lambda E: E.memset(ones[:], 1.0), inc=sg)
            ctx.do("gpsimd", lambda E: E.memset(v_augh[:], 0.0), inc=sg)
            ctx.do("gpsimd", lambda E: E.memset(v_augh[:, :, 128:129], 1.0), inc=sg)
            ctx.wait("vector", sx)
            ctx.do("vector", lambda E: E.tensor_copy(ident[:], xr[0][:, 0:128]), inc=sv)
            ctx.do("vector", lambda E: E.tensor_scalar(qs8[:], qs[:], 8.0, None, A.mult), inc=sv)
            barrier()

            for c in range(DC):
                b = c % 2
                ctx.wait("sync", st_)
                ctx.do("sync", lambda E, c=c, b=b: E.dma_start(
                    out=xr[b][:], in_=xT[c * 128:(c + 1) * 128, :]), inc=sx, dma=True)
                ctx.wait("scalar", sx, st_)
                ctx.do("scalar", lambda E, b=b: E.activation(
                    xr[2][:], xr[b][:], ACT.Square), inc=ssc)
                ctx.wait("tensor", sx, ssc)
                for h3 in range(3):
                    ctx.do("tensor", lambda E, c=c, b=b, h3=h3: E.matmul(
                        ps_a1[h3][:], ones[:], xr[b][:, h3 * 512:(h3 + 1) * 512],
                        start=(c == 0), stop=(c == DC - 1)), inc=st_)
                    ctx.do("tensor", lambda E, c=c, h3=h3: E.matmul(
                        ps_b1[h3][:], ones[:], xr[2][:, h3 * 512:(h3 + 1) * 512],
                        start=(c == 0), stop=(c == DC - 1)), inc=st_)
            ctx.wait("vector", st_)
            for h3 in range(3):
                ctx.do("vector", lambda E, h3=h3: E.tensor_scalar(
                    m_row[:, h3 * 512:(h3 + 1) * 512], ps_a1[h3][:], 1.0 / D, None,
                    A.mult), inc=sv)
                ctx.do("vector", lambda E, h3=h3: E.tensor_scalar(
                    e2_row[:, h3 * 512:(h3 + 1) * 512], ps_b1[h3][:], 1.0 / D, None,
                    A.mult), inc=sv)
            ctx.do("vector", lambda E: E.scalar_tensor_tensor(
                w_row, m_row, -1.0, m_row, A.mult, A.mult), inc=sv)
            ctx.do("vector", lambda E: E.tensor_tensor(w_row, e2_row, w_row, A.add), inc=sv)
            ctx.do("vector", lambda E: E.tensor_scalar(w_row, w_row, LN_EPS, None, A.add), inc=sv)
            ctx.wait("scalar", sv)
            ctx.do("scalar", lambda E: E.activation(r_row, w_row, ACT.Sqrt), inc=ssc)
            ctx.wait("vector", ssc)
            ctx.do("vector", lambda E: E.reciprocal(e2_row, r_row), inc=sv)
            ctx.wait("vector", sv)
            ctx.do("vector", lambda E: E.tensor_tensor(r_row, e2_row, e2_row, A.mult), inc=sv)
            ctx.do("vector", lambda E: E.tensor_tensor(r_row, w_row, r_row, A.mult), inc=sv)
            ctx.do("vector", lambda E: E.tensor_scalar(r_row, r_row, -0.5, 1.5, A.mult, A.add), inc=sv)
            ctx.do("vector", lambda E: E.tensor_tensor(r_row, e2_row, r_row, A.mult), inc=sv)
            ctx.wait("gpsimd", sv)
            ctx.do("gpsimd", lambda E: E.partition_broadcast(mb[:], m_row, channels=128), inc=sg)
            ctx.do("gpsimd", lambda E: E.partition_broadcast(rb[:], r_row, channels=128), inc=sg)
            barrier()

            for c in range(DC):
                b = c % 2
                ctx.wait("sync", sv)
                ctx.do("sync", lambda E, c=c, b=b: E.dma_start(
                    out=xr[b][:], in_=xT[c * 128:(c + 1) * 128, :]), inc=sx, dma=True)
                ctx.wait("vector", sx, sg)
                ctx.do("vector", lambda E, b=b: E.scalar_tensor_tensor(
                    xr[2][:], mb[:], -1.0, xr[b][:], A.mult, A.add), inc=sv)
                ctx.do("vector", lambda E, c=c: E.tensor_tensor(
                    xn[:, c, :], xr[2][:], rb[:], A.mult), inc=sv)
            barrier()

            ctx.do("sync", lambda E: E.dma_start(
                out=wf_st[0][:].rearrange("p (c w) -> p c w", c=DC)[:, :, 0:H],
                in_=WgT.rearrange("(c p) f -> p c f", p=128)), inc=sx, dma=True)
            ctx.wait("vector", sx)
            ctx.do("vector", lambda E: E.tensor_copy(
                wf_bf[0][:].rearrange("p (c w) -> p c w", c=DC)[:, :, 0:H],
                wf_st[0][:].rearrange("p (c w) -> p c w", c=DC)[:, :, 0:H]), inc=sv)
            for tc in range(TC):
                ctx.wait("tensor", sv)
                for c in range(DC):
                    ctx.do("tensor", lambda E, tc=tc, c=c: E.matmul(
                        ps_pr[3][:, 0:H],
                        xn[:, c, 512 + tc * 128: 512 + (tc + 1) * 128],
                        wf_bf[0][:].rearrange("p (c w) -> p c w", c=DC)[:, c, 0:H],
                        start=(c == 0), stop=(c == DC - 1)), inc=st_)
                ctx.wait("vector", st_, ssc)
                ctx.do("vector", lambda E: E.tensor_tensor(
                    xr[2][:, 0:H], ps_pr[3][:, 0:H], bgb[:], A.add), inc=sv)
                ctx.wait("scalar", sv)
                ctx.do("scalar", lambda E, tc=tc: E.activation(
                    gates[:, tc * H:(tc + 1) * H], xr[2][:, 0:H], ACT.Sigmoid), inc=ssc)
            ctx.do("vector", lambda E: E.tensor_reduce(
                rm_x[:], xn[:, :, 512:1536].rearrange("p c t -> p t c"), AXX, A.max,
                apply_absolute_value=True), inc=sv)
            ctx.wait("gpsimd", sv)
            ctx.do("gpsimd", lambda E: E.partition_all_reduce(rm_x[:], rm_x[:], 128, RMax), inc=sg)
            ctx.wait("vector", sg)
            ctx.do("vector", lambda E: E.reciprocal(rux448[:], rm_x[:]), inc=sv)
            ctx.wait("vector", sv)
            ctx.do("vector", lambda E: E.tensor_scalar(rux448[:], rux448[:], 448.0, None, A.mult), inc=sv)
            ctx.do("vector", lambda E: E.tensor_scalar(sxb[:], rm_x[:], 4.0 / 448.0, None, A.mult), inc=sv)
            _qa, _qb = broadcast_tensor_aps(
                xn[:, :, 512:1536], rux448[:].rearrange("p t -> p () t"))
            ctx.do("vector", lambda E, a=_qa, b=_qb: E._custom_dve(
                QCAST, out=qx[:], in0=a, in1=b, s1=0.5), inc=sv)
            barrier()

            RAdd = bass_isa.ReduceOp.add
            snap = {}
            mmsnap = {}

            def wdma(src_ap, li, guard=None):
                b = li % 2
                if guard is not None:
                    ctx.wait_val("sync", sv, guard)
                ctx.do("sync", lambda E, src=src_ap, li=li, b=b: E.dma_start(
                    out=wf_st[b][:].rearrange("p (c w) -> p c w", c=DC),
                    in_=src.rearrange("(c p) f -> p c f", p=128)[:, :, (li // 3) * 128:(li // 3 + 1) * 128]),
                    inc=sx, dma=True)

            def wcast(li, mmguard=None):
                b = li % 2
                ctx.wait("vector", sx)
                if mmguard is not None:
                    ctx.wait_val("vector", st_, mmguard)
                ctx.do("vector", lambda E, b=b: E.tensor_copy(wf_bf[b][:], wf_st[b][:]), inc=sv)
                snap[("cast", li)] = ctx.sems["v"].n

            def l2chain(psums, n, mm_snap, xr2_guard):
                ctx.wait_val("scalar", st_, mm_snap)
                if xr2_guard:
                    ctx.wait_val("scalar", sv, xr2_guard)
                ctx.wait("scalar", sg)
                for hf in range(len(psums)):
                    ctx.do("scalar", lambda E, hf=hf, ps=psums[hf]: E.activation(
                        xr[2][:, hf * 512:(hf + 1) * 512], ps[:], ACT.Square), inc=ssc)
                sq_snap = ctx.sems["s"].n
                ctx.wait("gpsimd", ssc, sv)
                ctx.do("gpsimd", lambda E, n=n: E.partition_all_reduce(
                    rnb[:, 0:n], xr[2][:, 0:n], 128, RAdd), inc=sg)
                ctx.wait("vector", sg)
                ctx.do("vector", lambda E, n=n: E.tensor_scalar(
                    rnb[:, 0:n], rnb[:, 0:n], 1e-24, None, A.max), inc=sv)
                ctx.wait_val("scalar", sv, ctx.sems["v"].n)
                ctx.do("scalar", lambda E, n=n: E.activation(
                    xr[1][:, 0:n], rnb[:, 0:n], ACT.Sqrt), inc=ssc)
                ctx.wait("vector", ssc)
                ctx.do("vector", lambda E, n=n: E.reciprocal_approx_fast(
                    rnb[:, 0:n], xr[1][:, 0:n]), inc=sv)
                ctx.wait("vector", sv)
                return sq_snap

            wdma(WqT, 0)
            wcast(0)
            wdma(WkT, 1)
            wcast(1)
            for h in range(H):
                if h > 0:
                    ctx.wait_val("tensor", sv, snap[("attnv", h - 1)])
                else:
                    ctx.wait_val("tensor", sv, snap[("cast", 0)])
                for c in range(DC):
                    for hf in range(2):
                        ctx.do("tensor", lambda E, h=h, hf=hf, c=c: E.matmul(
                            ps_pr[3 + hf][:], wf_bf[(3 * h) % 2][:, c * 128:(c + 1) * 128],
                            xn[:, c, 512 + hf * 512: 512 + (hf + 1) * 512],
                            start=(c == 0), stop=(c == DC - 1)), inc=st_)
                mmsnap[3 * h] = ctx.sems["t"].n
                wdma(WvT, 3 * h + 2, guard=snap[("cast", 3 * h)])
                ctx.wait_val("tensor", sv,
                             max(snap[("cast", 3 * h + 1)], snap.get(("kn", h - 1), 0)))
                if ("ksq", h - 1) in snap:
                    ctx.wait_val("tensor", ssc, snap[("ksq", h - 1)])
                for c in range(DC):
                    for hf in range(3):
                        ctx.do("tensor", lambda E, h=h, hf=hf, c=c: E.matmul(
                            ps_pr[hf][:], wf_bf[(3 * h + 1) % 2][:, c * 128:(c + 1) * 128],
                            xn[:, c, hf * 512:(hf + 1) * 512],
                            start=(c == 0), stop=(c == DC - 1)), inc=st_)
                mmsnap[3 * h + 1] = ctx.sems["t"].n
                l2chain([ps_pr[3], ps_pr[4]], T, mmsnap[3 * h], None)
                for hf in range(2):
                    ctx.do("vector", lambda E, hf=hf: E.scalar_tensor_tensor(
                        q_h[:, hf * 512:(hf + 1) * 512], ps_pr[3 + hf][:], qs8[:],
                        rnb[:, hf * 512:(hf + 1) * 512], A.mult, A.mult), inc=sv)
                snap[("qn", h)] = ctx.sems["v"].n
                wcast(3 * h + 2, mmguard=mmsnap[3 * h])
                vbanks = [ps_pr[3], ps_pr[4], ps_pv[0]]
                ctx.wait_val("tensor", sv, snap[("cast", 3 * h + 2)])
                for c in range(DC):
                    for hf in range(3):
                        ctx.do("tensor", lambda E, h=h, hf=hf, c=c, ps=vbanks[hf]: E.matmul(
                            ps[:], wf_bf[(3 * h + 2) % 2][:, c * 128:(c + 1) * 128],
                            xn[:, c, hf * 512:(hf + 1) * 512],
                            start=(c == 0), stop=(c == DC - 1)), inc=st_)
                mmsnap[3 * h + 2] = ctx.sems["t"].n
                if h + 1 < H:
                    wdma(WqT, 3 * h + 3, guard=snap[("cast", 3 * h + 1)])
                snap[("ksq", h)] = l2chain(
                    [ps_pr[0], ps_pr[1], ps_pr[2]], EXT, mmsnap[3 * h + 1], snap[("qn", h)])
                for hf in range(3):
                    ctx.do("vector", lambda E, hf=hf: E.scalar_tensor_tensor(
                        k_h[:, hf * 512:(hf + 1) * 512], ps_pr[hf][:], ks[:],
                        rnb[:, hf * 512:(hf + 1) * 512], A.mult, A.mult), inc=sv)
                snap[("kn", h)] = ctx.sems["v"].n
                if h + 1 < H:
                    wcast(3 * h + 3, mmguard=mmsnap[3 * h + 1])
                ctx.wait_val("vector", st_, mmsnap[3 * h + 2])
                for hf in range(3):
                    ctx.do("vector", lambda E, hf=hf, ps=vbanks[hf]: E.tensor_copy(
                        vfm[:, hf * 512:(hf + 1) * 512], ps[:]), inc=sv)
                snap[("vevac", h)] = ctx.sems["v"].n
                if h + 1 < H:
                    wdma(WkT, 3 * h + 4, guard=snap[("cast", 3 * h + 2)])
                prev = snap[("vevac", h)]
                for tc in range(TCE):
                    ctx.wait_val("tensor", sv, prev)
                    ctx.do("tensor", lambda E, tc=tc: E.transpose(
                        ps_tr[:, 0:128], vfm[:, tc * 128:(tc + 1) * 128], ident[:]), inc=st_)
                    ctx.wait("vector", st_)
                    ctx.do("vector", lambda E, tc=tc: E.tensor_copy(
                        v_augh[:, tc, 0:128], ps_tr[:, 0:128]), inc=sv)
                    prev = ctx.sems["v"].n
                if h + 1 < H:
                    wcast(3 * h + 4, mmguard=mmsnap[3 * h + 2])
                for qb in range(2):
                    mask_snap = {}
                    for kc in range(8):
                        if kc >= 2:
                            ctx.wait_val("tensor", sv, mask_snap[kc - 2])
                        ctx.do("tensor", lambda E, h=h, qb=qb, kc=kc: E.matmul(
                            ps_pr[3 + kc % 2][:],
                            k_h[:, qb * 512 + kc * 128: qb * 512 + (kc + 1) * 128],
                            q_h[:, qb * 512:(qb + 1) * 512],
                            start=True, stop=True), inc=st_)
                        ctx.wait("vector", st_, ssc)
                        col = (qb * 8 + kc) * 2
                        ctx.do("vector", lambda E, kc=kc, col=col: E.tensor_mask_reduce(
                            sc_sb[:], ps_pr[3 + kc % 2][:],
                            mse[:, col:col + 1], mse[:, col + 1:col + 2],
                            1.0, 0.0, A.max), inc=sv)
                        mask_snap[kc] = ctx.sems["v"].n
                        ctx.wait_val("scalar", sv, mask_snap[kc])
                        ctx.wait("scalar", st_)
                        ctx.do("scalar", lambda E, kc=kc: E.activation(
                            p_sb[:, kc, :], sc_sb[:], ACT.Exp), inc=ssc)
                    for qc in range(4):
                        ctx.wait("tensor", ssc)
                        for kc in range(8):
                            ctx.do("tensor", lambda E, qb=qb, qc=qc, kc=kc: E.matmul(
                                ps_pv[qc % 2][:, 0:129],
                                p_sb[:, kc, qc * 128:(qc + 1) * 128],
                                v_augh[:, qb * 4 + kc, 0:129],
                                start=(kc == 0), stop=(kc == 7)), inc=st_)
                        ctx.wait("vector", st_)
                        ctx.do("vector", lambda E, qc=qc: E.reciprocal(
                            oscl[:, qc % 2:qc % 2 + 1], ps_pv[qc % 2][:, 128:129]), inc=sv)
                        ctx.wait("vector", sv)
                        tc = qb * 4 + qc
                        ctx.do("vector", lambda E, qc=qc, tc=tc, h=h: E.tensor_tensor(
                            oscl[:, qc % 2:qc % 2 + 1], oscl[:, qc % 2:qc % 2 + 1],
                            gates[:, tc * H + h: tc * H + h + 1], A.mult), inc=sv)
                        ctx.wait("scalar", sv)
                        ctx.do("scalar", lambda E, qc=qc: E.activation(
                            otm_sb[qc % 2][:], ps_pv[qc % 2][:, 0:128], ACT.Copy,
                            scale=oscl[:, qc % 2:qc % 2 + 1]), inc=ssc)
                        ctx.wait("tensor", ssc, sv)
                        ctx.do("tensor", lambda E, qc=qc: E.transpose(
                            ps_tr[:, 128:256], otm_sb[qc % 2][:], ident[:]), inc=st_)
                        ctx.wait("vector", st_)
                        ctx.do("vector", lambda E, h=h, qb=qb, qc=qc: E.tensor_copy(
                            o_fm[:, h, qb * 512 + qc * 128: qb * 512 + (qc + 1) * 128],
                            ps_tr[:, 128:256]), inc=sv)
                snap[("attnv", h)] = ctx.sems["v"].n
            barrier()

            snap_qv = {}
            snap_qs = {}

            def ph5_dma(ic):
                b = ic % 2
                if ic - 2 in snap_qv:
                    ctx.wait_val("sync", sv, snap_qv[ic - 2])
                    ctx.wait_val("sync", ssc, snap_qs[ic - 2])
                ctx.do("sync", lambda E, ic=ic, b=b: E.dma_start(
                    out=wg_st[b][:].rearrange("p (c w) -> p c w", c=DC),
                    in_=wgP.rearrange("(c p) f -> p c f", p=128)[:, :, ic * 128:(ic + 1) * 128]),
                    inc=sx, dma=True)
                ctx.do("sync", lambda E, ic=ic, b=b: E.dma_start(
                    out=wu_st[b][:].rearrange("p (c w) -> p c w", c=DC),
                    in_=wuP.rearrange("(c p) f -> p c f", p=128)[:, :, ic * 128:(ic + 1) * 128]),
                    inc=sx, dma=True)

            def ph5_chain(ic):
                b = ic % 2
                for wi, wst, pp in ((0, wg_st, wsg_pp), (1, wu_st, wsu_pp)):
                    ctx.wait("scalar", sx, sv, sg)
                    ctx.do("scalar", lambda E, b=b, wst=wst: E.activation(
                        wab[:], wst[b][:], ACT.Abs), inc=ssc)
                    ctx.wait("vector", ssc)
                    w_ = DC * 128
                    for step in (w_ // 2, w_ // 4, w_ // 8, w_ // 16):
                        ctx.do("vector", lambda E, step=step: E.tensor_tensor(
                            wab[:, 0:step], wab[:, 0:step], wab[:, step:2 * step],
                            A.max), inc=sv)
                    ctx.wait("gpsimd", sv)
                    ctx.do("gpsimd", lambda E, b=b, wi=wi: E.partition_all_reduce(
                        rug[b][:, 256 + wi * 128:256 + (wi + 1) * 128], wab[:, 0:128],
                        128, RMax), inc=sg)
                snap_qs[ic] = ctx.sems["s"].n
                ctx.wait("scalar", sg)
                ctx.do("scalar", lambda E, ic=ic, b=b: E.dma_start(
                    out=wsg_pp[:, ic:ic + 1], in_=rug[b][0:1, 256:384]), inc=sst, dma=True)
                ctx.do("scalar", lambda E, ic=ic, b=b: E.dma_start(
                    out=wsu_pp[:, ic:ic + 1], in_=rug[b][0:1, 384:512]), inc=sst, dma=True)
                ctx.wait("vector", sg, sst)
                ctx.do("vector", lambda E, ic=ic: E.tensor_scalar(
                    wsg_pp[:, ic:ic + 1], wsg_pp[:, ic:ic + 1], 1.0 / 448.0, None,
                    A.mult), inc=sv)
                ctx.do("vector", lambda E, ic=ic: E.tensor_scalar(
                    wsu_pp[:, ic:ic + 1], wsu_pp[:, ic:ic + 1], 1.0 / 448.0, None,
                    A.mult), inc=sv)
                for wi in range(2):
                    ctx.do("vector", lambda E, b=b, wi=wi: E.reciprocal(
                        rug[b][:, wi * 128:(wi + 1) * 128],
                        rug[b][:, 256 + wi * 128:256 + (wi + 1) * 128]), inc=sv)
                    ctx.wait("vector", sv)
                    ctx.do("vector", lambda E, b=b, wi=wi: E.tensor_scalar(
                        rug[b][:, wi * 128:(wi + 1) * 128],
                        rug[b][:, wi * 128:(wi + 1) * 128], 448.0, None, A.mult), inc=sv)
                for wi, wst, qw in ((0, wg_st, qwg), (1, wu_st, qwu)):
                    _ia, _ib = broadcast_tensor_aps(
                        wst[b][:].rearrange("p (c w) -> p c w", c=DC),
                        rug[b][:, wi * 128:(wi + 1) * 128].rearrange("p c -> p () c"))
                    ctx.do("vector", lambda E, qw=qw, b=b, a=_ia, bb=_ib: E._custom_dve(
                        QCAST, out=qw[b][:], in0=a, in1=bb, s1=0.5), inc=sv)
                snap_qv[ic] = ctx.sems["v"].n

            ph5_dma(0)
            ph5_chain(0)
            for ic in range(IC):
                b = ic % 2
                if ic + 1 < IC:
                    ph5_dma(ic + 1)
                    ph5_chain(ic + 1)
                ctx.wait_val("tensor", sv, snap_qv[ic])
                for c in range(DC):
                    for wi, qw in ((0, qwg), (1, qwu)):
                        for hf in range(2):
                            ctx.do("tensor", lambda E, b=b, qw=qw, wi=wi, hf=hf, c=c: E.matmul(
                                ps_gu[b * 4 + wi * 2 + hf][:], qw[b][:, c, :],
                                qx[:, c, hf * 512:(hf + 1) * 512],
                                start=(c == 0), stop=(c == DC - 1)), inc=st_)
                ctx.wait("vector", st_)
                for hf in range(2):
                    ctx.wait("vector", ssc)
                    ctx.do("vector", lambda E, ic=ic, b=b, hf=hf: E.scalar_tensor_tensor(
                        gu_sb[:, 0:512], ps_gu[b * 4 + hf][:], wsg_pp[:, ic:ic + 1],
                        sxb[:, hf * 512:(hf + 1) * 512], A.mult, A.mult), inc=sv)
                    ctx.wait("scalar", sv)
                    ctx.do("scalar", lambda E, hf=hf: E.activation(
                        sil_sb[hf][:], gu_sb[:, 0:512], ACT.Silu), inc=ssc)
                    ctx.do("vector", lambda E, ic=ic, b=b, hf=hf: E.scalar_tensor_tensor(
                        gu_sb[:, 512:1024], ps_gu[b * 4 + 2 + hf][:], wsu_pp[:, ic:ic + 1],
                        sxb[:, hf * 512:(hf + 1) * 512], A.mult, A.mult), inc=sv)
                    ctx.wait("vector", ssc)
                    ctx.do("vector", lambda E, ic=ic, hf=hf: E.tensor_tensor(
                        t_sb[:, ic, hf * 512:(hf + 1) * 512], sil_sb[hf][:],
                        gu_sb[:, 512:1024], A.mult), inc=sv)
            barrier()

            ctx.do("vector", lambda E: E.tensor_reduce(
                rm_t[:], t_sb[:].rearrange("p c t -> p t c"), AXX, A.max,
                apply_absolute_value=True), inc=sv)
            ctx.wait("gpsimd", sv)
            ctx.do("gpsimd", lambda E: E.partition_all_reduce(rm_t[:], rm_t[:], 128, RMax), inc=sg)
            ctx.wait("vector", sg)
            ctx.do("vector", lambda E: E.reciprocal(rut448[:], rm_t[:]), inc=sv)
            ctx.wait("vector", sv)
            ctx.do("vector", lambda E: E.tensor_scalar(rut448[:], rut448[:], 448.0, None, A.mult), inc=sv)
            ctx.do("vector", lambda E: E.tensor_scalar(sxtb[:], rm_t[:], 4.0 / 448.0, None, A.mult), inc=sv)
            _ta, _tb = broadcast_tensor_aps(
                t_sb[:], rut448[:].rearrange("p t -> p () t"))
            ctx.do("vector", lambda E, a=_ta, b=_tb: E._custom_dve(
                QCAST, out=qt[:], in0=a, in1=b, s1=0.5), inc=sv)
            barrier()

            for ic in range(IC):
                b = ic % 2
                ctx.wait("sync", sv)
                ctx.do("sync", lambda E, ic=ic, b=b: E.dma_start(
                    out=wd_st[b][:], in_=wdT[ic * 128:(ic + 1) * 128, :]), inc=sx, dma=True)
                ctx.wait("vector", sx)
                ctx.do("vector", lambda E, ic=ic, b=b: E.scalar_tensor_tensor(
                    rm_wd[:], wd_st[b][:], -1.0,
                    (wd_st[b][:] if ic == 0 else rm_wd[:]), A.mult, A.max), inc=sv)
                ctx.do("vector", lambda E, b=b: E.scalar_tensor_tensor(
                    rm_wd[:], wd_st[b][:], 1.0, rm_wd[:], A.mult, A.max), inc=sv)
            ctx.wait("gpsimd", sv)
            ctx.do("gpsimd", lambda E: E.partition_all_reduce(rm_wd[:], rm_wd[:], 128, RMax), inc=sg)
            ctx.wait("scalar", sg)
            for oc in range(DC):
                ctx.do("scalar", lambda E, oc=oc: E.dma_start(
                    out=wsd_pp[:, oc:oc + 1], in_=rm_wd[0:1, oc * 128:(oc + 1) * 128]),
                    inc=sst, dma=True)
            ctx.wait("vector", sst, sg)
            ctx.do("vector", lambda E: E.tensor_scalar(
                wsd_pp[:], wsd_pp[:], 1.0 / 448.0, None, A.mult), inc=sv)
            ctx.do("vector", lambda E: E.reciprocal(ruwd448[:], rm_wd[:]), inc=sv)
            ctx.wait("vector", sv)
            ctx.do("vector", lambda E: E.tensor_scalar(ruwd448[:], ruwd448[:], 448.0, None, A.mult), inc=sv)
            barrier()

            for grp in range(2):
                for ic in range(IC):
                    b = ic % 2
                    ctx.wait("sync", sv)
                    ctx.do("sync", lambda E, ic=ic, b=b: E.dma_start(
                        out=wd_st[b][:], in_=wdT[ic * 128:(ic + 1) * 128, :]),
                        inc=sx, dma=True)
                    ctx.wait("vector", sx)
                    ctx.do("vector", lambda E, grp=grp, ic=ic, b=b: E._custom_dve(
                        QCAST, out=qwd[:, ic, :],
                        in0=wd_st[b][:, grp * 1024:(grp + 1) * 1024],
                        in1=ruwd448[:, grp * 1024:(grp + 1) * 1024], s1=0.5), inc=sv)
                for oci in range(8):
                    oc = grp * 8 + oci
                    ctx.wait("sync", sv, sst)
                    ctx.do("sync", lambda E, oc=oc: E.dma_start(
                        out=xo_sb[:], in_=xT[oc * 128:(oc + 1) * 128, 512:1536]),
                        inc=sx, dma=True)
                    ctx.do("sync", lambda E, oc=oc: E.dma_start(
                        out=wo_st[:].rearrange("p (c w) -> p c w", c=H),
                        in_=WoT.rearrange("(c p) f -> p c f", p=128)[:, :, oc * 128:(oc + 1) * 128]),
                        inc=sx, dma=True)
                    ctx.wait("vector", sx)
                    ctx.do("vector", lambda E: E.tensor_copy(wo_bf[:], wo_st[:]), inc=sv)
                    ctx.wait("tensor", sv)
                    for hf in range(2):
                        for ic in range(IC):
                            ctx.do("tensor", lambda E, oci=oci, hf=hf, ic=ic: E.matmul(
                                ps_d[hf][:], qwd[:, ic, oci * 128:(oci + 1) * 128],
                                qt[:, ic, hf * 512:(hf + 1) * 512],
                                start=(ic == 0), stop=(ic == IC - 1)), inc=st_)
                        for h in range(H):
                            ctx.do("tensor", lambda E, hf=hf, h=h: E.matmul(
                                ps_wo[hf][:], wo_bf[:, h * 128:(h + 1) * 128],
                                o_fm[:, h, hf * 512:(hf + 1) * 512],
                                start=(h == 0), stop=(h == H - 1)), inc=st_)
                    ctx.wait("vector", st_, sst)
                    for hf in range(2):
                        sl = slice(hf * 512, (hf + 1) * 512)
                        ctx.do("vector", lambda E, oc=oc, hf=hf, sl=sl: E.scalar_tensor_tensor(
                            out_sb[:, sl], ps_d[hf][:], wsd_pp[:, oc:oc + 1],
                            sxtb[:, sl], A.mult, A.mult), inc=sv)
                        ctx.do("vector", lambda E, hf=hf, sl=sl: E.scalar_tensor_tensor(
                            out_sb[:, sl], ps_wo[hf][:], 1.0, out_sb[:, sl],
                            A.mult, A.add), inc=sv)
                        ctx.do("vector", lambda E, hf=hf, sl=sl: E.tensor_tensor(
                            out_sb[:, sl], out_sb[:, sl], xo_sb[:, sl], A.add), inc=sv)
                    ctx.wait("scalar", sv)
                    ctx.do("scalar", lambda E, oc=oc: E.dma_start(
                        out=out_d[oc * 128:(oc + 1) * 128, :], in_=out_sb[:]),
                        inc=sst, dma=True)
            ctx.wait("scalar", sst)
            ctx.wait("sync", sst)

        @blk.sync
        def _(_e):
            walk(Ctx("sync", nc, sems))

        @blk.scalar
        def _(_e):
            walk(Ctx("scalar", nc, sems))

        @blk.vector
        def _(_e):
            walk(Ctx("vector", nc, sems))

        @blk.tensor
        def _(_e):
            walk(Ctx("tensor", nc, sems))

        @blk.gpsimd
        def _(_e):
            walk(Ctx("gpsimd", nc, sems))

        es5.close()
        es6.close()

    inst_map = {}
    for lib in all_libraries:
        for it in lib.instructions:
            inst_map[it] = inst_map.get(it, 0) | (1 << lib.index)
    _bass_rust.insert_library_loads(nc, inst_map, len(all_libraries), _lib_std.index)
    mybir.codegen_inst_isa_subclasses(nc)
    return nc


def _stage_inputs(inputs):
    f32 = np.float32
    x = np.ascontiguousarray(np.asarray(inputs["L0_attn_x"], dtype=f32)).reshape(B * N, D)
    Wq = np.asarray(inputs["L0_attn_Wq"], dtype=f32)
    Wkv = np.asarray(inputs["L0_attn_Wkv"], dtype=f32)
    Wg = np.asarray(inputs["L0_attn_Wg"], dtype=f32)
    bg = np.asarray(inputs["L0_attn_bg"], dtype=f32)
    Wo = np.asarray(inputs["L0_attn_Wo"], dtype=f32)
    w_gate = np.asarray(inputs["L0_ffn_w_gate"], dtype=f32)
    w_up = np.asarray(inputs["L0_ffn_w_up"], dtype=f32)
    w_down = np.asarray(inputs["L0_ffn_w_down"], dtype=f32)
    perm = np.asarray(inputs["L0_ffn_down_reorder_index"]).astype(np.int64)
    q_scale = np.asarray(inputs["L0_attn_q_scale"], dtype=f32)
    k_scale = np.asarray(inputs["L0_attn_k_scale"], dtype=f32)

    shared = {
        "WqT": np.ascontiguousarray(Wq.T),
        "WkT": np.ascontiguousarray(Wkv[:H * DH].T),
        "WvT": np.ascontiguousarray(Wkv[H * DH:].T),
        "WgT": np.ascontiguousarray(Wg.T),
        "WoT": np.ascontiguousarray(Wo.T),
        "wgP": np.ascontiguousarray(w_gate[perm].T),
        "wuP": np.ascontiguousarray(w_up[perm].T),
        "wdT": np.ascontiguousarray(w_down.T),
        "qs": np.ascontiguousarray(q_scale.reshape(128, 1)),
        "ks": np.ascontiguousarray(k_scale.reshape(128, 1)),
        "bgb": np.ascontiguousarray(np.tile(bg[None, :], (128, 1))),
        "ident": np.eye(128, dtype=f32),
    }
    in_maps = []
    for c in range(NCORES):
        r0 = c * T
        own = x[r0:r0 + T]
        halo = np.zeros((512, D), dtype=f32) if c % 4 == 0 else x[r0 - 512:r0]
        xTc = np.ascontiguousarray(np.concatenate([halo, own], axis=0).T)
        msk = np.zeros((128, 32), dtype=f32)
        p = np.arange(128, dtype=f32)
        for qb in range(2):
            for kc in range(8):
                j = kc * 128 + p
                start = j - 512.0
                end = j + 1.0
                if (c % 4 == 0) and qb == 0 and kc < 4:
                    start = np.full(128, 512.0, dtype=f32)
                    end = np.full(128, 513.0, dtype=f32)
                col = (qb * 8 + kc) * 2
                msk[:, col] = start
                msk[:, col + 1] = end
        m = dict(shared)
        m["xT"] = xTc
        m["mse"] = msk
        in_maps.append(m)
    return in_maps


def kernel(**inputs):
    in_maps = _stage_inputs(inputs)
    nc = build_nc()
    res = run_bass_kernel_spmd(nc, in_maps, core_ids=list(range(NCORES)))
    out = np.empty((B * N, D), dtype=np.float32)
    for c in range(NCORES):
        out[c * T:(c + 1) * T, :] = np.asarray(res.results[c]["out"]).T
    return out.reshape(B, N, D)


# version 08
# speedup vs baseline: 3.9886x; 1.2371x over previous
import numpy as np
import ml_dtypes
import concourse.bass as bass
import concourse.mybir as mybir
from concourse.bass_utils import run_bass_kernel_spmd
from concourse import dve_ops
from concourse.dve_op import DveOp
from concourse.dve_spec import Spec, Src0, Src1, C1, C2, lower
from concourse.dve_table_gen import dve_ver_for
import concourse.bass_isa as bass_isa
import bass_rust as _bass_rust
from concourse.library_config import all_libraries, standard as _lib_std
from contextlib import ExitStack
from concourse.bass import broadcast_tensor_aps

F32 = mybir.dt.float32
BF16 = mybir.dt.bfloat16
FP8 = mybir.dt.float8e4
A = mybir.AluOpType
ACT = mybir.ActivationFunctionType
AXX = mybir.AxisListType.X
RMax = bass_isa.ReduceOp.max

NCORES = 8
B, N, D = 2, 4096, 2048
H, DH = 8, 128
I = 5632
T = 1024
EXT = 1536
DC = D // 128
IC = I // 128
TC = T // 128
TCE = EXT // 128
VA = 132
LN_EPS = 1e-5
ENGINES = ("sync", "scalar", "vector", "tensor", "gpsimd")


def _qcast_ref(in0, in1, s0, s1, imm2):
    v = np.asarray(in0) * np.asarray(in1)
    q = v.astype(ml_dtypes.float8_e4m3fn).astype(v.dtype)
    return q * (np.asarray(s1) if s1 is not None else 1.0)


def _register_qcast():
    for op in dve_ops.OPS:
        if op.name == "QCAST":
            return op
    spec = Spec(body=(Src0 * Src1) * C1)
    ver = dve_ver_for("TRN2")
    sha = lower(spec, ver=ver, name="QCAST").sha(ver)
    op = DveOp("QCAST", spec, subdim=False, uops_sha={ver: sha}, reference=_qcast_ref)
    dve_ops.OPS.append(op)
    dve_ops.CUSTOM_DVE_SPECS["QCAST"] = spec
    dve_ops.CUSTOM_DVE_REFS["QCAST"] = _qcast_ref
    dve_ops._SUB_OPCODE_FOR_NAME["QCAST"] = dve_ops._CUSTOM_DVE_ROW_BASE + len(dve_ops.OPS) - 1
    return op


QCAST = _register_qcast()


class _Sem:
    __slots__ = ("h", "n")
    def __init__(self, h):
        self.h = h
        self.n = 0


SERIAL = False

class Ctx:
    def __init__(self, eng_name, nc, sems):
        self.eng = eng_name
        self.nc = nc
        self.sems = sems
        for s in sems.values():
            s.n = 0

    def _E(self):
        return getattr(self.nc, self.eng)

    def do(self, eng, fn, inc=None, dma=False):
        if self.eng == eng:
            E = getattr(self.nc, eng)
            if SERIAL:
                for s in self.sems.values():
                    if s.n > 0:
                        E.wait_ge(s.h, s.n)
            ins = fn(E)
            if inc is not None:
                ins.then_inc(inc.h, 16 if dma else 1)
        if inc is not None:
            inc.n += 16 if dma else 1

    def wait(self, eng, *sems):
        if self.eng == eng:
            for s in sems:
                getattr(self.nc, eng).wait_ge(s.h, s.n)

    def wait_val(self, eng, sem, val):
        if self.eng == eng and val > 0:
            getattr(self.nc, eng).wait_ge(sem.h, val)


def build_nc():
    nc = bass.Bass("TRN2", target_bir_lowering=False)
    din = lambda name, shape, dt=F32: nc.dram_tensor(name, shape, dt, kind="ExternalInput").ap()
    xT = din("xT", [D, EXT])
    WqT = din("WqT", [D, H * DH])
    WkT = din("WkT", [D, H * DH])
    WvT = din("WvT", [D, H * DH])
    WgT = din("WgT", [D, H])
    WoT = din("WoT", [H * DH, D])
    wgP = din("wgP", [D, I])
    wuP = din("wuP", [D, I])
    wdT = din("wdT", [I, D])
    qs_i = din("qs", [128, 1])
    ks_i = din("ks", [128, 1])
    bgb_i = din("bgb", [128, H])
    mse_i = din("mse", [128, 32])
    id_i = din("ident", [128, 128])
    out_d = nc.dram_tensor("out", [D, T], F32, kind="ExternalOutput").ap()

    es = ExitStack()
    with es:
        SB = lambda st, name, shape, dt=F32: st.enter_context(nc.sbuf_tensor(name, shape, dt))
        SBR = lambda st, name, shape, dt=F32: st.enter_context(nc.sbuf_tensor(name, shape, dt, side="right"))
        PS = lambda st, name, shape, dt=F32: st.enter_context(nc.psum_tensor(name, shape, dt))
        qs = SB(es, "qs_t", [128, 1]); ks = SB(es, "ks_t", [128, 1])
        qs8 = SB(es, "qs8_t", [128, 1])
        bgb = SB(es, "bgb_t", [128, H]); mse = SB(es, "mse_t", [128, 32])
        ident = SB(es, "ident_b", [128, 128], BF16)
        ones = SB(es, "ones_t", [128, 1])
        gates = SB(es, "gates_t", [128, TC * H])
        oscl = SB(es, "oscl", [128, 2])
        wsg_pp = SB(es, "wsg_pp", [128, IC]); wsu_pp = SB(es, "wsu_pp", [128, IC])
        wsd_pp = SB(es, "wsd_pp", [128, DC])
        o_fm = SB(es, "o_fm", [128, H, T], BF16)
        rm_wd = SB(es, "rm_wd", [128, D])
        wfold = [SB(es, f"wfold{i}", [128, T]) for i in range(2)]

        sems = {k: _Sem(es.enter_context(nc.semaphore("sem_" + k)))
                for k in ("x", "v", "s", "t", "g", "st", "ph")}

        es1 = ExitStack()
        mb = SB(es1, "mb", [128, EXT]); rb = SB(es1, "rb", [128, EXT])
        ps_a1 = [PS(es1, f"ps_a1{i}", [1, 512]) for i in range(3)]
        ps_b1 = [PS(es1, f"ps_b1{i}", [1, 512]) for i in range(3)]
        esX = ExitStack()
        rows = SBR(esX, "rows", [1, 4 * EXT])
        xr = [SBR(esX, f"xr{i}", [128, EXT]) for i in range(3)]
        es2 = ExitStack()
        xn = SBR(es2, "xn", [128, DC, EXT], BF16)
        rux448 = SBR(es2, "rux448", [128, T])
        rm_x = SBR(es2, "rm_x", [128, T])
        es3 = ExitStack()
        q_h = SBR(es3, "q_h", [128, T], BF16)
        k_h = SBR(es3, "k_h", [128, EXT], BF16)
        v_augh = SBR(es3, "v_augh", [128, TCE, VA], BF16)
        wf_st = [SBR(es3, f"wf_st{i}", [128, DC * 128]) for i in range(2)]
        wf_bf = [SBR(es3, f"wf_bf{i}", [128, DC * 128], BF16) for i in range(2)]
        rnb = SBR(es3, "rnb", [128, EXT])
        p_sb = SBR(es3, "p_sb", [128, 8, 512], BF16)
        sc_sb = SBR(es3, "sc_sb", [128, 512])
        vfm = SBR(es3, "vfm", [128, EXT], BF16)
        otm_sb = [SBR(es3, f"otm{i}", [128, 128], BF16) for i in range(2)]
        es1.close()
        ps_pr = [PS(es3, f"ps_pr{i}", [128, 512]) for i in range(5)]
        ps_tr = PS(es3, "ps_tr", [128, 1024], BF16)
        ps_pv = [PS(es3, f"ps_pv{i}", [128, 512]) for i in range(2)]
        es2b = ExitStack()
        qx = SB(es2b, "qx", [128, DC, T], FP8)
        sxb = SB(es2b, "sxb", [128, T])
        es3.close()
        es2.close()
        esX.close()
        es4 = ExitStack()
        t_sb = SBR(es4, "t_sb", [128, IC, T], BF16)
        es4a = ExitStack()
        wg_st = [SBR(es4a, f"wg_st{i}", [128, DC * 128]) for i in range(2)]
        wu_st = [SBR(es4a, f"wu_st{i}", [128, DC * 128]) for i in range(2)]
        qwg = [SBR(es4a, f"qwg{i}", [128, DC, 128], FP8) for i in range(2)]
        qwu = [SBR(es4a, f"qwu{i}", [128, DC, 128], FP8) for i in range(2)]
        gu_sb = SBR(es4a, "gu_sb", [128, 1024])
        sil_sb = [SBR(es4a, f"sil_sb{i}", [128, 512]) for i in range(2)]
        wab = SBR(es4a, "wab", [128, DC * 128])
        rug = [SBR(es4a, f"rug{i}", [128, 512]) for i in range(2)]
        ps_gu = [PS(es4a, f"ps_gu{i}", [128, 512]) for i in range(8)]
        es5 = ExitStack()
        es2b.close()
        es4a.close()
        qt = SB(es5, "qt", [128, IC, T], FP8)
        sxtb = SB(es5, "sxtb", [128, T])
        rm_t = sxtb
        rut448 = wfold[0]
        es4.close()
        es6 = ExitStack()
        qwd = SBR(es6, "qwd", [128, IC, D], FP8)
        wo_bf = SBR(es6, "wo_bf", [128, H * 128], BF16)
        out_sb = SBR(es6, "out_sb", [128, T])
        ps7 = [PS(es6, f"ps7_{i}", [128, 512]) for i in range(8)]

        blk = es.enter_context(nc.Block())

        def walk(ctx: Ctx):
            sx, sv, ssc, st_, sg, sst, sph = (ctx.sems[k] for k in
                                              ("x", "v", "s", "t", "g", "st", "ph"))
            phase_n = [0]

            def barrier():
                phase_n[0] += 1
                tgt = 5 * phase_n[0]
                for e in ENGINES:
                    ctx.do(e, lambda E: E.nop(), inc=sph)
                if ctx.eng is not None:
                    getattr(ctx.nc, ctx.eng).wait_ge(sph.h, tgt)

            m_row = rows[:, 0:EXT]
            e2_row = rows[:, EXT:2 * EXT]
            w_row = rows[:, 2 * EXT:3 * EXT]
            r_row = rows[:, 3 * EXT:4 * EXT]

            for src, dst in ((qs_i, qs), (ks_i, ks), (bgb_i, bgb), (mse_i, mse)):
                ctx.do("sync", lambda E, s=src, d=dst: E.dma_start(out=d[:], in_=s),
                       inc=sx, dma=True)
            ctx.do("sync", lambda E: E.dma_start(out=xr[0][:, 0:128], in_=id_i),
                   inc=sx, dma=True)
            ctx.do("gpsimd", lambda E: E.memset(ones[:], 1.0), inc=sg)
            ctx.do("gpsimd", lambda E: E.memset(v_augh[:], 0.0), inc=sg)
            ctx.do("gpsimd", lambda E: E.memset(v_augh[:, :, 128:129], 1.0), inc=sg)
            ctx.wait("vector", sx)
            ctx.do("vector", lambda E: E.tensor_copy(ident[:], xr[0][:, 0:128]), inc=sv)
            ctx.do("vector", lambda E: E.tensor_scalar(qs8[:], qs[:], 8.0, None, A.mult), inc=sv)
            barrier()

            for c in range(DC):
                b = c % 2
                ctx.wait("sync", st_)
                ctx.do("sync", lambda E, c=c, b=b: E.dma_start(
                    out=xr[b][:], in_=xT[c * 128:(c + 1) * 128, :]), inc=sx, dma=True)
                ctx.wait("scalar", sx, st_)
                ctx.do("scalar", lambda E, b=b: E.activation(
                    xr[2][:], xr[b][:], ACT.Square), inc=ssc)
                ctx.wait("tensor", sx, ssc)
                for h3 in range(3):
                    ctx.do("tensor", lambda E, c=c, b=b, h3=h3: E.matmul(
                        ps_a1[h3][:], ones[:], xr[b][:, h3 * 512:(h3 + 1) * 512],
                        start=(c == 0), stop=(c == DC - 1)), inc=st_)
                    ctx.do("tensor", lambda E, c=c, h3=h3: E.matmul(
                        ps_b1[h3][:], ones[:], xr[2][:, h3 * 512:(h3 + 1) * 512],
                        start=(c == 0), stop=(c == DC - 1)), inc=st_)
            ctx.wait("vector", st_)
            for h3 in range(3):
                ctx.do("vector", lambda E, h3=h3: E.tensor_scalar(
                    m_row[:, h3 * 512:(h3 + 1) * 512], ps_a1[h3][:], 1.0 / D, None,
                    A.mult), inc=sv)
                ctx.do("vector", lambda E, h3=h3: E.tensor_scalar(
                    e2_row[:, h3 * 512:(h3 + 1) * 512], ps_b1[h3][:], 1.0 / D, None,
                    A.mult), inc=sv)
            ctx.do("vector", lambda E: E.scalar_tensor_tensor(
                w_row, m_row, -1.0, m_row, A.mult, A.mult), inc=sv)
            ctx.do("vector", lambda E: E.tensor_tensor(w_row, e2_row, w_row, A.add), inc=sv)
            ctx.do("vector", lambda E: E.tensor_scalar(w_row, w_row, LN_EPS, None, A.add), inc=sv)
            ctx.wait("scalar", sv)
            ctx.do("scalar", lambda E: E.activation(r_row, w_row, ACT.Sqrt), inc=ssc)
            ctx.wait("vector", ssc)
            ctx.do("vector", lambda E: E.reciprocal(e2_row, r_row), inc=sv)
            ctx.wait("vector", sv)
            ctx.do("vector", lambda E: E.tensor_tensor(r_row, e2_row, e2_row, A.mult), inc=sv)
            ctx.do("vector", lambda E: E.tensor_tensor(r_row, w_row, r_row, A.mult), inc=sv)
            ctx.do("vector", lambda E: E.tensor_scalar(r_row, r_row, -0.5, 1.5, A.mult, A.add), inc=sv)
            ctx.do("vector", lambda E: E.tensor_tensor(r_row, e2_row, r_row, A.mult), inc=sv)
            ctx.wait("gpsimd", sv)
            ctx.do("gpsimd", lambda E: E.partition_broadcast(mb[:], m_row, channels=128), inc=sg)
            ctx.do("gpsimd", lambda E: E.partition_broadcast(rb[:], r_row, channels=128), inc=sg)
            barrier()

            for c in range(DC):
                b = c % 2
                ctx.wait("sync", sv)
                ctx.do("sync", lambda E, c=c, b=b: E.dma_start(
                    out=xr[b][:], in_=xT[c * 128:(c + 1) * 128, :]), inc=sx, dma=True)
                ctx.wait("vector", sx, sg)
                ctx.do("vector", lambda E, b=b: E.scalar_tensor_tensor(
                    xr[2][:], mb[:], -1.0, xr[b][:], A.mult, A.add), inc=sv)
                ctx.do("vector", lambda E, c=c: E.tensor_tensor(
                    xn[:, c, :], xr[2][:], rb[:], A.mult), inc=sv)
            barrier()

            ctx.do("sync", lambda E: E.dma_start(
                out=wf_st[0][:].rearrange("p (c w) -> p c w", c=DC)[:, :, 0:H],
                in_=WgT.rearrange("(c p) f -> p c f", p=128)), inc=sx, dma=True)
            ctx.wait("vector", sx)
            ctx.do("vector", lambda E: E.tensor_copy(
                wf_bf[0][:].rearrange("p (c w) -> p c w", c=DC)[:, :, 0:H],
                wf_st[0][:].rearrange("p (c w) -> p c w", c=DC)[:, :, 0:H]), inc=sv)
            for tc in range(TC):
                ctx.wait("tensor", sv)
                for c in range(DC):
                    ctx.do("tensor", lambda E, tc=tc, c=c: E.matmul(
                        ps_pr[3][:, 0:H],
                        xn[:, c, 512 + tc * 128: 512 + (tc + 1) * 128],
                        wf_bf[0][:].rearrange("p (c w) -> p c w", c=DC)[:, c, 0:H],
                        start=(c == 0), stop=(c == DC - 1)), inc=st_)
                ctx.wait("vector", st_, ssc)
                ctx.do("vector", lambda E: E.tensor_tensor(
                    xr[2][:, 0:H], ps_pr[3][:, 0:H], bgb[:], A.add), inc=sv)
                ctx.wait("scalar", sv)
                ctx.do("scalar", lambda E, tc=tc: E.activation(
                    gates[:, tc * H:(tc + 1) * H], xr[2][:, 0:H], ACT.Sigmoid), inc=ssc)
            ctx.do("vector", lambda E: E.tensor_reduce(
                rm_x[:], xn[:, :, 512:1536].rearrange("p c t -> p t c"), AXX, A.max,
                apply_absolute_value=True), inc=sv)
            ctx.wait("gpsimd", sv)
            ctx.do("gpsimd", lambda E: E.partition_all_reduce(rm_x[:], rm_x[:], 128, RMax), inc=sg)
            ctx.wait("vector", sg)
            ctx.do("vector", lambda E: E.reciprocal(rux448[:], rm_x[:]), inc=sv)
            ctx.wait("vector", sv)
            ctx.do("vector", lambda E: E.tensor_scalar(rux448[:], rux448[:], 448.0, None, A.mult), inc=sv)
            ctx.do("vector", lambda E: E.tensor_scalar(sxb[:], rm_x[:], 4.0 / 448.0, None, A.mult), inc=sv)
            _qa, _qb = broadcast_tensor_aps(
                xn[:, :, 512:1536], rux448[:].rearrange("p t -> p () t"))
            ctx.do("vector", lambda E, a=_qa, b=_qb: E._custom_dve(
                QCAST, out=qx[:], in0=a, in1=b, s1=0.5), inc=sv)
            barrier()

            RAdd = bass_isa.ReduceOp.add
            snap = {}
            mmsnap = {}

            def wdma(src_ap, li, guard=None):
                b = li % 2
                if guard is not None:
                    ctx.wait_val("sync", sv, guard)
                ctx.do("sync", lambda E, src=src_ap, li=li, b=b: E.dma_start(
                    out=wf_st[b][:].rearrange("p (c w) -> p c w", c=DC),
                    in_=src.rearrange("(c p) f -> p c f", p=128)[:, :, (li // 3) * 128:(li // 3 + 1) * 128]),
                    inc=sx, dma=True)

            def wcast(li, mmguard=None):
                b = li % 2
                ctx.wait("vector", sx)
                if mmguard is not None:
                    ctx.wait_val("vector", st_, mmguard)
                ctx.do("vector", lambda E, b=b: E.tensor_copy(wf_bf[b][:], wf_st[b][:]), inc=sv)
                snap[("cast", li)] = ctx.sems["v"].n

            def l2chain(psums, n, mm_snap, xr2_guard):
                ctx.wait_val("scalar", st_, mm_snap)
                if xr2_guard:
                    ctx.wait_val("scalar", sv, xr2_guard)
                ctx.wait("scalar", sg)
                for hf in range(len(psums)):
                    ctx.do("scalar", lambda E, hf=hf, ps=psums[hf]: E.activation(
                        xr[2][:, hf * 512:(hf + 1) * 512], ps[:], ACT.Square), inc=ssc)
                sq_snap = ctx.sems["s"].n
                ctx.wait("gpsimd", ssc, sv)
                ctx.do("gpsimd", lambda E, n=n: E.partition_all_reduce(
                    rnb[:, 0:n], xr[2][:, 0:n], 128, RAdd), inc=sg)
                ctx.wait("vector", sg)
                ctx.do("vector", lambda E, n=n: E.tensor_scalar(
                    rnb[:, 0:n], rnb[:, 0:n], 1e-24, None, A.max), inc=sv)
                ctx.wait_val("scalar", sv, ctx.sems["v"].n)
                ctx.do("scalar", lambda E, n=n: E.activation(
                    xr[1][:, 0:n], rnb[:, 0:n], ACT.Sqrt), inc=ssc)
                ctx.wait("vector", ssc)
                ctx.do("vector", lambda E, n=n: E.reciprocal_approx_fast(
                    rnb[:, 0:n], xr[1][:, 0:n]), inc=sv)
                ctx.wait("vector", sv)
                return sq_snap

            wdma(WqT, 0)
            wcast(0)
            wdma(WkT, 1)
            wcast(1)
            for h in range(H):
                if h > 0:
                    ctx.wait_val("tensor", sv, snap[("attnv", h - 1)])
                else:
                    ctx.wait_val("tensor", sv, snap[("cast", 0)])
                for c in range(DC):
                    for hf in range(2):
                        ctx.do("tensor", lambda E, h=h, hf=hf, c=c: E.matmul(
                            ps_pr[3 + hf][:], wf_bf[(3 * h) % 2][:, c * 128:(c + 1) * 128],
                            xn[:, c, 512 + hf * 512: 512 + (hf + 1) * 512],
                            start=(c == 0), stop=(c == DC - 1)), inc=st_)
                mmsnap[3 * h] = ctx.sems["t"].n
                wdma(WvT, 3 * h + 2, guard=snap[("cast", 3 * h)])
                ctx.wait_val("tensor", sv,
                             max(snap[("cast", 3 * h + 1)], snap.get(("kn", h - 1), 0)))
                if ("ksq", h - 1) in snap:
                    ctx.wait_val("tensor", ssc, snap[("ksq", h - 1)])
                for c in range(DC):
                    for hf in range(3):
                        ctx.do("tensor", lambda E, h=h, hf=hf, c=c: E.matmul(
                            ps_pr[hf][:], wf_bf[(3 * h + 1) % 2][:, c * 128:(c + 1) * 128],
                            xn[:, c, hf * 512:(hf + 1) * 512],
                            start=(c == 0), stop=(c == DC - 1)), inc=st_)
                mmsnap[3 * h + 1] = ctx.sems["t"].n
                l2chain([ps_pr[3], ps_pr[4]], T, mmsnap[3 * h], None)
                for hf in range(2):
                    ctx.do("vector", lambda E, hf=hf: E.scalar_tensor_tensor(
                        q_h[:, hf * 512:(hf + 1) * 512], ps_pr[3 + hf][:], qs8[:],
                        rnb[:, hf * 512:(hf + 1) * 512], A.mult, A.mult), inc=sv)
                snap[("qn", h)] = ctx.sems["v"].n
                wcast(3 * h + 2, mmguard=mmsnap[3 * h])
                vbanks = [ps_pr[3], ps_pr[4], ps_pv[0]]
                ctx.wait_val("tensor", sv, snap[("cast", 3 * h + 2)])
                for c in range(DC):
                    for hf in range(3):
                        ctx.do("tensor", lambda E, h=h, hf=hf, c=c, ps=vbanks[hf]: E.matmul(
                            ps[:], wf_bf[(3 * h + 2) % 2][:, c * 128:(c + 1) * 128],
                            xn[:, c, hf * 512:(hf + 1) * 512],
                            start=(c == 0), stop=(c == DC - 1)), inc=st_)
                mmsnap[3 * h + 2] = ctx.sems["t"].n
                if h + 1 < H:
                    wdma(WqT, 3 * h + 3, guard=snap[("cast", 3 * h + 1)])
                snap[("ksq", h)] = l2chain(
                    [ps_pr[0], ps_pr[1], ps_pr[2]], EXT, mmsnap[3 * h + 1], snap[("qn", h)])
                for hf in range(3):
                    ctx.do("vector", lambda E, hf=hf: E.scalar_tensor_tensor(
                        k_h[:, hf * 512:(hf + 1) * 512], ps_pr[hf][:], ks[:],
                        rnb[:, hf * 512:(hf + 1) * 512], A.mult, A.mult), inc=sv)
                snap[("kn", h)] = ctx.sems["v"].n
                if h + 1 < H:
                    wcast(3 * h + 3, mmguard=mmsnap[3 * h + 1])
                ctx.wait_val("vector", st_, mmsnap[3 * h + 2])
                for hf in range(3):
                    ctx.do("vector", lambda E, hf=hf, ps=vbanks[hf]: E.tensor_copy(
                        vfm[:, hf * 512:(hf + 1) * 512], ps[:]), inc=sv)
                snap[("vevac", h)] = ctx.sems["v"].n
                if h + 1 < H:
                    wdma(WkT, 3 * h + 4, guard=snap[("cast", 3 * h + 2)])
                prev = snap[("vevac", h)]
                for tc in range(TCE):
                    ctx.wait_val("tensor", sv, prev)
                    ctx.do("tensor", lambda E, tc=tc: E.transpose(
                        ps_tr[:, 0:128], vfm[:, tc * 128:(tc + 1) * 128], ident[:]), inc=st_)
                    ctx.wait("vector", st_)
                    ctx.do("vector", lambda E, tc=tc: E.tensor_copy(
                        v_augh[:, tc, 0:128], ps_tr[:, 0:128]), inc=sv)
                    prev = ctx.sems["v"].n
                if h + 1 < H:
                    wcast(3 * h + 4, mmguard=mmsnap[3 * h + 2])
                for qb in range(2):
                    mask_snap = {}
                    for kc in range(8):
                        if kc >= 2:
                            ctx.wait_val("tensor", sv, mask_snap[kc - 2])
                        ctx.do("tensor", lambda E, h=h, qb=qb, kc=kc: E.matmul(
                            ps_pr[3 + kc % 2][:],
                            k_h[:, qb * 512 + kc * 128: qb * 512 + (kc + 1) * 128],
                            q_h[:, qb * 512:(qb + 1) * 512],
                            start=True, stop=True), inc=st_)
                        ctx.wait("vector", st_, ssc)
                        col = (qb * 8 + kc) * 2
                        ctx.do("vector", lambda E, kc=kc, col=col: E.tensor_mask_reduce(
                            sc_sb[:], ps_pr[3 + kc % 2][:],
                            mse[:, col:col + 1], mse[:, col + 1:col + 2],
                            1.0, 0.0, A.max), inc=sv)
                        mask_snap[kc] = ctx.sems["v"].n
                        ctx.wait_val("scalar", sv, mask_snap[kc])
                        ctx.wait("scalar", st_)
                        ctx.do("scalar", lambda E, kc=kc: E.activation(
                            p_sb[:, kc, :], sc_sb[:], ACT.Exp), inc=ssc)
                    for qc in range(4):
                        ctx.wait("tensor", ssc)
                        for kc in range(8):
                            ctx.do("tensor", lambda E, qb=qb, qc=qc, kc=kc: E.matmul(
                                ps_pv[qc % 2][:, 0:129],
                                p_sb[:, kc, qc * 128:(qc + 1) * 128],
                                v_augh[:, qb * 4 + kc, 0:129],
                                start=(kc == 0), stop=(kc == 7)), inc=st_)
                        ctx.wait("vector", st_)
                        ctx.do("vector", lambda E, qc=qc: E.reciprocal(
                            oscl[:, qc % 2:qc % 2 + 1], ps_pv[qc % 2][:, 128:129]), inc=sv)
                        ctx.wait("vector", sv)
                        tc = qb * 4 + qc
                        ctx.do("vector", lambda E, qc=qc, tc=tc, h=h: E.tensor_tensor(
                            oscl[:, qc % 2:qc % 2 + 1], oscl[:, qc % 2:qc % 2 + 1],
                            gates[:, tc * H + h: tc * H + h + 1], A.mult), inc=sv)
                        ctx.wait("scalar", sv)
                        ctx.do("scalar", lambda E, qc=qc: E.activation(
                            otm_sb[qc % 2][:], ps_pv[qc % 2][:, 0:128], ACT.Copy,
                            scale=oscl[:, qc % 2:qc % 2 + 1]), inc=ssc)
                        ctx.wait("tensor", ssc, sv)
                        ctx.do("tensor", lambda E, qc=qc: E.transpose(
                            ps_tr[:, 128:256], otm_sb[qc % 2][:], ident[:]), inc=st_)
                        ctx.wait("vector", st_)
                        ctx.do("vector", lambda E, h=h, qb=qb, qc=qc: E.tensor_copy(
                            o_fm[:, h, qb * 512 + qc * 128: qb * 512 + (qc + 1) * 128],
                            ps_tr[:, 128:256]), inc=sv)
                snap[("attnv", h)] = ctx.sems["v"].n
                for j in range(h * 11, h * 11 + 11):
                    ic, half = j // 2, j % 2
                    if j >= 2:
                        ctx.wait_val("gpsimd", sv, snap[("wfold", j - 2)])
                    ctx.do("gpsimd", lambda E, ic=ic, half=half, j=j: E.dma_start(
                        out=wfold[j % 2][:],
                        in_=wdT[ic * 128:(ic + 1) * 128, half * 1024:(half + 1) * 1024]),
                        inc=sst, dma=True)
                    snap[("wddma", j)] = ctx.sems["st"].n
                    ctx.wait_val("vector", sst, snap[("wddma", j)])
                    rms = rm_wd[:, half * 1024:(half + 1) * 1024]
                    if j < 2:
                        ctx.do("vector", lambda E, j=j, rms=rms: E.scalar_tensor_tensor(
                            rms, wfold[j % 2][:], -1.0, wfold[j % 2][:],
                            A.mult, A.max), inc=sv)
                    else:
                        ctx.do("vector", lambda E, j=j, rms=rms: E.scalar_tensor_tensor(
                            rms, wfold[j % 2][:], -1.0, rms, A.mult, A.max), inc=sv)
                        ctx.do("vector", lambda E, j=j, rms=rms: E.scalar_tensor_tensor(
                            rms, wfold[j % 2][:], 1.0, rms, A.mult, A.max), inc=sv)
                    snap[("wfold", j)] = ctx.sems["v"].n
            barrier()

            snap_qv = {}
            snap_qs = {}

            def ph5_dma(ic):
                b = ic % 2
                if ic - 2 in snap_qv:
                    ctx.wait_val("sync", sv, snap_qv[ic - 2])
                    ctx.wait_val("sync", ssc, snap_qs[ic - 2])
                ctx.do("sync", lambda E, ic=ic, b=b: E.dma_start(
                    out=wg_st[b][:].rearrange("p (c w) -> p c w", c=DC),
                    in_=wgP.rearrange("(c p) f -> p c f", p=128)[:, :, ic * 128:(ic + 1) * 128]),
                    inc=sx, dma=True)
                ctx.do("sync", lambda E, ic=ic, b=b: E.dma_start(
                    out=wu_st[b][:].rearrange("p (c w) -> p c w", c=DC),
                    in_=wuP.rearrange("(c p) f -> p c f", p=128)[:, :, ic * 128:(ic + 1) * 128]),
                    inc=sx, dma=True)

            def ph5_chain(ic):
                b = ic % 2
                for wi, wst, pp in ((0, wg_st, wsg_pp), (1, wu_st, wsu_pp)):
                    ctx.wait("scalar", sx, sv, sg)
                    ctx.do("scalar", lambda E, b=b, wst=wst: E.activation(
                        wab[:], wst[b][:], ACT.Abs), inc=ssc)
                    ctx.wait("vector", ssc)
                    w_ = DC * 128
                    for step in (w_ // 2, w_ // 4, w_ // 8, w_ // 16):
                        ctx.do("vector", lambda E, step=step: E.tensor_tensor(
                            wab[:, 0:step], wab[:, 0:step], wab[:, step:2 * step],
                            A.max), inc=sv)
                    ctx.wait("gpsimd", sv)
                    ctx.do("gpsimd", lambda E, b=b, wi=wi: E.partition_all_reduce(
                        rug[b][:, 256 + wi * 128:256 + (wi + 1) * 128], wab[:, 0:128],
                        128, RMax), inc=sg)
                snap_qs[ic] = ctx.sems["s"].n
                ctx.wait("scalar", sg)
                ctx.do("scalar", lambda E, ic=ic, b=b: E.dma_start(
                    out=wsg_pp[:, ic:ic + 1], in_=rug[b][0:1, 256:384]), inc=sst, dma=True)
                ctx.do("scalar", lambda E, ic=ic, b=b: E.dma_start(
                    out=wsu_pp[:, ic:ic + 1], in_=rug[b][0:1, 384:512]), inc=sst, dma=True)
                ctx.wait("vector", sg, sst)
                ctx.do("vector", lambda E, ic=ic: E.tensor_scalar(
                    wsg_pp[:, ic:ic + 1], wsg_pp[:, ic:ic + 1], 1.0 / 448.0, None,
                    A.mult), inc=sv)
                ctx.do("vector", lambda E, ic=ic: E.tensor_scalar(
                    wsu_pp[:, ic:ic + 1], wsu_pp[:, ic:ic + 1], 1.0 / 448.0, None,
                    A.mult), inc=sv)
                for wi in range(2):
                    ctx.do("vector", lambda E, b=b, wi=wi: E.reciprocal(
                        rug[b][:, wi * 128:(wi + 1) * 128],
                        rug[b][:, 256 + wi * 128:256 + (wi + 1) * 128]), inc=sv)
                    ctx.wait("vector", sv)
                    ctx.do("vector", lambda E, b=b, wi=wi: E.tensor_scalar(
                        rug[b][:, wi * 128:(wi + 1) * 128],
                        rug[b][:, wi * 128:(wi + 1) * 128], 448.0, None, A.mult), inc=sv)
                for wi, wst, qw in ((0, wg_st, qwg), (1, wu_st, qwu)):
                    _ia, _ib = broadcast_tensor_aps(
                        wst[b][:].rearrange("p (c w) -> p c w", c=DC),
                        rug[b][:, wi * 128:(wi + 1) * 128].rearrange("p c -> p () c"))
                    ctx.do("vector", lambda E, qw=qw, b=b, a=_ia, bb=_ib: E._custom_dve(
                        QCAST, out=qw[b][:], in0=a, in1=bb, s1=0.5), inc=sv)
                snap_qv[ic] = ctx.sems["v"].n

            ph5_dma(0)
            ph5_chain(0)
            for ic in range(IC):
                b = ic % 2
                if ic + 1 < IC:
                    ph5_dma(ic + 1)
                    ph5_chain(ic + 1)
                ctx.wait_val("tensor", sv, snap_qv[ic])
                for c in range(DC):
                    for wi, qw in ((0, qwg), (1, qwu)):
                        for hf in range(2):
                            ctx.do("tensor", lambda E, b=b, qw=qw, wi=wi, hf=hf, c=c: E.matmul(
                                ps_gu[b * 4 + wi * 2 + hf][:], qw[b][:, c, :],
                                qx[:, c, hf * 512:(hf + 1) * 512],
                                start=(c == 0), stop=(c == DC - 1)), inc=st_)
                ctx.wait("vector", st_)
                for hf in range(2):
                    ctx.wait("vector", ssc)
                    ctx.do("vector", lambda E, ic=ic, b=b, hf=hf: E.scalar_tensor_tensor(
                        gu_sb[:, 0:512], ps_gu[b * 4 + hf][:], wsg_pp[:, ic:ic + 1],
                        sxb[:, hf * 512:(hf + 1) * 512], A.mult, A.mult), inc=sv)
                    ctx.wait("scalar", sv)
                    ctx.do("scalar", lambda E, hf=hf: E.activation(
                        sil_sb[hf][:], gu_sb[:, 0:512], ACT.Silu), inc=ssc)
                    ctx.do("vector", lambda E, ic=ic, b=b, hf=hf: E.scalar_tensor_tensor(
                        gu_sb[:, 512:1024], ps_gu[b * 4 + 2 + hf][:], wsu_pp[:, ic:ic + 1],
                        sxb[:, hf * 512:(hf + 1) * 512], A.mult, A.mult), inc=sv)
                    ctx.wait("vector", ssc)
                    ctx.do("vector", lambda E, ic=ic, hf=hf: E.tensor_tensor(
                        t_sb[:, ic, hf * 512:(hf + 1) * 512], sil_sb[hf][:],
                        gu_sb[:, 512:1024], A.mult), inc=sv)
            barrier()

            ctx.wait("gpsimd", sv)
            ctx.do("gpsimd", lambda E: E.partition_all_reduce(rm_wd[:], rm_wd[:], 128, RMax), inc=sg)
            snap_gwd = ctx.sems["g"].n
            ctx.do("vector", lambda E: E.tensor_reduce(
                rm_t[:], t_sb[:].rearrange("p c t -> p t c"), AXX, A.max,
                apply_absolute_value=True), inc=sv)
            ctx.wait("gpsimd", sv)
            ctx.do("gpsimd", lambda E: E.partition_all_reduce(rm_t[:], rm_t[:], 128, RMax), inc=sg)
            ctx.wait_val("scalar", sg, snap_gwd)
            for oc in range(DC):
                ctx.do("scalar", lambda E, oc=oc: E.dma_start(
                    out=wsd_pp[:, oc:oc + 1], in_=rm_wd[0:1, oc * 128:(oc + 1) * 128]),
                    inc=sst, dma=True)
            ctx.wait_val("vector", sg, snap_gwd)
            ctx.wait("vector", sst)
            ctx.do("vector", lambda E: E.tensor_scalar(
                wsd_pp[:], wsd_pp[:], 1.0 / 448.0, None, A.mult), inc=sv)
            for half in range(2):
                ctx.do("vector", lambda E, half=half: E.reciprocal(
                    wfold[half][:], rm_wd[:, half * 1024:(half + 1) * 1024]), inc=sv)
            ctx.wait("vector", sv)
            for half in range(2):
                ctx.do("vector", lambda E, half=half: E.tensor_scalar(
                    rm_wd[:, half * 1024:(half + 1) * 1024], wfold[half][:],
                    448.0, None, A.mult), inc=sv)
            snap_rw = ctx.sems["v"].n
            ctx.wait("vector", sg)
            ctx.do("vector", lambda E: E.reciprocal(rut448[:], rm_t[:]), inc=sv)
            ctx.wait("vector", sv)
            ctx.do("vector", lambda E: E.tensor_scalar(rut448[:], rut448[:], 448.0, None, A.mult), inc=sv)
            ctx.do("vector", lambda E: E.tensor_scalar(sxtb[:], rm_t[:], 4.0 / 448.0, None, A.mult), inc=sv)
            qt_cast = {}
            for hf in range(2):
                _ta, _tb = broadcast_tensor_aps(
                    t_sb[:, :, hf * 512:(hf + 1) * 512],
                    rut448[:, hf * 512:(hf + 1) * 512].rearrange("p t -> p () t"))
                ctx.do("vector", lambda E, a=_ta, b=_tb, hf=hf: E._custom_dve(
                    QCAST, out=qt[:, :, hf * 512:(hf + 1) * 512],
                    in0=a, in1=b, s1=0.5), inc=sv)
                qt_cast[hf] = ctx.sems["v"].n
            cast_done = {}
            for j in range(2 * IC):
                ic, half = j // 2, j % 2
                ctx.wait_val("sync", sv, cast_done.get(j - 2, qt_cast[1]))
                ctx.do("sync", lambda E, ic=ic, half=half, j=j: E.dma_start(
                    out=wfold[j % 2][:],
                    in_=wdT[ic * 128:(ic + 1) * 128, half * 1024:(half + 1) * 1024]),
                    inc=sx, dma=True)
                dsnap = ctx.sems["x"].n
                ctx.wait_val("vector", sx, dsnap)
                ctx.do("vector", lambda E, ic=ic, half=half, j=j: E._custom_dve(
                    QCAST, out=qwd[:, ic, half * 1024:(half + 1) * 1024],
                    in0=wfold[j % 2][:],
                    in1=rm_wd[:, half * 1024:(half + 1) * 1024], s1=0.5), inc=sv)
                cast_done[j] = ctx.sems["v"].n
            wo_st = wfold[0]
            xo_sb = wfold[1]
            wo_cast = {}
            xo_read = {}
            mm_done = {}
            evac_done = {}
            store_done = {}
            for oc in range(DC):
                ctx.wait_val("sync", sv, wo_cast.get(oc - 1, cast_done[2 * IC - 2]))
                ctx.do("sync", lambda E, oc=oc: E.dma_start(
                    out=wo_st[:].rearrange("p (c w) -> p c w", c=H),
                    in_=WoT.rearrange("(c p) f -> p c f", p=128)[:, :, oc * 128:(oc + 1) * 128]),
                    inc=sx, dma=True)
                wod = ctx.sems["x"].n
                ctx.wait_val("sync", sv, xo_read.get(oc - 1, cast_done[2 * IC - 1]))
                ctx.do("sync", lambda E, oc=oc: E.dma_start(
                    out=xo_sb[:], in_=xT[oc * 128:(oc + 1) * 128, 512:1536]),
                    inc=sx, dma=True)
                xod = ctx.sems["x"].n
                ctx.wait_val("vector", sx, wod)
                if oc >= 1:
                    ctx.wait_val("vector", st_, mm_done[oc - 1])
                ctx.do("vector", lambda E: E.tensor_copy(wo_bf[:], wo_st[:]), inc=sv)
                wo_cast[oc] = ctx.sems["v"].n
                if oc >= 2:
                    ctx.wait_val("tensor", sv, evac_done[oc - 2])
                if oc == 0:
                    ctx.wait_val("tensor", sv, cast_done[2 * IC - 1])
                for hf in range(2):
                    for ic in range(IC):
                        ctx.do("tensor", lambda E, oc=oc, hf=hf, ic=ic: E.matmul(
                            ps7[hf * 2 + oc % 2][:], qwd[:, ic, oc * 128:(oc + 1) * 128],
                            qt[:, ic, hf * 512:(hf + 1) * 512],
                            start=(ic == 0), stop=(ic == IC - 1)), inc=st_)
                    if hf == 0:
                        ctx.wait_val("tensor", sv, wo_cast[oc])
                    for h in range(H):
                        ctx.do("tensor", lambda E, oc=oc, hf=hf, h=h: E.matmul(
                            ps7[4 + hf * 2 + oc % 2][:], wo_bf[:, h * 128:(h + 1) * 128],
                            o_fm[:, h, hf * 512:(hf + 1) * 512],
                            start=(h == 0), stop=(h == H - 1)), inc=st_)
                mm_done[oc] = ctx.sems["t"].n
                ctx.wait_val("vector", st_, mm_done[oc])
                ctx.wait_val("vector", sx, xod)
                if oc >= 1:
                    ctx.wait_val("vector", sst, store_done[oc - 1])
                for hf in range(2):
                    sl = slice(hf * 512, (hf + 1) * 512)
                    ctx.do("vector", lambda E, oc=oc, hf=hf, sl=sl: E.scalar_tensor_tensor(
                        out_sb[:, sl], ps7[hf * 2 + oc % 2][:], wsd_pp[:, oc:oc + 1],
                        sxtb[:, sl], A.mult, A.mult), inc=sv)
                    ctx.do("vector", lambda E, oc=oc, hf=hf, sl=sl: E.scalar_tensor_tensor(
                        out_sb[:, sl], ps7[4 + hf * 2 + oc % 2][:], 1.0, out_sb[:, sl],
                        A.mult, A.add), inc=sv)
                    ctx.do("vector", lambda E, hf=hf, sl=sl: E.tensor_tensor(
                        out_sb[:, sl], out_sb[:, sl], xo_sb[:, sl], A.add), inc=sv)
                xo_read[oc] = ctx.sems["v"].n
                evac_done[oc] = ctx.sems["v"].n
                ctx.wait_val("scalar", sv, evac_done[oc])
                ctx.do("scalar", lambda E, oc=oc: E.dma_start(
                    out=out_d[oc * 128:(oc + 1) * 128, :], in_=out_sb[:]),
                    inc=sst, dma=True)
                store_done[oc] = ctx.sems["st"].n
            ctx.wait("scalar", sst)
            ctx.wait("sync", sst)

        @blk.sync
        def _(_e):
            walk(Ctx("sync", nc, sems))

        @blk.scalar
        def _(_e):
            walk(Ctx("scalar", nc, sems))

        @blk.vector
        def _(_e):
            walk(Ctx("vector", nc, sems))

        @blk.tensor
        def _(_e):
            walk(Ctx("tensor", nc, sems))

        @blk.gpsimd
        def _(_e):
            walk(Ctx("gpsimd", nc, sems))

        es5.close()
        es6.close()

    inst_map = {}
    for lib in all_libraries:
        for it in lib.instructions:
            inst_map[it] = inst_map.get(it, 0) | (1 << lib.index)
    _bass_rust.insert_library_loads(nc, inst_map, len(all_libraries), _lib_std.index)
    mybir.codegen_inst_isa_subclasses(nc)
    return nc


def _stage_inputs(inputs):
    f32 = np.float32
    x = np.ascontiguousarray(np.asarray(inputs["L0_attn_x"], dtype=f32)).reshape(B * N, D)
    Wq = np.asarray(inputs["L0_attn_Wq"], dtype=f32)
    Wkv = np.asarray(inputs["L0_attn_Wkv"], dtype=f32)
    Wg = np.asarray(inputs["L0_attn_Wg"], dtype=f32)
    bg = np.asarray(inputs["L0_attn_bg"], dtype=f32)
    Wo = np.asarray(inputs["L0_attn_Wo"], dtype=f32)
    w_gate = np.asarray(inputs["L0_ffn_w_gate"], dtype=f32)
    w_up = np.asarray(inputs["L0_ffn_w_up"], dtype=f32)
    w_down = np.asarray(inputs["L0_ffn_w_down"], dtype=f32)
    perm = np.asarray(inputs["L0_ffn_down_reorder_index"]).astype(np.int64)
    q_scale = np.asarray(inputs["L0_attn_q_scale"], dtype=f32)
    k_scale = np.asarray(inputs["L0_attn_k_scale"], dtype=f32)

    shared = {
        "WqT": np.ascontiguousarray(Wq.T),
        "WkT": np.ascontiguousarray(Wkv[:H * DH].T),
        "WvT": np.ascontiguousarray(Wkv[H * DH:].T),
        "WgT": np.ascontiguousarray(Wg.T),
        "WoT": np.ascontiguousarray(Wo.T),
        "wgP": np.ascontiguousarray(w_gate[perm].T),
        "wuP": np.ascontiguousarray(w_up[perm].T),
        "wdT": np.ascontiguousarray(w_down.T),
        "qs": np.ascontiguousarray(q_scale.reshape(128, 1)),
        "ks": np.ascontiguousarray(k_scale.reshape(128, 1)),
        "bgb": np.ascontiguousarray(np.tile(bg[None, :], (128, 1))),
        "ident": np.eye(128, dtype=f32),
    }
    in_maps = []
    for c in range(NCORES):
        r0 = c * T
        own = x[r0:r0 + T]
        halo = np.zeros((512, D), dtype=f32) if c % 4 == 0 else x[r0 - 512:r0]
        xTc = np.ascontiguousarray(np.concatenate([halo, own], axis=0).T)
        msk = np.zeros((128, 32), dtype=f32)
        p = np.arange(128, dtype=f32)
        for qb in range(2):
            for kc in range(8):
                j = kc * 128 + p
                start = j - 512.0
                end = j + 1.0
                if (c % 4 == 0) and qb == 0 and kc < 4:
                    start = np.full(128, 512.0, dtype=f32)
                    end = np.full(128, 513.0, dtype=f32)
                col = (qb * 8 + kc) * 2
                msk[:, col] = start
                msk[:, col + 1] = end
        m = dict(shared)
        m["xT"] = xTc
        m["mse"] = msk
        in_maps.append(m)
    return in_maps


def kernel(**inputs):
    in_maps = _stage_inputs(inputs)
    nc = build_nc()
    res = run_bass_kernel_spmd(nc, in_maps, core_ids=list(range(NCORES)))
    out = np.empty((B * N, D), dtype=np.float32)
    for c in range(NCORES):
        out[c * T:(c + 1) * T, :] = np.asarray(res.results[c]["out"]).T
    return out.reshape(B, N, D)


# version 14 (final)
# speedup vs baseline: 4.0972x; 1.0272x over previous
import numpy as np
import ml_dtypes
import concourse.bass as bass
import concourse.mybir as mybir
from concourse.bass_utils import run_bass_kernel_spmd
from concourse import dve_ops
from concourse.dve_op import DveOp
from concourse.dve_spec import Spec, Src0, Src1, C1, C2, lower
from concourse.dve_table_gen import dve_ver_for
import concourse.bass_isa as bass_isa
import bass_rust as _bass_rust
from concourse.library_config import all_libraries, standard as _lib_std
from contextlib import ExitStack
from concourse.bass import broadcast_tensor_aps

F32 = mybir.dt.float32
BF16 = mybir.dt.bfloat16
FP8 = mybir.dt.float8e4
A = mybir.AluOpType
ACT = mybir.ActivationFunctionType
AXX = mybir.AxisListType.X
RMax = bass_isa.ReduceOp.max

NCORES = 8
B, N, D = 2, 4096, 2048
H, DH = 8, 128
I = 5632
T = 1024
EXT = 1536
DC = D // 128
IC = I // 128
TC = T // 128
TCE = EXT // 128
VA = 132
LN_EPS = 1e-5
ENGINES = ("sync", "scalar", "vector", "tensor", "gpsimd")


def _qcast_ref(in0, in1, s0, s1, imm2):
    v = np.asarray(in0) * np.asarray(in1)
    q = v.astype(ml_dtypes.float8_e4m3fn).astype(v.dtype)
    return q * (np.asarray(s1) if s1 is not None else 1.0)


def _register_qcast():
    for op in dve_ops.OPS:
        if op.name == "QCAST":
            return op
    spec = Spec(body=(Src0 * Src1) * C1)
    ver = dve_ver_for("TRN2")
    sha = lower(spec, ver=ver, name="QCAST").sha(ver)
    op = DveOp("QCAST", spec, subdim=False, uops_sha={ver: sha}, reference=_qcast_ref)
    dve_ops.OPS.append(op)
    dve_ops.CUSTOM_DVE_SPECS["QCAST"] = spec
    dve_ops.CUSTOM_DVE_REFS["QCAST"] = _qcast_ref
    dve_ops._SUB_OPCODE_FOR_NAME["QCAST"] = dve_ops._CUSTOM_DVE_ROW_BASE + len(dve_ops.OPS) - 1
    return op


QCAST = _register_qcast()


class _Sem:
    __slots__ = ("h", "n")
    def __init__(self, h):
        self.h = h
        self.n = 0


SERIAL = False

class Ctx:
    def __init__(self, eng_name, nc, sems):
        self.eng = eng_name
        self.nc = nc
        self.sems = sems
        for s in sems.values():
            s.n = 0

    def _E(self):
        return getattr(self.nc, self.eng)

    def do(self, eng, fn, inc=None, dma=False):
        if self.eng == eng:
            E = getattr(self.nc, eng)
            if SERIAL:
                for s in self.sems.values():
                    if s.n > 0:
                        E.wait_ge(s.h, s.n)
            ins = fn(E)
            if inc is not None:
                ins.then_inc(inc.h, 16 if dma else 1)
        if inc is not None:
            inc.n += 16 if dma else 1

    def wait(self, eng, *sems):
        if self.eng == eng:
            for s in sems:
                getattr(self.nc, eng).wait_ge(s.h, s.n)

    def wait_val(self, eng, sem, val):
        if self.eng == eng and val > 0:
            getattr(self.nc, eng).wait_ge(sem.h, val)


def build_nc():
    nc = bass.Bass("TRN2", target_bir_lowering=False)
    din = lambda name, shape, dt=F32: nc.dram_tensor(name, shape, dt, kind="ExternalInput").ap()
    xT = din("xT", [D, EXT])
    WqT = din("WqT", [D, H * DH])
    WkT = din("WkT", [D, H * DH])
    WvT = din("WvT", [D, H * DH])
    WgT = din("WgT", [D, H])
    WoT = din("WoT", [H * DH, D])
    wgP = din("wgP", [D, I])
    wuP = din("wuP", [D, I])
    wdT = din("wdT", [I, D])
    qs_i = din("qs", [128, 1])
    ks_i = din("ks", [128, 1])
    bgb_i = din("bgb", [128, H])
    mse_i = din("mse", [128, 32])
    id_i = din("ident", [128, 128])
    out_d = nc.dram_tensor("out", [D, T], F32, kind="ExternalOutput").ap()

    es = ExitStack()
    with es:
        SB = lambda st, name, shape, dt=F32: st.enter_context(nc.sbuf_tensor(name, shape, dt))
        SBR = lambda st, name, shape, dt=F32: st.enter_context(nc.sbuf_tensor(name, shape, dt, side="right"))
        PS = lambda st, name, shape, dt=F32: st.enter_context(nc.psum_tensor(name, shape, dt))
        qs = SB(es, "qs_t", [128, 1]); ks = SB(es, "ks_t", [128, 1])
        qs8 = SB(es, "qs8_t", [128, 1])
        bgb = SB(es, "bgb_t", [128, H]); mse = SB(es, "mse_t", [128, 32])
        ident = SB(es, "ident_b", [128, 128], BF16)
        ones = SB(es, "ones_t", [128, 1])
        gates = SB(es, "gates_t", [128, TC * H])
        oscl = SB(es, "oscl", [128, 2])
        wsg_pp = SB(es, "wsg_pp", [128, IC]); wsu_pp = SB(es, "wsu_pp", [128, IC])
        wsd_pp = SB(es, "wsd_pp", [128, DC])
        o_fm = SB(es, "o_fm", [128, H, T], BF16)
        rm_wd = SB(es, "rm_wd", [128, D])
        wfold = [SB(es, f"wfold{i}", [128, T]) for i in range(2)]

        sems = {k: _Sem(es.enter_context(nc.semaphore("sem_" + k)))
                for k in ("x", "v", "s", "t", "g", "st", "ph")}

        es1 = ExitStack()
        mb = SB(es1, "mb", [128, EXT]); rb = SB(es1, "rb", [128, EXT])
        ps_a1 = [PS(es1, f"ps_a1{i}", [1, 512]) for i in range(3)]
        ps_b1 = [PS(es1, f"ps_b1{i}", [1, 512]) for i in range(3)]
        esX = ExitStack()
        rows = SBR(esX, "rows", [1, 4 * EXT])
        xr = [SBR(esX, f"xr{i}", [128, EXT]) for i in range(3)]
        es2 = ExitStack()
        xn = SBR(es2, "xn", [128, DC, EXT], BF16)
        rux448 = SBR(es2, "rux448", [128, T])
        rm_x = SBR(es2, "rm_x", [128, T])
        es3 = ExitStack()
        q_h = SBR(es3, "q_h", [128, T], BF16)
        k_h = SBR(es3, "k_h", [128, EXT], BF16)
        v_augh = SBR(es3, "v_augh", [128, TCE, VA], BF16)
        wf_st = [SBR(es3, f"wf_st{i}", [128, DC * 128]) for i in range(2)]
        wf_bf = [SBR(es3, f"wf_bf{i}", [128, DC * 128], BF16) for i in range(2)]
        rnb = SBR(es3, "rnb", [128, EXT])
        p_sb = SBR(es3, "p_sb", [128, 8, 512], BF16)
        sc_sb = SBR(es3, "sc_sb", [128, 2, 512])
        vfm = SBR(es3, "vfm", [128, EXT], BF16)
        otm_sb = [SBR(es3, f"otm{i}", [128, 128], BF16) for i in range(2)]
        es1.close()
        ps_pr = [PS(es3, f"ps_pr{i}", [128, 512]) for i in range(5)]
        ps_tr = PS(es3, "ps_tr", [128, 1024], BF16)
        ps_pv = [PS(es3, f"ps_pv{i}", [128, 512]) for i in range(2)]
        es2b = ExitStack()
        qx = SB(es2b, "qx", [128, DC, T], FP8)
        sxb = SB(es2b, "sxb", [128, T])
        es3.close()
        es2.close()
        esX.close()
        es4 = ExitStack()
        t_sb = SBR(es4, "t_sb", [128, IC, T], BF16)
        es4a = ExitStack()
        wg_st = [SBR(es4a, f"wg_st{i}", [128, DC * 128]) for i in range(2)]
        wu_st = [SBR(es4a, f"wu_st{i}", [128, DC * 128]) for i in range(2)]
        qwg = [SBR(es4a, f"qwg{i}", [128, DC, 128], FP8) for i in range(2)]
        qwu = [SBR(es4a, f"qwu{i}", [128, DC, 128], FP8) for i in range(2)]
        gu_sb = SBR(es4a, "gu_sb", [128, 1024])
        sil_sb = [SBR(es4a, f"sil_sb{i}", [128, 512]) for i in range(2)]
        wab = SBR(es4a, "wab", [128, DC * 128])
        rug = [SBR(es4a, f"rug{i}", [128, 512]) for i in range(2)]
        ps_gu = [PS(es4a, f"ps_gu{i}", [128, 512]) for i in range(8)]
        es5 = ExitStack()
        es2b.close()
        es4a.close()
        qt = SB(es5, "qt", [128, IC, T], FP8)
        sxtb = SB(es5, "sxtb", [128, T])
        rm_t = sxtb
        rut448 = wfold[0]
        es4.close()
        es6 = ExitStack()
        qwd = SBR(es6, "qwd", [128, IC, D], FP8)
        wo_bf = SBR(es6, "wo_bf", [128, H * 128], BF16)
        out_sb = SBR(es6, "out_sb", [128, T])
        ps7 = [PS(es6, f"ps7_{i}", [128, 512]) for i in range(8)]

        blk = es.enter_context(nc.Block())

        def walk(ctx: Ctx):
            sx, sv, ssc, st_, sg, sst, sph = (ctx.sems[k] for k in
                                              ("x", "v", "s", "t", "g", "st", "ph"))
            phase_n = [0]

            def barrier():
                phase_n[0] += 1
                tgt = 5 * phase_n[0]
                for e in ENGINES:
                    ctx.do(e, lambda E: E.nop(), inc=sph)
                if ctx.eng is not None:
                    getattr(ctx.nc, ctx.eng).wait_ge(sph.h, tgt)

            m_row = rows[:, 0:EXT]
            e2_row = rows[:, EXT:2 * EXT]
            w_row = rows[:, 2 * EXT:3 * EXT]
            r_row = rows[:, 3 * EXT:4 * EXT]

            for src, dst in ((qs_i, qs), (ks_i, ks), (bgb_i, bgb), (mse_i, mse)):
                ctx.do("sync", lambda E, s=src, d=dst: E.dma_start(out=d[:], in_=s),
                       inc=sx, dma=True)
            ctx.do("sync", lambda E: E.dma_start(out=xr[0][:, 0:128], in_=id_i),
                   inc=sx, dma=True)
            ctx.do("gpsimd", lambda E: E.memset(ones[:], 1.0), inc=sg)
            ctx.do("gpsimd", lambda E: E.memset(v_augh[:], 0.0), inc=sg)
            ctx.do("gpsimd", lambda E: E.memset(v_augh[:, :, 128:129], 1.0), inc=sg)
            ctx.wait("vector", sx)
            ctx.do("vector", lambda E: E.tensor_copy(ident[:], xr[0][:, 0:128]), inc=sv)
            ctx.do("vector", lambda E: E.tensor_scalar(qs8[:], qs[:], 8.0, None, A.mult), inc=sv)
            barrier()

            for c in range(DC):
                b = c % 2
                ctx.wait("sync", st_)
                ctx.do("sync", lambda E, c=c, b=b: E.dma_start(
                    out=xr[b][:], in_=xT[c * 128:(c + 1) * 128, :]), inc=sx, dma=True)
                ctx.wait("scalar", sx, st_)
                ctx.do("scalar", lambda E, b=b: E.activation(
                    xr[2][:], xr[b][:], ACT.Square), inc=ssc)
                ctx.wait("tensor", sx, ssc)
                for h3 in range(3):
                    ctx.do("tensor", lambda E, c=c, b=b, h3=h3: E.matmul(
                        ps_a1[h3][:], ones[:], xr[b][:, h3 * 512:(h3 + 1) * 512],
                        start=(c == 0), stop=(c == DC - 1)), inc=st_)
                    ctx.do("tensor", lambda E, c=c, h3=h3: E.matmul(
                        ps_b1[h3][:], ones[:], xr[2][:, h3 * 512:(h3 + 1) * 512],
                        start=(c == 0), stop=(c == DC - 1)), inc=st_)
            ctx.wait("vector", st_)
            for h3 in range(3):
                ctx.do("vector", lambda E, h3=h3: E.tensor_scalar(
                    m_row[:, h3 * 512:(h3 + 1) * 512], ps_a1[h3][:], 1.0 / D, None,
                    A.mult), inc=sv)
                ctx.do("vector", lambda E, h3=h3: E.tensor_scalar(
                    e2_row[:, h3 * 512:(h3 + 1) * 512], ps_b1[h3][:], 1.0 / D, None,
                    A.mult), inc=sv)
            ctx.do("vector", lambda E: E.scalar_tensor_tensor(
                w_row, m_row, -1.0, m_row, A.mult, A.mult), inc=sv)
            ctx.do("vector", lambda E: E.tensor_tensor(w_row, e2_row, w_row, A.add), inc=sv)
            ctx.do("vector", lambda E: E.tensor_scalar(w_row, w_row, LN_EPS, None, A.add), inc=sv)
            ctx.wait("scalar", sv)
            ctx.do("scalar", lambda E: E.activation(r_row, w_row, ACT.Sqrt), inc=ssc)
            ctx.wait("vector", ssc)
            ctx.do("vector", lambda E: E.reciprocal(e2_row, r_row), inc=sv)
            ctx.wait("vector", sv)
            ctx.do("vector", lambda E: E.tensor_tensor(r_row, e2_row, e2_row, A.mult), inc=sv)
            ctx.do("vector", lambda E: E.tensor_tensor(r_row, w_row, r_row, A.mult), inc=sv)
            ctx.do("vector", lambda E: E.tensor_scalar(r_row, r_row, -0.5, 1.5, A.mult, A.add), inc=sv)
            ctx.do("vector", lambda E: E.tensor_tensor(r_row, e2_row, r_row, A.mult), inc=sv)
            ctx.wait("gpsimd", sv)
            ctx.do("gpsimd", lambda E: E.partition_broadcast(mb[:], m_row, channels=128), inc=sg)
            ctx.do("gpsimd", lambda E: E.partition_broadcast(rb[:], r_row, channels=128), inc=sg)
            barrier()

            for c in range(DC):
                b = c % 2
                ctx.wait("sync", sv)
                ctx.do("sync", lambda E, c=c, b=b: E.dma_start(
                    out=xr[b][:], in_=xT[c * 128:(c + 1) * 128, :]), inc=sx, dma=True)
                ctx.wait("vector", sx, sg)
                ctx.do("vector", lambda E, b=b: E.scalar_tensor_tensor(
                    xr[2][:], mb[:], -1.0, xr[b][:], A.mult, A.add), inc=sv)
                ctx.do("vector", lambda E, c=c: E.tensor_tensor(
                    xn[:, c, :], xr[2][:], rb[:], A.mult), inc=sv)
            barrier()

            ctx.do("sync", lambda E: E.dma_start(
                out=wf_st[0][:].rearrange("p (c w) -> p c w", c=DC)[:, :, 0:H],
                in_=WgT.rearrange("(c p) f -> p c f", p=128)), inc=sx, dma=True)
            ctx.wait("vector", sx)
            ctx.do("vector", lambda E: E.tensor_copy(
                wf_bf[0][:].rearrange("p (c w) -> p c w", c=DC)[:, :, 0:H],
                wf_st[0][:].rearrange("p (c w) -> p c w", c=DC)[:, :, 0:H]), inc=sv)
            for tc in range(TC):
                ctx.wait("tensor", sv)
                for c in range(DC):
                    ctx.do("tensor", lambda E, tc=tc, c=c: E.matmul(
                        ps_pr[3][:, 0:H],
                        xn[:, c, 512 + tc * 128: 512 + (tc + 1) * 128],
                        wf_bf[0][:].rearrange("p (c w) -> p c w", c=DC)[:, c, 0:H],
                        start=(c == 0), stop=(c == DC - 1)), inc=st_)
                ctx.wait("vector", st_, ssc)
                ctx.do("vector", lambda E: E.tensor_tensor(
                    xr[2][:, 0:H], ps_pr[3][:, 0:H], bgb[:], A.add), inc=sv)
                ctx.wait("scalar", sv)
                ctx.do("scalar", lambda E, tc=tc: E.activation(
                    gates[:, tc * H:(tc + 1) * H], xr[2][:, 0:H], ACT.Sigmoid), inc=ssc)
            ctx.do("vector", lambda E: E.tensor_reduce(
                rm_x[:], xn[:, :, 512:1536].rearrange("p c t -> p t c"), AXX, A.max,
                apply_absolute_value=True), inc=sv)
            ctx.wait("gpsimd", sv)
            ctx.do("gpsimd", lambda E: E.partition_all_reduce(rm_x[:], rm_x[:], 128, RMax), inc=sg)
            ctx.wait("vector", sg)
            ctx.do("vector", lambda E: E.reciprocal(rux448[:], rm_x[:]), inc=sv)
            ctx.wait("vector", sv)
            ctx.do("vector", lambda E: E.tensor_scalar(rux448[:], rux448[:], 448.0, None, A.mult), inc=sv)
            ctx.do("vector", lambda E: E.tensor_scalar(sxb[:], rm_x[:], 4.0 / 448.0, None, A.mult), inc=sv)
            _qa, _qb = broadcast_tensor_aps(
                xn[:, :, 512:1536], rux448[:].rearrange("p t -> p () t"))
            ctx.do("vector", lambda E, a=_qa, b=_qb: E._custom_dve(
                QCAST, out=qx[:], in0=a, in1=b, s1=0.5), inc=sv)
            barrier()

            RAdd = bass_isa.ReduceOp.add
            snap = {}
            mmsnap = {}

            def wdma(src_ap, li, guard=None):
                b = li % 2
                if guard is not None:
                    ctx.wait_val("sync", sv, guard)
                ctx.do("sync", lambda E, src=src_ap, li=li, b=b: E.dma_start(
                    out=wf_st[b][:].rearrange("p (c w) -> p c w", c=DC),
                    in_=src.rearrange("(c p) f -> p c f", p=128)[:, :, (li // 3) * 128:(li // 3 + 1) * 128]),
                    inc=sx, dma=True)

            def wcast(li, mmguard=None):
                b = li % 2
                ctx.wait("vector", sx)
                if mmguard is not None:
                    ctx.wait_val("vector", st_, mmguard)
                ctx.do("vector", lambda E, b=b: E.tensor_copy(wf_bf[b][:], wf_st[b][:]), inc=sv)
                snap[("cast", li)] = ctx.sems["v"].n

            def l2chain(psums, n, mm_snap, xr2_guard):
                ctx.wait_val("scalar", st_, mm_snap)
                if xr2_guard:
                    ctx.wait_val("scalar", sv, xr2_guard)
                ctx.wait("scalar", sg)
                for hf in range(len(psums)):
                    ctx.do("scalar", lambda E, hf=hf, ps=psums[hf]: E.activation(
                        xr[2][:, hf * 512:(hf + 1) * 512], ps[:], ACT.Square), inc=ssc)
                sq_snap = ctx.sems["s"].n
                ctx.wait("gpsimd", ssc, sv)
                ctx.do("gpsimd", lambda E, n=n: E.partition_all_reduce(
                    rnb[:, 0:n], xr[2][:, 0:n], 128, RAdd), inc=sg)
                ctx.wait("vector", sg)
                ctx.do("vector", lambda E, n=n: E.tensor_scalar(
                    rnb[:, 0:n], rnb[:, 0:n], 1e-24, None, A.max), inc=sv)
                ctx.wait_val("scalar", sv, ctx.sems["v"].n)
                ctx.do("scalar", lambda E, n=n: E.activation(
                    xr[1][:, 0:n], rnb[:, 0:n], ACT.Sqrt), inc=ssc)
                ctx.wait("vector", ssc)
                ctx.do("vector", lambda E, n=n: E.reciprocal_approx_fast(
                    rnb[:, 0:n], xr[1][:, 0:n]), inc=sv)
                ctx.wait("vector", sv)
                return sq_snap

            wdma(WqT, 0)
            wcast(0)
            wdma(WkT, 1)
            wcast(1)
            for h in range(H):
                if h > 0:
                    ctx.wait_val("tensor", sv, snap[("attnv", h - 1)])
                else:
                    ctx.wait_val("tensor", sv, snap[("cast", 0)])
                for c in range(DC):
                    for hf in range(2):
                        ctx.do("tensor", lambda E, h=h, hf=hf, c=c: E.matmul(
                            ps_pr[3 + hf][:], wf_bf[(3 * h) % 2][:, c * 128:(c + 1) * 128],
                            xn[:, c, 512 + hf * 512: 512 + (hf + 1) * 512],
                            start=(c == 0), stop=(c == DC - 1)), inc=st_)
                mmsnap[3 * h] = ctx.sems["t"].n
                wdma(WvT, 3 * h + 2, guard=snap[("cast", 3 * h)])
                ctx.wait_val("tensor", sv,
                             max(snap[("cast", 3 * h + 1)], snap.get(("kn", h - 1), 0)))
                if ("ksq", h - 1) in snap:
                    ctx.wait_val("tensor", ssc, snap[("ksq", h - 1)])
                for c in range(DC):
                    for hf in range(3):
                        ctx.do("tensor", lambda E, h=h, hf=hf, c=c: E.matmul(
                            ps_pr[hf][:], wf_bf[(3 * h + 1) % 2][:, c * 128:(c + 1) * 128],
                            xn[:, c, hf * 512:(hf + 1) * 512],
                            start=(c == 0), stop=(c == DC - 1)), inc=st_)
                mmsnap[3 * h + 1] = ctx.sems["t"].n
                l2chain([ps_pr[3], ps_pr[4]], T, mmsnap[3 * h], None)
                for hf in range(2):
                    ctx.do("vector", lambda E, hf=hf: E.scalar_tensor_tensor(
                        q_h[:, hf * 512:(hf + 1) * 512], ps_pr[3 + hf][:], qs8[:],
                        rnb[:, hf * 512:(hf + 1) * 512], A.mult, A.mult), inc=sv)
                snap[("qn", h)] = ctx.sems["v"].n
                wcast(3 * h + 2, mmguard=mmsnap[3 * h])
                vbanks = [ps_pr[3], ps_pr[4], ps_pv[0]]
                ctx.wait_val("tensor", sv, snap[("cast", 3 * h + 2)])
                for c in range(DC):
                    for hf in range(3):
                        ctx.do("tensor", lambda E, h=h, hf=hf, c=c, ps=vbanks[hf]: E.matmul(
                            ps[:], wf_bf[(3 * h + 2) % 2][:, c * 128:(c + 1) * 128],
                            xn[:, c, hf * 512:(hf + 1) * 512],
                            start=(c == 0), stop=(c == DC - 1)), inc=st_)
                mmsnap[3 * h + 2] = ctx.sems["t"].n
                if h + 1 < H:
                    wdma(WqT, 3 * h + 3, guard=snap[("cast", 3 * h + 1)])
                snap[("ksq", h)] = l2chain(
                    [ps_pr[0], ps_pr[1], ps_pr[2]], EXT, mmsnap[3 * h + 1], snap[("qn", h)])
                for hf in range(3):
                    ctx.do("vector", lambda E, hf=hf: E.scalar_tensor_tensor(
                        k_h[:, hf * 512:(hf + 1) * 512], ps_pr[hf][:], ks[:],
                        rnb[:, hf * 512:(hf + 1) * 512], A.mult, A.mult), inc=sv)
                snap[("kn", h)] = ctx.sems["v"].n
                if h + 1 < H:
                    wcast(3 * h + 3, mmguard=mmsnap[3 * h + 1])
                ctx.wait_val("vector", st_, mmsnap[3 * h + 2])
                for hf in range(3):
                    ctx.do("vector", lambda E, hf=hf, ps=vbanks[hf]: E.tensor_copy(
                        vfm[:, hf * 512:(hf + 1) * 512], ps[:]), inc=sv)
                snap[("vevac", h)] = ctx.sems["v"].n
                if h + 1 < H:
                    wdma(WkT, 3 * h + 4, guard=snap[("cast", 3 * h + 2)])
                prev = snap[("vevac", h)]
                for tc in range(TCE):
                    ctx.wait_val("tensor", sv, prev)
                    ctx.do("tensor", lambda E, tc=tc: E.transpose(
                        ps_tr[:, 0:128], vfm[:, tc * 128:(tc + 1) * 128], ident[:]), inc=st_)
                    ctx.wait("vector", st_)
                    ctx.do("vector", lambda E, tc=tc: E.tensor_copy(
                        v_augh[:, tc, 0:128], ps_tr[:, 0:128]), inc=sv)
                    prev = ctx.sems["v"].n
                if h + 1 < H:
                    wcast(3 * h + 4, mmguard=mmsnap[3 * h + 2])
                for qb in range(2):
                    mask_snap = {}
                    for kc in range(8):
                        if kc >= 2:
                            ctx.wait_val("tensor", sv, mask_snap[kc - 2])
                        ctx.do("tensor", lambda E, h=h, qb=qb, kc=kc: E.matmul(
                            ps_pr[3 + kc % 2][:],
                            k_h[:, qb * 512 + kc * 128: qb * 512 + (kc + 1) * 128],
                            q_h[:, qb * 512:(qb + 1) * 512],
                            start=True, stop=True), inc=st_)
                        ctx.wait("vector", st_)
                        gi = h * 16 + qb * 8 + kc
                        if gi >= 2:
                            ctx.wait_val("vector", ssc, snap[("exp", gi - 2)])
                        col = (qb * 8 + kc) * 2
                        ctx.do("vector", lambda E, kc=kc, col=col: E.tensor_mask_reduce(
                            sc_sb[:, kc % 2, :], ps_pr[3 + kc % 2][:],
                            mse[:, col:col + 1], mse[:, col + 1:col + 2],
                            1.0, 0.0, A.max), inc=sv)
                        mask_snap[kc] = ctx.sems["v"].n
                        ctx.wait_val("scalar", sv, mask_snap[kc])
                        ctx.wait("scalar", st_)
                        ctx.do("scalar", lambda E, kc=kc: E.activation(
                            p_sb[:, kc, :], sc_sb[:, kc % 2, :], ACT.Exp), inc=ssc)
                        snap[("exp", gi)] = ctx.sems["s"].n
                    for qc in range(4):
                        ctx.wait("tensor", ssc)
                        for kc in range(8):
                            ctx.do("tensor", lambda E, qb=qb, qc=qc, kc=kc: E.matmul(
                                ps_pv[qc % 2][:, 0:129],
                                p_sb[:, kc, qc * 128:(qc + 1) * 128],
                                v_augh[:, qb * 4 + kc, 0:129],
                                start=(kc == 0), stop=(kc == 7)), inc=st_)
                        ctx.wait("vector", st_)
                        ctx.do("vector", lambda E, qc=qc: E.reciprocal(
                            oscl[:, qc % 2:qc % 2 + 1], ps_pv[qc % 2][:, 128:129]), inc=sv)
                        ctx.wait("vector", sv)
                        tc = qb * 4 + qc
                        ctx.do("vector", lambda E, qc=qc, tc=tc, h=h: E.tensor_tensor(
                            oscl[:, qc % 2:qc % 2 + 1], oscl[:, qc % 2:qc % 2 + 1],
                            gates[:, tc * H + h: tc * H + h + 1], A.mult), inc=sv)
                        ctx.wait("scalar", sv)
                        ctx.do("scalar", lambda E, qc=qc: E.activation(
                            otm_sb[qc % 2][:], ps_pv[qc % 2][:, 0:128], ACT.Copy,
                            scale=oscl[:, qc % 2:qc % 2 + 1]), inc=ssc)
                        ctx.wait("tensor", ssc, sv)
                        ctx.do("tensor", lambda E, qc=qc: E.transpose(
                            ps_tr[:, 128:256], otm_sb[qc % 2][:], ident[:]), inc=st_)
                        ctx.wait("vector", st_)
                        ctx.do("vector", lambda E, h=h, qb=qb, qc=qc: E.tensor_copy(
                            o_fm[:, h, qb * 512 + qc * 128: qb * 512 + (qc + 1) * 128],
                            ps_tr[:, 128:256]), inc=sv)
                snap[("attnv", h)] = ctx.sems["v"].n
                for j in range(h * 11, h * 11 + 11):
                    ic, half = j // 2, j % 2
                    if j >= 2:
                        ctx.wait_val("gpsimd", sv, snap[("wfold", j - 2)])
                    ctx.do("gpsimd", lambda E, ic=ic, half=half, j=j: E.dma_start(
                        out=wfold[j % 2][:],
                        in_=wdT[ic * 128:(ic + 1) * 128, half * 1024:(half + 1) * 1024]),
                        inc=sst, dma=True)
                    snap[("wddma", j)] = ctx.sems["st"].n
                    ctx.wait_val("vector", sst, snap[("wddma", j)])
                    rms = rm_wd[:, half * 1024:(half + 1) * 1024]
                    if j < 2:
                        ctx.do("vector", lambda E, j=j, rms=rms: E.scalar_tensor_tensor(
                            rms, wfold[j % 2][:], -1.0, wfold[j % 2][:],
                            A.mult, A.max), inc=sv)
                    else:
                        ctx.do("vector", lambda E, j=j, rms=rms: E.scalar_tensor_tensor(
                            rms, wfold[j % 2][:], -1.0, rms, A.mult, A.max), inc=sv)
                        ctx.do("vector", lambda E, j=j, rms=rms: E.scalar_tensor_tensor(
                            rms, wfold[j % 2][:], 1.0, rms, A.mult, A.max), inc=sv)
                    snap[("wfold", j)] = ctx.sems["v"].n
            barrier()

            snap_qv = {}
            snap_qs = {}

            def ph5_dma(ic):
                b = ic % 2
                if ic - 2 in snap_qv:
                    ctx.wait_val("sync", sv, snap_qv[ic - 2])
                    ctx.wait_val("sync", ssc, snap_qs[ic - 2])
                ctx.do("sync", lambda E, ic=ic, b=b: E.dma_start(
                    out=wg_st[b][:].rearrange("p (c w) -> p c w", c=DC),
                    in_=wgP.rearrange("(c p) f -> p c f", p=128)[:, :, ic * 128:(ic + 1) * 128]),
                    inc=sx, dma=True)
                ctx.do("sync", lambda E, ic=ic, b=b: E.dma_start(
                    out=wu_st[b][:].rearrange("p (c w) -> p c w", c=DC),
                    in_=wuP.rearrange("(c p) f -> p c f", p=128)[:, :, ic * 128:(ic + 1) * 128]),
                    inc=sx, dma=True)

            def ph5_chain(ic):
                b = ic % 2
                for wi, wst, pp in ((0, wg_st, wsg_pp), (1, wu_st, wsu_pp)):
                    ctx.wait("scalar", sx, sv, sg)
                    ctx.do("scalar", lambda E, b=b, wst=wst: E.activation(
                        wab[:], wst[b][:], ACT.Abs), inc=ssc)
                    ctx.wait("vector", ssc)
                    w_ = DC * 128
                    for step in (w_ // 2, w_ // 4, w_ // 8, w_ // 16):
                        ctx.do("vector", lambda E, step=step: E.tensor_tensor(
                            wab[:, 0:step], wab[:, 0:step], wab[:, step:2 * step],
                            A.max), inc=sv)
                    ctx.wait("gpsimd", sv)
                    ctx.do("gpsimd", lambda E, b=b, wi=wi: E.partition_all_reduce(
                        rug[b][:, 256 + wi * 128:256 + (wi + 1) * 128], wab[:, 0:128],
                        128, RMax), inc=sg)
                snap_qs[ic] = ctx.sems["s"].n
                ctx.wait("scalar", sg)
                ctx.do("scalar", lambda E, ic=ic, b=b: E.dma_start(
                    out=wsg_pp[:, ic:ic + 1], in_=rug[b][0:1, 256:384]), inc=sst, dma=True)
                ctx.do("scalar", lambda E, ic=ic, b=b: E.dma_start(
                    out=wsu_pp[:, ic:ic + 1], in_=rug[b][0:1, 384:512]), inc=sst, dma=True)
                ctx.wait("vector", sg, sst)
                ctx.do("vector", lambda E, ic=ic: E.tensor_scalar(
                    wsg_pp[:, ic:ic + 1], wsg_pp[:, ic:ic + 1], 1.0 / 448.0, None,
                    A.mult), inc=sv)
                ctx.do("vector", lambda E, ic=ic: E.tensor_scalar(
                    wsu_pp[:, ic:ic + 1], wsu_pp[:, ic:ic + 1], 1.0 / 448.0, None,
                    A.mult), inc=sv)
                for wi in range(2):
                    ctx.do("vector", lambda E, b=b, wi=wi: E.reciprocal(
                        rug[b][:, wi * 128:(wi + 1) * 128],
                        rug[b][:, 256 + wi * 128:256 + (wi + 1) * 128]), inc=sv)
                    ctx.wait("vector", sv)
                    ctx.do("vector", lambda E, b=b, wi=wi: E.tensor_scalar(
                        rug[b][:, wi * 128:(wi + 1) * 128],
                        rug[b][:, wi * 128:(wi + 1) * 128], 448.0, None, A.mult), inc=sv)
                for wi, wst, qw in ((0, wg_st, qwg), (1, wu_st, qwu)):
                    _ia, _ib = broadcast_tensor_aps(
                        wst[b][:].rearrange("p (c w) -> p c w", c=DC),
                        rug[b][:, wi * 128:(wi + 1) * 128].rearrange("p c -> p () c"))
                    ctx.do("vector", lambda E, qw=qw, b=b, a=_ia, bb=_ib: E._custom_dve(
                        QCAST, out=qw[b][:], in0=a, in1=bb, s1=0.5), inc=sv)
                snap_qv[ic] = ctx.sems["v"].n

            ph5_dma(0)
            ph5_chain(0)
            for ic in range(IC):
                b = ic % 2
                if ic + 1 < IC:
                    ph5_dma(ic + 1)
                    ph5_chain(ic + 1)
                ctx.wait_val("tensor", sv, snap_qv[ic])
                for c in range(DC):
                    for wi, qw in ((0, qwg), (1, qwu)):
                        for hf in range(2):
                            ctx.do("tensor", lambda E, b=b, qw=qw, wi=wi, hf=hf, c=c: E.matmul(
                                ps_gu[b * 4 + wi * 2 + hf][:], qw[b][:, c, :],
                                qx[:, c, hf * 512:(hf + 1) * 512],
                                start=(c == 0), stop=(c == DC - 1)), inc=st_)
                ctx.wait("vector", st_)
                for hf in range(2):
                    ctx.wait("vector", ssc)
                    ctx.do("vector", lambda E, ic=ic, b=b, hf=hf: E.scalar_tensor_tensor(
                        gu_sb[:, 0:512], ps_gu[b * 4 + hf][:], wsg_pp[:, ic:ic + 1],
                        sxb[:, hf * 512:(hf + 1) * 512], A.mult, A.mult), inc=sv)
                    ctx.wait("scalar", sv)
                    ctx.do("scalar", lambda E, hf=hf: E.activation(
                        sil_sb[hf][:], gu_sb[:, 0:512], ACT.Silu), inc=ssc)
                    ctx.do("vector", lambda E, ic=ic, b=b, hf=hf: E.scalar_tensor_tensor(
                        gu_sb[:, 512:1024], ps_gu[b * 4 + 2 + hf][:], wsu_pp[:, ic:ic + 1],
                        sxb[:, hf * 512:(hf + 1) * 512], A.mult, A.mult), inc=sv)
                    ctx.wait("vector", ssc)
                    ctx.do("vector", lambda E, ic=ic, hf=hf: E.tensor_tensor(
                        t_sb[:, ic, hf * 512:(hf + 1) * 512], sil_sb[hf][:],
                        gu_sb[:, 512:1024], A.mult), inc=sv)
            barrier()

            ctx.wait("gpsimd", sv)
            ctx.do("gpsimd", lambda E: E.partition_all_reduce(rm_wd[:], rm_wd[:], 128, RMax), inc=sg)
            snap_gwd = ctx.sems["g"].n
            ctx.do("vector", lambda E: E.tensor_reduce(
                rm_t[:], t_sb[:].rearrange("p c t -> p t c"), AXX, A.max,
                apply_absolute_value=True), inc=sv)
            ctx.wait("gpsimd", sv)
            ctx.do("gpsimd", lambda E: E.partition_all_reduce(rm_t[:], rm_t[:], 128, RMax), inc=sg)
            ctx.wait_val("scalar", sg, snap_gwd)
            for oc in range(DC):
                ctx.do("scalar", lambda E, oc=oc: E.dma_start(
                    out=wsd_pp[:, oc:oc + 1], in_=rm_wd[0:1, oc * 128:(oc + 1) * 128]),
                    inc=sst, dma=True)
            ctx.wait_val("vector", sg, snap_gwd)
            ctx.wait("vector", sst)
            ctx.do("vector", lambda E: E.tensor_scalar(
                wsd_pp[:], wsd_pp[:], 1.0 / 448.0, None, A.mult), inc=sv)
            for half in range(2):
                ctx.do("vector", lambda E, half=half: E.reciprocal(
                    wfold[half][:], rm_wd[:, half * 1024:(half + 1) * 1024]), inc=sv)
            ctx.wait("vector", sv)
            for half in range(2):
                ctx.do("vector", lambda E, half=half: E.tensor_scalar(
                    rm_wd[:, half * 1024:(half + 1) * 1024], wfold[half][:],
                    448.0, None, A.mult), inc=sv)
            snap_rw = ctx.sems["v"].n
            ctx.wait("vector", sg)
            ctx.do("vector", lambda E: E.reciprocal(rut448[:], rm_t[:]), inc=sv)
            ctx.wait("vector", sv)
            ctx.do("vector", lambda E: E.tensor_scalar(rut448[:], rut448[:], 448.0, None, A.mult), inc=sv)
            ctx.do("vector", lambda E: E.tensor_scalar(sxtb[:], rm_t[:], 4.0 / 448.0, None, A.mult), inc=sv)
            qt_cast = {}
            for hf in range(2):
                _ta, _tb = broadcast_tensor_aps(
                    t_sb[:, :, hf * 512:(hf + 1) * 512],
                    rut448[:, hf * 512:(hf + 1) * 512].rearrange("p t -> p () t"))
                ctx.do("vector", lambda E, a=_ta, b=_tb, hf=hf: E._custom_dve(
                    QCAST, out=qt[:, :, hf * 512:(hf + 1) * 512],
                    in0=a, in1=b, s1=0.5), inc=sv)
                qt_cast[hf] = ctx.sems["v"].n
            cast_done = {}
            for j in range(2 * IC):
                ic, half = j // 2, j % 2
                ctx.wait_val("sync", sv, cast_done.get(j - 2, qt_cast[1]))
                ctx.do("sync", lambda E, ic=ic, half=half, j=j: E.dma_start(
                    out=wfold[j % 2][:],
                    in_=wdT[ic * 128:(ic + 1) * 128, half * 1024:(half + 1) * 1024]),
                    inc=sx, dma=True)
                dsnap = ctx.sems["x"].n
                ctx.wait_val("vector", sx, dsnap)
                ctx.do("vector", lambda E, ic=ic, half=half, j=j: E._custom_dve(
                    QCAST, out=qwd[:, ic, half * 1024:(half + 1) * 1024],
                    in0=wfold[j % 2][:],
                    in1=rm_wd[:, half * 1024:(half + 1) * 1024], s1=0.5), inc=sv)
                cast_done[j] = ctx.sems["v"].n
            wo_st = wfold[0]
            xo_sb = wfold[1]
            wo_cast = {}
            xo_read = {}
            mm_done = {}
            evac_done = {}
            store_done = {}
            for oc in range(DC):
                ctx.wait_val("sync", sv, wo_cast.get(oc - 1, cast_done[2 * IC - 2]))
                ctx.do("sync", lambda E, oc=oc: E.dma_start(
                    out=wo_st[:].rearrange("p (c w) -> p c w", c=H),
                    in_=WoT.rearrange("(c p) f -> p c f", p=128)[:, :, oc * 128:(oc + 1) * 128]),
                    inc=sx, dma=True)
                wod = ctx.sems["x"].n
                ctx.wait_val("sync", sv, xo_read.get(oc - 1, cast_done[2 * IC - 1]))
                ctx.do("sync", lambda E, oc=oc: E.dma_start(
                    out=xo_sb[:], in_=xT[oc * 128:(oc + 1) * 128, 512:1536]),
                    inc=sx, dma=True)
                xod = ctx.sems["x"].n
                ctx.wait_val("vector", sx, wod)
                if oc >= 1:
                    ctx.wait_val("vector", st_, mm_done[oc - 1])
                ctx.do("vector", lambda E: E.tensor_copy(wo_bf[:], wo_st[:]), inc=sv)
                wo_cast[oc] = ctx.sems["v"].n
                if oc >= 2:
                    ctx.wait_val("tensor", sv, evac_done[oc - 2])
                if oc == 0:
                    ctx.wait_val("tensor", sv, cast_done[2 * IC - 1])
                for hf in range(2):
                    for ic in range(IC):
                        ctx.do("tensor", lambda E, oc=oc, hf=hf, ic=ic: E.matmul(
                            ps7[hf * 2 + oc % 2][:], qwd[:, ic, oc * 128:(oc + 1) * 128],
                            qt[:, ic, hf * 512:(hf + 1) * 512],
                            start=(ic == 0), stop=(ic == IC - 1)), inc=st_)
                    if hf == 0:
                        ctx.wait_val("tensor", sv, wo_cast[oc])
                    for h in range(H):
                        ctx.do("tensor", lambda E, oc=oc, hf=hf, h=h: E.matmul(
                            ps7[4 + hf * 2 + oc % 2][:], wo_bf[:, h * 128:(h + 1) * 128],
                            o_fm[:, h, hf * 512:(hf + 1) * 512],
                            start=(h == 0), stop=(h == H - 1)), inc=st_)
                mm_done[oc] = ctx.sems["t"].n
                ctx.wait_val("vector", st_, mm_done[oc])
                ctx.wait_val("vector", sx, xod)
                if oc >= 1:
                    ctx.wait_val("vector", sst, store_done[oc - 1])
                for hf in range(2):
                    sl = slice(hf * 512, (hf + 1) * 512)
                    ctx.do("vector", lambda E, oc=oc, hf=hf, sl=sl: E.scalar_tensor_tensor(
                        out_sb[:, sl], ps7[hf * 2 + oc % 2][:], wsd_pp[:, oc:oc + 1],
                        sxtb[:, sl], A.mult, A.mult), inc=sv)
                    ctx.do("vector", lambda E, oc=oc, hf=hf, sl=sl: E.scalar_tensor_tensor(
                        out_sb[:, sl], ps7[4 + hf * 2 + oc % 2][:], 1.0, out_sb[:, sl],
                        A.mult, A.add), inc=sv)
                    ctx.do("vector", lambda E, hf=hf, sl=sl: E.tensor_tensor(
                        out_sb[:, sl], out_sb[:, sl], xo_sb[:, sl], A.add), inc=sv)
                xo_read[oc] = ctx.sems["v"].n
                evac_done[oc] = ctx.sems["v"].n
                ctx.wait_val("scalar", sv, evac_done[oc])
                ctx.do("scalar", lambda E, oc=oc: E.dma_start(
                    out=out_d[oc * 128:(oc + 1) * 128, :], in_=out_sb[:]),
                    inc=sst, dma=True)
                store_done[oc] = ctx.sems["st"].n
            ctx.wait("scalar", sst)
            ctx.wait("sync", sst)

        @blk.sync
        def _(_e):
            walk(Ctx("sync", nc, sems))

        @blk.scalar
        def _(_e):
            walk(Ctx("scalar", nc, sems))

        @blk.vector
        def _(_e):
            walk(Ctx("vector", nc, sems))

        @blk.tensor
        def _(_e):
            walk(Ctx("tensor", nc, sems))

        @blk.gpsimd
        def _(_e):
            walk(Ctx("gpsimd", nc, sems))

        es5.close()
        es6.close()

    inst_map = {}
    for lib in all_libraries:
        for it in lib.instructions:
            inst_map[it] = inst_map.get(it, 0) | (1 << lib.index)
    _bass_rust.insert_library_loads(nc, inst_map, len(all_libraries), _lib_std.index)
    mybir.codegen_inst_isa_subclasses(nc)
    return nc


def _stage_inputs(inputs):
    f32 = np.float32
    x = np.ascontiguousarray(np.asarray(inputs["L0_attn_x"], dtype=f32)).reshape(B * N, D)
    Wq = np.asarray(inputs["L0_attn_Wq"], dtype=f32)
    Wkv = np.asarray(inputs["L0_attn_Wkv"], dtype=f32)
    Wg = np.asarray(inputs["L0_attn_Wg"], dtype=f32)
    bg = np.asarray(inputs["L0_attn_bg"], dtype=f32)
    Wo = np.asarray(inputs["L0_attn_Wo"], dtype=f32)
    w_gate = np.asarray(inputs["L0_ffn_w_gate"], dtype=f32)
    w_up = np.asarray(inputs["L0_ffn_w_up"], dtype=f32)
    w_down = np.asarray(inputs["L0_ffn_w_down"], dtype=f32)
    perm = np.asarray(inputs["L0_ffn_down_reorder_index"]).astype(np.int64)
    q_scale = np.asarray(inputs["L0_attn_q_scale"], dtype=f32)
    k_scale = np.asarray(inputs["L0_attn_k_scale"], dtype=f32)

    shared = {
        "WqT": np.ascontiguousarray(Wq.T),
        "WkT": np.ascontiguousarray(Wkv[:H * DH].T),
        "WvT": np.ascontiguousarray(Wkv[H * DH:].T),
        "WgT": np.ascontiguousarray(Wg.T),
        "WoT": np.ascontiguousarray(Wo.T),
        "wgP": np.ascontiguousarray(w_gate[perm].T),
        "wuP": np.ascontiguousarray(w_up[perm].T),
        "wdT": np.ascontiguousarray(w_down.T),
        "qs": np.ascontiguousarray(q_scale.reshape(128, 1)),
        "ks": np.ascontiguousarray(k_scale.reshape(128, 1)),
        "bgb": np.ascontiguousarray(np.tile(bg[None, :], (128, 1))),
        "ident": np.eye(128, dtype=f32),
    }
    in_maps = []
    for c in range(NCORES):
        r0 = c * T
        own = x[r0:r0 + T]
        halo = np.zeros((512, D), dtype=f32) if c % 4 == 0 else x[r0 - 512:r0]
        xTc = np.ascontiguousarray(np.concatenate([halo, own], axis=0).T)
        msk = np.zeros((128, 32), dtype=f32)
        p = np.arange(128, dtype=f32)
        for qb in range(2):
            for kc in range(8):
                j = kc * 128 + p
                start = j - 512.0
                end = j + 1.0
                if (c % 4 == 0) and qb == 0 and kc < 4:
                    start = np.full(128, 512.0, dtype=f32)
                    end = np.full(128, 513.0, dtype=f32)
                col = (qb * 8 + kc) * 2
                msk[:, col] = start
                msk[:, col + 1] = end
        m = dict(shared)
        m["xT"] = xTc
        m["mse"] = msk
        in_maps.append(m)
    return in_maps


def kernel(**inputs):
    in_maps = _stage_inputs(inputs)
    nc = build_nc()
    res = run_bass_kernel_spmd(nc, in_maps, core_ids=list(range(NCORES)))
    out = np.empty((B * N, D), dtype=np.float32)
    for c in range(NCORES):
        out[c * T:(c + 1) * T, :] = np.asarray(res.results[c]["out"]).T
    return out.reshape(B, N, D)
